# version 17
# baseline (speedup 1.0000x reference)
import numpy as np
import ml_dtypes
from contextlib import ExitStack

import concourse.bass as bass
import concourse.mybir as mybir
from concourse.bass_utils import run_bass_kernel_spmd

F32 = mybir.dt.float32
BF16 = mybir.dt.bfloat16
I8 = mybir.dt.int8
AF = mybir.ActivationFunctionType
ALU = mybir.AluOpType
AX = mybir.AxisListType

B, S, D, H = 2, 8192, 2048, 16
DN, DR, DK, KVL, DV = 128, 64, 192, 512, 128
INTER = 8192
NSLOT = 16
OWN = NSLOT * 128
RMS_EPS = 1.1920928955078125e-07
LN_EPS = 1e-5
SSCALE = float(1.0 / np.sqrt(DK))
BOUND_MARGIN = 1.04

ENGS = ("PE", "ACT", "DVE", "SP", "PL")


class Prog:

    NPOOL = 4

    def __init__(self, nc, ctx):
        self.nc = nc
        self.sem = {}
        self.cnt = {}
        for name in ("pe", "act", "dve", "plc"):
            self.sem[name] = ctx.enter_context(nc.semaphore(name))
            self.cnt[name] = 0
        self.pool = {}
        self.pool_idx = {}
        self.open = {}
        self.all_pool_sems = set()
        for q in ("SP", "PL", "SPst", "PLst", "ACTst"):
            names = []
            for i in range(self.NPOOL):
                nm = f"{q.lower()}{i}"
                self.sem[nm] = ctx.enter_context(nc.semaphore(nm))
                self.cnt[nm] = 0
                names.append(nm)
                self.all_pool_sems.add(nm)
            self.pool[q] = names
            self.pool_idx[q] = -1
            self.open[q] = None
        self.qs = None

    def phase(self):
        self.qs = {e: [] for e in ENGS}

    _AUTO = {"DVE": "dve", "ACT": "act", "PL": "plc"}

    def op(self, eng, fn, waits=(), inc=None, n=1, noclose=()):
        auto = self._AUTO.get(eng)
        if auto is not None and (inc is None or inc == auto):
            waits = list(waits) + [(auto, self.cnt[auto])]
            inc = auto
        ws = []
        for (s, v) in waits:
            if s in self.all_pool_sems:
                if s not in noclose:
                    v = self.cnt[s]
                    for q in self.pool:
                        if self.open[q] == s:
                            self.open[q] = None
            if v > 0:
                ws.append((s, v))
        dedup = {}
        for (s, v) in ws:
            dedup[s] = max(v, dedup.get(s, 0))
        ws = tuple(dedup.items())
        self.qs[eng].append((fn, ws, inc, n))
        if inc is not None:
            self.cnt[inc] += n
            return self.cnt[inc]
        return None

    def dma(self, queue, out, in_, waits=(), cls="ld"):
        eng = queue
        if cls == "st":
            queue = queue + "st"
        elif queue == "ACT":
            queue = "ACTst"
        extra = []
        if self.open[queue] is None:
            self.pool_idx[queue] += 1
            sem = self.pool[queue][self.pool_idx[queue] % self.NPOOL]
            self.open[queue] = sem
            if self.cnt[sem] > 0:
                extra = [(sem, self.cnt[sem])]
        sem = self.open[queue]
        self.op(eng, lambda e, o=out, i=in_: e.dma_start(out=o, in_=i),
                waits=list(waits) + extra, inc=sem, n=16, noclose=(sem,))
        return (sem, self.cnt[sem])

    def dma_all(self, queue):
        toks = []
        qs = (queue, queue + "st") if queue != "ACT" else ("ACTst",)
        for q in qs:
            self.open[q] = None
            toks += [(s, self.cnt[s]) for s in self.pool[q] if self.cnt[s] > 0]
        return toks

    def emit(self):
        blk_ctx = self.nc.Block()
        blk = blk_ctx.__enter__()
        try:
            for engname, attr in (("PE", "tensor"), ("ACT", "scalar"),
                                  ("DVE", "vector"), ("SP", "sync"),
                                  ("PL", "gpsimd")):
                ops = self.qs[engname]
                if not ops:
                    continue

                def run(e, ops=ops):
                    for fn, waits, inc, n in ops:
                        for (s, v) in waits:
                            e.wait_ge(self.sem[s], v)
                        i = fn(e)
                        if inc is not None:
                            i.then_inc(self.sem[inc], n)
                getattr(blk, attr)(run)
        finally:
            blk_ctx.__exit__(None, None, None)



def _rope_tables():
    freqs = 1.0 / 10000.0 ** (np.arange(0, DR, 2, dtype=np.float64) / DR)
    t = np.arange(S, dtype=np.float64)[:, None] * freqs[None, :]
    return (np.cos(t).T.astype(np.float32), np.sin(t).T.astype(np.float32))


def _wq_col_order():
    nope = np.concatenate([192 * h + np.arange(128) for h in range(H)])
    rope = []
    for r in range(8):
        h0, h1 = 2 * r, 2 * r + 1
        rope += [192 * h0 + 128 + 2 * np.arange(32),
                 192 * h1 + 128 + 2 * np.arange(32),
                 192 * h0 + 128 + 2 * np.arange(32) + 1,
                 192 * h1 + 128 + 2 * np.arange(32) + 1]
    return np.concatenate([nope] + rope)


def host_prep(inputs):
    x = np.ascontiguousarray(np.asarray(inputs["L0_attn_x"]))
    Wq = np.asarray(inputs["L0_attn_Wq"])
    Wkva = np.asarray(inputs["L0_attn_Wkv_a"])
    kvnw = np.asarray(inputs["L0_attn_kv_norm_w"])
    Wkvb = np.asarray(inputs["L0_attn_Wkv_b"])
    Wo = np.asarray(inputs["L0_attn_Wo"])
    wg = np.asarray(inputs["L0_ffn_w_gate"])
    wu = np.asarray(inputs["L0_ffn_w_up"])
    wd = np.asarray(inputs["L0_ffn_w_down"])
    sg = np.asarray(inputs["L0_ffn_s_gate"])
    su = np.asarray(inputs["L0_ffn_s_up"])
    sd = np.asarray(inputs["L0_ffn_s_down"])

    cosf, sinf = _rope_tables()
    cos2 = np.ascontiguousarray(np.vstack([cosf, cosf]))
    sin2 = np.ascontiguousarray(np.vstack([sinf, sinf]))

    wq_p = np.ascontiguousarray(
        Wq[:, _wq_col_order()].reshape(16, 128, 24, 128).transpose(2, 1, 0, 3)
        .reshape(24, 128, 2048))
    rc = 512 + np.concatenate([2 * np.arange(32), 2 * np.arange(32) + 1])
    wkva_p = np.ascontiguousarray(
        np.concatenate([Wkva[:, :512], Wkva[:, rc]], axis=1)
        .reshape(16, 128, 576))
    kvnw_p = np.ascontiguousarray(kvnw.reshape(4, 128).T)
    wkvb_p = np.ascontiguousarray(Wkvb.reshape(4, 128, 4096))
    wo_p = np.ascontiguousarray(
        Wo.reshape(16, 128, 16, 128).transpose(2, 1, 0, 3).reshape(16, 128, 2048))
    wg_p = np.ascontiguousarray(
        wg.reshape(16, 128, 64, 128).transpose(2, 1, 0, 3))
    wu_p = np.ascontiguousarray(
        wu.reshape(16, 128, 64, 128).transpose(2, 1, 0, 3))
    wd_p = np.ascontiguousarray(
        wd.reshape(64, 128, 16, 128).transpose(2, 1, 0, 3))
    sg_p = np.ascontiguousarray(sg.reshape(64, 128).T)
    su_p = np.ascontiguousarray(su.reshape(64, 128).T)
    sd_p = np.ascontiguousarray(sd.reshape(16, 128).T)
    selv = np.zeros((128, 2), np.float32)
    selv[0:32, 0] = 1.0
    selv[64:96, 0] = 1.0
    selv[32:64, 1] = 1.0
    selv[96:128, 1] = 1.0
    sel = np.ascontiguousarray(np.repeat(selv, 128, axis=1))
    kposc = (np.arange(128, dtype=np.float32)[:, None]
             + 128.0 * np.arange(64, dtype=np.float32)[None, :])
    kposc = np.ascontiguousarray(kposc)
    consts = np.zeros((128, 4), np.float32)
    consts[:, 0] = RMS_EPS
    consts[:, 1] = LN_EPS
    consts = np.ascontiguousarray(consts)

    xf = [np.ascontiguousarray(x[b].T) for b in range(B)]

    in_maps = []
    tiles_per_core = []
    for core in range(8):
        b, j = divmod(core, 4)
        tiles = [4 * s + j for s in range(NSLOT)]
        tiles_per_core.append((b, tiles))
        qcols = np.concatenate([np.arange(128 * t, 128 * t + 128)
                                for t in tiles])
        xq = np.ascontiguousarray(xf[b][:, qcols])
        qpos_rep = np.ascontiguousarray(
            np.broadcast_to(qcols.astype(np.float32)[None, :], (128, OWN)))
        cos_own = np.ascontiguousarray(cos2[:, qcols])
        sin_own = np.ascontiguousarray(sin2[:, qcols])
        in_maps.append(dict(
            xf=xf[b], xq=xq, qpos=qpos_rep,
            cos_own=cos_own, sin_own=sin_own, cosf=cosf, sinf=sinf,
            kposc=kposc, sel=sel, consts=consts,
            wq=wq_p, wkva=wkva_p, kvnw=kvnw_p, wkvb=wkvb_p, wo=wo_p,
            wg=wg_p, wu=wu_p, wd=wd_p, sg=sg_p, su=su_p, sd=sd_p,
        ))
    return in_maps, tiles_per_core



def build_nc(upto=5, dbg=()):
    nc = bass.Bass("TRN2", target_bir_lowering=False)
    dbg = set(dbg)

    t_xf = nc.dram_tensor("xf", [D, S], F32, kind="ExternalInput").ap()
    t_xq = nc.dram_tensor("xq", [D, OWN], F32, kind="ExternalInput").ap()
    t_qpos = nc.dram_tensor("qpos", [128, OWN], F32, kind="ExternalInput").ap()
    t_cos_own = nc.dram_tensor("cos_own", [64, OWN], F32, kind="ExternalInput").ap()
    t_sin_own = nc.dram_tensor("sin_own", [64, OWN], F32, kind="ExternalInput").ap()
    t_cosf = nc.dram_tensor("cosf", [32, S], F32, kind="ExternalInput").ap()
    t_sinf = nc.dram_tensor("sinf", [32, S], F32, kind="ExternalInput").ap()
    t_kposc = nc.dram_tensor("kposc", [128, 64], F32, kind="ExternalInput").ap()
    t_sel = nc.dram_tensor("sel", [128, 256], F32, kind="ExternalInput").ap()
    t_consts = nc.dram_tensor("consts", [128, 4], F32, kind="ExternalInput").ap()
    t_wq = nc.dram_tensor("wq", [24, 128, 2048], F32, kind="ExternalInput").ap()
    t_wkva = nc.dram_tensor("wkva", [16, 128, 576], F32, kind="ExternalInput").ap()
    t_kvnw = nc.dram_tensor("kvnw", [128, 4], F32, kind="ExternalInput").ap()
    t_wkvb = nc.dram_tensor("wkvb", [4, 128, 4096], F32, kind="ExternalInput").ap()
    t_wo = nc.dram_tensor("wo", [16, 128, 2048], F32, kind="ExternalInput").ap()
    t_wg = nc.dram_tensor("wg", [64, 128, 16, 128], I8, kind="ExternalInput").ap()
    t_wu = nc.dram_tensor("wu", [64, 128, 16, 128], I8, kind="ExternalInput").ap()
    t_wd = nc.dram_tensor("wd", [16, 128, 64, 128], I8, kind="ExternalInput").ap()
    t_sg = nc.dram_tensor("sg", [128, 64], F32, kind="ExternalInput").ap()
    t_su = nc.dram_tensor("su", [128, 64], F32, kind="ExternalInput").ap()
    t_sd = nc.dram_tensor("sd", [128, 16], F32, kind="ExternalInput").ap()

    t_out = nc.dram_tensor("out", [OWN, D], F32, kind="ExternalOutput").ap()

    q_scr = nc.dram_tensor("q_scr", [3072, OWN], BF16, kind="Internal").ap()
    o_scr = nc.dram_tensor("o_scr", [4, 128, 16, 512], BF16, kind="Internal").ap()
    h1_scr = nc.dram_tensor("h1_scr", [D, OWN], F32, kind="Internal").ap()
    wo_bf = nc.dram_tensor("wo_bf", [16, 128, 2048], BF16, kind="Internal").ap()

    dbg_aps = {}
    def dbg_out(name, shape, dtype):
        if name in dbg:
            dbg_aps[name] = nc.dram_tensor(
                "dbg_" + name, shape, dtype, kind="ExternalOutput").ap()
            return dbg_aps[name]
        return None

    d_qscr = dbg_out("qscr", [3072, OWN], BF16)
    d_ckv = dbg_out("ckv", [128, 4 * S], BF16)
    d_krope = dbg_out("krope", [64, S], BF16)
    d_qmax = dbg_out("qmax", [1, 16], F32)
    d_krn = dbg_out("krn", [1, S], BF16)
    d_oscr = dbg_out("oscr", [4, 128, 16, 512], BF16)
    d_h1 = dbg_out("h1", [D, OWN], F32)
    d_kn15 = dbg_out("kn15", [128, S], BF16)
    d_vt15 = dbg_out("vt15", [128, S], BF16)

    with ExitStack() as g:
        P = Prog(nc, g)
        qmaxN = g.enter_context(nc.sbuf_tensor("qmaxN", [1, 16], F32))
        qmaxR = g.enter_context(nc.sbuf_tensor("qmaxR", [1, 16], F32))
        qsum_t = g.enter_context(nc.sbuf_tensor("qsum_t", [1, 1], F32))
        ones128 = g.enter_context(nc.sbuf_tensor("ones128", [128, 1], BF16))
        ones1 = g.enter_context(nc.sbuf_tensor("ones1", [1, 128], BF16))
        ones1f = g.enter_context(nc.sbuf_tensor("ones1f", [1, 128], F32))
        ones_mat = g.enter_context(nc.sbuf_tensor("ones_mat", [128, 128], BF16))
        consts_sb = g.enter_context(nc.sbuf_tensor("consts_sb", [128, 4], F32))

        P.phase()
        P.op("PL", lambda e: e.memset(ones128[:, :], 1.0), inc="plc")
        P.op("PL", lambda e: e.memset(ones1[:, :], 1.0), inc="plc")
        P.op("PL", lambda e: e.memset(ones1f[:, :], 1.0), inc="plc")
        P.op("PL", lambda e: e.memset(ones_mat[:, :], 1.0), inc="plc")
        P.op("PL", lambda e: e.memset(qmaxN[:, :], -1e30), inc="plc")
        P.op("PL", lambda e: e.memset(qmaxR[:, :], -1e30), inc="plc")
        t_c = P.dma("SP", consts_sb[:, :], t_consts)
        init_done = [("plc", P.cnt["plc"]), t_c]

        with ExitStack() as a:
            wqb = [a.enter_context(nc.sbuf_tensor(f"wqb{i}", [128, 2048], BF16)) for i in range(2)]
            xb = [a.enter_context(nc.sbuf_tensor(f"xbA{i}", [128, 16 * 512], BF16)) for i in range(4)]
            cosA = a.enter_context(nc.sbuf_tensor("cosA", [64, OWN], F32))
            sinA = a.enter_context(nc.sbuf_tensor("sinA", [64, OWN], F32))
            sel_sb = a.enter_context(nc.sbuf_tensor("sel_sb", [128, 256], BF16))
            qev = [a.enter_context(nc.sbuf_tensor(f"qev{i}", [128, 512], BF16)) for i in range(2)]
            qevr = [a.enter_context(nc.sbuf_tensor(f"qevr{i}", [128, 512], BF16)) for i in range(2)]
            sqb = [a.enter_context(nc.sbuf_tensor(f"sqbA{i}", [128, 512], BF16)) for i in range(4)]
            rm1 = a.enter_context(nc.sbuf_tensor("rm1", [64, 512], F32))
            rm2 = a.enter_context(nc.sbuf_tensor("rm2", [64, 512], F32))
            qm_t = a.enter_context(nc.sbuf_tensor("qm_t", [2, 1], F32))
            pq = [a.enter_context(nc.psum_tensor(f"pqA{i}", [128, 512], F32)) for i in range(4)]
            pn = [a.enter_context(nc.psum_tensor(f"pnA{i}", [128, 512], F32)) for i in range(2)]
            pnR = [a.enter_context(nc.psum_tensor(f"pnR{i}", [128, 512], F32)) for i in range(2)]

            def load_xb(tt):
                src = t_xq[:, 512 * tt:512 * (tt + 1)].rearrange(
                    "(c p) q -> p c q", p=128)
                dst = xb[tt][:, :].rearrange("p (c q) -> p c q", c=16)
                return [P.dma("PL", dst, src)]
            t_xb = {0: load_xb(0)}
            P.dma("SP", cosA[:, :], t_cos_own)
            t_trig = P.dma("SP", sinA[:, :], t_sin_own)
            t_sel = P.dma("PL", sel_sb[:, :], t_sel)
            t_wocast = [P.dma("PL", wo_bf[c], t_wo[c], cls="st") for c in range(16)]

            bank_free = {i: [] for i in range(4)}
            wqb_free = {0: [], 1: []}
            qev_free = {0: [], 1: []}
            qevr_free = {0: [], 1: []}
            pn_free = {0: [], 1: []}
            sq_free = {i: [] for i in range(4)}
            t_stores = []
            pending_qn = []

            def flush_qn():
                nonlocal_state = None
                for (kind, ocb2, tt2, tok_sq2) in pending_qn:
                    par = tt2 % 2
                    spar = tt2
                    if kind == "nope":
                        h = ocb2
                        tok_n = P.op("PE", lambda e, par=par, spar=spar: e.matmul(
                            pn[par][:, :], ones_mat[:, :], sqb[spar][:, :],
                            start=True, stop=True),
                            waits=[("act", tok_sq2), ("plc", P.cnt["plc"])] + pn_free[par], inc="pe")
                        pn_free[par] = []
                        P.op("DVE", lambda e, par=par: e.reduce_max(
                            qm_t[0:1, :], pn[par][0:1, :], AX.X), waits=[("pe", tok_n)])
                        tok_mx = P.op("DVE", lambda e, h=h: e.tensor_max(
                            qmaxN[:, h:h + 1], qmaxN[:, h:h + 1], qm_t[0:1, :]), inc="dve")
                        pn_free[par] = [("dve", tok_mx)]
                    else:
                        r = ocb2 - 16
                        P.op("PE", lambda e, par=par, spar=spar: e.matmul(
                            pn[par][:, :], sel_sb[:, 0:128], sqb[spar][:, :],
                            start=True, stop=True),
                            waits=[("act", tok_sq2)] + pn_free[par])
                        tok_n = P.op("PE", lambda e, par=par, spar=spar: e.matmul(
                            pnR[par][:, :], sel_sb[:, 128:256], sqb[spar][:, :],
                            start=True, stop=True), inc="pe")
                        pn_free[par] = []
                        P.op("DVE", lambda e, par=par: e.reduce_max(
                            qm_t[0:1, :], pn[par][0:1, :], AX.X), waits=[("pe", tok_n)])
                        P.op("DVE", lambda e, r=r: e.tensor_max(
                            qmaxR[:, 2 * r:2 * r + 1], qmaxR[:, 2 * r:2 * r + 1], qm_t[0:1, :]))
                        P.op("DVE", lambda e, par=par: e.reduce_max(
                            qm_t[0:1, :], pnR[par][0:1, :], AX.X))
                        tok_mx = P.op("DVE", lambda e, r=r: e.tensor_max(
                            qmaxR[:, 2 * r + 1:2 * r + 2], qmaxR[:, 2 * r + 1:2 * r + 2], qm_t[0:1, :]), inc="dve")
                        pn_free[par] = [("dve", tok_mx)]
                    sq_free[spar] = [("pe", tok_n)]
                pending_qn.clear()

            for ocb in range(24):
                wpar = ocb % 2
                t_w = P.dma("PL", wqb[wpar][:, :], t_wq[ocb], waits=wqb_free[wpar])
                wqb_free[wpar] = []
                flush_qn()
                for tt in range(4):
                    bk = (4 * ocb + tt) % 4
                    par = tt % 2
                    for c in range(16):
                        waits = []
                        if c == 0:
                            waits = [t_w, t_xb[tt][-1]] + bank_free[bk] + ([t_sel] if ocb == 0 else [])
                            bank_free[bk] = []
                        P.op("PE", lambda e, c=c, bk=bk, tt=tt, wpar=wpar: e.matmul(
                            pq[bk][:, :], wqb[wpar][:, 128 * c:128 * (c + 1)],
                            xb[tt][:, 512 * c:512 * (c + 1)],
                            start=(c == 0), stop=(c == 15)),
                            waits=waits, inc="pe" if c == 15 else None)
                    tok_mm = P.cnt["pe"]
                    if ocb == 0 and tt + 1 < 4:
                        t_xb[tt + 1] = load_xb(tt + 1)
                    if ocb < 16:
                        h = ocb
                        tok_cp = P.op("ACT", lambda e, par=par, bk=bk: e.copy(
                            qev[par][:, :], pq[bk][:, :]),
                            waits=[("pe", tok_mm)] + qev_free[par], inc="act")
                        qev_free[par] = []
                        tok_sq = P.op("ACT", lambda e, tt=tt, bk=bk: e.activation(
                            sqb[tt][:, :], pq[bk][:, :], AF.Square),
                            waits=sq_free[tt], inc="act")
                        sq_free[tt] = []
                        bank_free[bk] = [("act", tok_sq)]
                        t_st = P.dma("SP", q_scr[128 * h:128 * (h + 1),
                                               512 * tt:512 * (tt + 1)],
                                     qev[par][:, :], waits=[("act", tok_cp)], cls="st")
                        qev_free[par] = [t_st]
                        t_stores.append(t_st)
                        pending_qn.append(("nope", ocb, tt, tok_sq))
                    else:
                        r = ocb - 16
                        tok_sq = P.op("ACT", lambda e, tt=tt, bk=bk: e.activation(
                            sqb[tt][:, :], pq[bk][:, :], AF.Square),
                            waits=[("pe", tok_mm)] + sq_free[tt], inc="act")
                        sq_free[tt] = []
                        pending_qn.append(("rope", ocb, tt, tok_sq))
                        cs = cosA[:, 512 * tt:512 * (tt + 1)]
                        sn = sinA[:, 512 * tt:512 * (tt + 1)]
                        e_ap = pq[bk][0:64, :]
                        o_ap = pq[bk][64:128, :]
                        wts = [("act", tok_sq), t_trig] + qevr_free[par]
                        qevr_free[par] = []
                        P.op("DVE", lambda e, a1=e_ap, c1=cs: e.tensor_mul(rm1[:, :], a1, c1), waits=wts)
                        P.op("DVE", lambda e, a1=o_ap, s1=sn: e.tensor_mul(rm2[:, :], a1, s1))
                        P.op("DVE", lambda e, par=par: e.tensor_sub(qevr[par][0:64, :], rm1[:, :], rm2[:, :]))
                        P.op("DVE", lambda e, a1=o_ap, c1=cs: e.tensor_mul(rm1[:, :], a1, c1))
                        P.op("DVE", lambda e, a1=e_ap, s1=sn: e.tensor_mul(rm2[:, :], a1, s1))
                        tok_rope = P.op("DVE", lambda e, par=par: e.tensor_add(
                            qevr[par][64:128, :], rm1[:, :], rm2[:, :]), inc="dve")
                        bank_free[bk] = [("dve", tok_rope)]
                        h0, h1 = 2 * r, 2 * r + 1
                        for (rows_s, rows_d) in (((0, 32), 2048 + 64 * h0),
                                                 ((32, 64), 2048 + 64 * h1),
                                                 ((64, 96), 2048 + 64 * h0 + 32),
                                                 ((96, 128), 2048 + 64 * h1 + 32)):
                            t_st = P.dma("SP",
                                         q_scr[rows_d:rows_d + 32, 512 * tt:512 * (tt + 1)],
                                         qevr[par][rows_s[0]:rows_s[1], :],
                                         waits=[("dve", tok_rope)], cls="st")
                            t_stores.append(t_st)
                        qevr_free[par] = [t_stores[-1]]
                wqb_free[wpar] = [("pe", P.cnt["pe"])]
            flush_qn()
            P.op("SP", lambda e: e.nop(), waits=P.dma_all("SP") + P.dma_all("PL") + [("dve", P.cnt["dve"])])
            if d_qscr is not None:
                P.dma("SP", d_qscr, q_scr)
                P.op("SP", lambda e: e.nop(), waits=P.dma_all("SP") + P.dma_all("PL"))
            P.emit()


        if upto < 2:
            _final_wait(nc, P, t_out)
            return nc, dbg_aps

        with ExitStack() as bc:
            ckv_sb = bc.enter_context(nc.sbuf_tensor("ckv_sb", [128, 4 * S], BF16))
            krope_sb = bc.enter_context(nc.sbuf_tensor("krope_sb", [128, S], BF16))
            krn_sb = bc.enter_context(nc.sbuf_tensor("krn_sb", [1, S], BF16))

            P.phase()
            with ExitStack() as b:
                wkva_sb = b.enter_context(nc.sbuf_tensor("wkva_sb", [128, 16 * 576], BF16))
                kvnw_sb = b.enter_context(nc.sbuf_tensor("kvnw_sb", [128, 4], F32))
                xb2 = [b.enter_context(nc.sbuf_tensor(f"xbB{i}", [128, 16 * 512], BF16)) for i in range(2)]
                cosB = [b.enter_context(nc.sbuf_tensor(f"cosB{i}", [32, 512], F32)) for i in range(2)]
                sinB = [b.enter_context(nc.sbuf_tensor(f"sinB{i}", [32, 512], F32)) for i in range(2)]
                ckvraw = [b.enter_context(nc.sbuf_tensor(f"ckvraw{i}", [128, 4 * 512], F32)) for i in range(2)]
                sqB = [b.enter_context(nc.sbuf_tensor(f"sqB{i}", [128, 4 * 512], BF16)) for i in range(2)]
                sqRB = [b.enter_context(nc.sbuf_tensor(f"sqRB{i}", [64, 512], BF16)) for i in range(2)]
                rstdB = b.enter_context(nc.sbuf_tensor("rstdB", [128, 512], F32))
                rmB1 = b.enter_context(nc.sbuf_tensor("rmB1", [32, 512], F32))
                rmB2 = b.enter_context(nc.sbuf_tensor("rmB2", [32, 512], F32))
                pkv = [b.enter_context(nc.psum_tensor(f"pkvB{i}", [128, 512], F32)) for i in range(3)]
                prope = b.enter_context(nc.psum_tensor("propeB", [64, 512], F32))
                pss = b.enter_context(nc.psum_tensor("pssB", [128, 512], F32))
                pkr = b.enter_context(nc.psum_tensor("pkrB", [128, 512], F32))


                t_wk_ld = [P.dma("PL", wkva_sb[:, 576 * c:576 * (c + 1)], t_wkva[c])
                           for c in range(16)]
                t_kvnw = P.dma("SP", kvnw_sb[:, :], t_kvnw)

                bank_free = {0: [], 1: [], 2: []}
                rope_free = []
                ss_free = []
                rstd_free = []
                kr_free = []
                raw_free = {0: [], 1: []}
                sq_freeB = {0: [], 1: []}
                xb_free = {0: [], 1: []}
                cos_free = {0: [], 1: []}
                pending_B = None

                def flush_B(pend):
                    nonlocal ss_free, rstd_free, kr_free, sq_freeB, raw_free
                    tt2 = pend["tt"]
                    rp = tt2 % 2
                    for oc in range(4):
                        P.op("PE", lambda e, oc=oc, rp=rp: e.matmul(
                            pss[:, :], ones_mat[:, :], sqB[rp][:, 512 * oc:512 * (oc + 1)],
                            start=(oc == 0), stop=(oc == 3)),
                            waits=([("act", pend["sq3"])] + ss_free) if oc == 0 else [],
                            inc="pe" if oc == 3 else None)
                    tok_ss = P.cnt["pe"]
                    ss_free = []
                    tok_kr = P.op("PE", lambda e, rp=rp: e.matmul(
                        pkr[:, :], ones_mat[0:64, :], sqRB[rp][:, :],
                        start=True, stop=True), waits=[("act", pend["sqr"])] + kr_free, inc="pe")
                    tok_krc = P.op("DVE", lambda e, tt2=tt2: e.tensor_copy(
                        krn_sb[:, 512 * tt2:512 * (tt2 + 1)], pkr[0:1, :]),
                        waits=[("pe", tok_kr)], inc="dve")
                    kr_free = [("dve", tok_krc)]
                    tok_ln = P.op("ACT", lambda e: e.activation(
                        rstdB[:, :], pss[:, :], AF.Ln,
                        bias=consts_sb[:, 0:1], scale=1.0 / 512.0),
                        waits=[("pe", tok_ss), t_c] + rstd_free)
                    rstd_free = []
                    tok_ex = P.op("ACT", lambda e: e.activation(
                        rstdB[:, :], rstdB[:, :], AF.Exp, scale=-0.5), inc="act")
                    ss_free = [("act", tok_ex)]
                    for oc in range(4):
                        P.op("DVE", lambda e, oc=oc, tt2=tt2, rp=rp: e.scalar_tensor_tensor(
                            ckv_sb[:, S * oc + 512 * tt2: S * oc + 512 * (tt2 + 1)],
                            ckvraw[rp][:, 512 * oc:512 * (oc + 1)],
                            kvnw_sb[:, oc:oc + 1], rstdB[:, :],
                            ALU.mult, ALU.mult),
                            waits=[("act", tok_ex), t_kvnw] if oc == 0 else [],
                            inc="dve")
                    ss_free = ss_free + [("dve", P.cnt["dve"])]
                    rstd_free = [("dve", P.cnt["dve"])]
                    sq_freeB[rp] = [("pe", tok_ss)]
                    raw_free[rp] = [("dve", P.cnt["dve"])]

                for tt in range(16):
                    par_x = tt % 2
                    t_x = [P.dma("PL", xb2[par_x][:, :].rearrange("p (c q) -> p c q", c=16),
                                 t_xf[:, 512 * tt:512 * (tt + 1)].rearrange(
                                     "(c p) q -> p c q", p=128),
                                 waits=xb_free[par_x])]
                    xb_free[par_x] = []
                    t_cs = P.dma("SP", cosB[par_x][:, :],
                                 t_cosf[:, 512 * tt:512 * (tt + 1)], waits=cos_free[par_x])
                    t_sn = P.dma("SP", sinB[par_x][:, :],
                                 t_sinf[:, 512 * tt:512 * (tt + 1)])
                    cos_free[par_x] = []
                    mm_tok = {}
                    for oc in range(5):
                        par = oc % 3
                        if oc < 4:
                            dst, lo, w = (pkv[par], 128 * oc, 128)
                        else:
                            dst, lo, w = (prope, 512, 64)
                        def mk(c, dst=dst, lo=lo, w=w, par_x=par_x):
                            def f(e):
                                return e.matmul(
                                    dst[:, :],
                                    wkva_sb[:, 576 * c + lo:576 * c + lo + w],
                                    xb2[par_x][:, 512 * c:512 * (c + 1)],
                                    start=(c == 0), stop=(c == 15))
                            return f
                        for c in range(16):
                            waits = []
                            if c == 0:
                                waits = [t_x[-1], t_wk_ld[15]]
                                waits += (bank_free[par] if oc < 4 else rope_free)
                                if oc < 4:
                                    bank_free[par] = []
                                else:
                                    rope_free = []
                            P.op("PE", mk(c), waits=waits, inc="pe" if c == 15 else None)
                        mm_tok[oc] = P.cnt["pe"]
                        if oc < 4:
                            tok_cp = P.op("ACT", lambda e, par=par, oc=oc, rp=tt % 2: e.copy(
                                ckvraw[rp][:, 512 * oc:512 * (oc + 1)], pkv[par][:, :]),
                                waits=[("pe", mm_tok[oc])] + (raw_free[tt % 2] if oc == 0 else []), inc="act")
                            if oc == 0:
                                raw_free[tt % 2] = []
                            tok_sq = P.op("ACT", lambda e, par=par, oc=oc, rp=tt % 2: e.activation(
                                sqB[rp][:, 512 * oc:512 * (oc + 1)], pkv[par][:, :], AF.Square),
                                waits=(sq_freeB[tt % 2] if oc == 0 else []), inc="act")
                            if oc == 0:
                                sq_freeB[tt % 2] = []
                            bank_free[par] = [("act", tok_sq)]
                            mm_tok["sq", oc] = tok_sq
                        else:
                            tok_sqr = P.op("ACT", lambda e, rp=tt % 2: e.activation(
                                sqRB[rp][:, :], prope[:, :], AF.Square),
                                waits=[("pe", mm_tok[oc])] + kr_free, inc="act")
                            kr_free = []
                            mm_tok["sqr"] = tok_sqr
                            wts = [("pe", mm_tok[oc]), ("act", tok_sqr), t_sn, t_cs]
                            P.op("DVE", lambda e, p=par_x: e.tensor_mul(rmB1[:, :], prope[0:32, :], cosB[p][:, :]), waits=wts)
                            P.op("DVE", lambda e, p=par_x: e.tensor_mul(rmB2[:, :], prope[32:64, :], sinB[p][:, :]))
                            P.op("DVE", lambda e, tt=tt: e.tensor_sub(
                                krope_sb[0:32, 512 * tt:512 * (tt + 1)], rmB1[:, :], rmB2[:, :]))
                            P.op("DVE", lambda e, p=par_x: e.tensor_mul(rmB1[:, :], prope[32:64, :], cosB[p][:, :]))
                            P.op("DVE", lambda e, p=par_x: e.tensor_mul(rmB2[:, :], prope[0:32, :], sinB[p][:, :]))
                            tok_rope = P.op("DVE", lambda e, tt=tt: e.tensor_add(
                                krope_sb[32:64, 512 * tt:512 * (tt + 1)], rmB1[:, :], rmB2[:, :]), inc="dve")
                            cos_free[par_x] = [("dve", tok_rope)]
                            rope_free = [("dve", tok_rope), ("act", tok_sqr)]
                    if pending_B is not None:
                        flush_B(pending_B)
                    pending_B = dict(tt=tt, sq3=mm_tok["sq", 3], sqr=mm_tok["sqr"])
                    xb_free[par_x] = [("pe", mm_tok[4])]
                flush_B(pending_B)
                P.dma("SP", krope_sb[64:128, :], krope_sb[0:64, :],
                      waits=[("dve", P.cnt["dve"])], cls="st")
                P.op("SP", lambda e: e.nop(),
                     waits=P.dma_all("SP") + P.dma_all("PL") + [("dve", P.cnt["dve"])])
                if d_ckv is not None:
                    tk = P.op("DVE", lambda e: e.nop(), waits=[("dve", P.cnt["dve"])], inc="dve")
                    P.dma("SP", d_ckv, ckv_sb[:, :], waits=[("dve", tk)])
                    P.dma("SP", d_krope, krope_sb[:, :])
                    P.dma("SP", d_krn, krn_sb[:, :])
                    P.dma("SP", d_qmax, qmaxN[:, :])
                    P.op("SP", lambda e: e.nop(), waits=P.dma_all("SP") + P.dma_all("PL"))
                P.emit()

            if upto < 3:
                _final_wait(nc, P, t_out)
                return nc, dbg_aps

            P.phase()
            with ExitStack() as cph:
                wkvb_sb = cph.enter_context(nc.sbuf_tensor("wkvb_sb", [128, 4 * 4096], BF16))
                qpos_sb = cph.enter_context(nc.sbuf_tensor("qpos_sb", [128, OWN], F32))
                kposc_sb = cph.enter_context(nc.sbuf_tensor("kposc_sb", [128, 64], F32))
                knope_sb = cph.enter_context(nc.sbuf_tensor("knope_sb", [128, S], BF16))
                vtm_sb = cph.enter_context(nc.sbuf_tensor("vtm_sb", [128, S], BF16))
                qn_sb = [cph.enter_context(nc.sbuf_tensor(f"qn_sb{i}", [128, 512], BF16)) for i in range(2)]
                qr_sb = [cph.enter_context(nc.sbuf_tensor(f"qr_sb{i}", [128, 512], BF16)) for i in range(2)]
                ptb = [cph.enter_context(nc.sbuf_tensor(f"ptb{i}", [128, 512], BF16)) for i in range(4)]
                sqCs = cph.enter_context(nc.sbuf_tensor("sqCs", [128, 16 * 512], BF16))
                km_t = cph.enter_context(nc.sbuf_tensor("km_t", [1, 1], F32))
                kmax = cph.enter_context(nc.sbuf_tensor("kmax", [1, 1], F32))
                bias_sb = cph.enter_context(nc.sbuf_tensor("bias_sb", [128, 1], F32))
                bias_t = cph.enter_context(nc.sbuf_tensor("bias_t", [1, 1], F32))
                den_row = cph.enter_context(nc.sbuf_tensor("den_row", [1, 512], F32))
                dfull = cph.enter_context(nc.sbuf_tensor("dfull", [128, 512], F32))
                rrep = cph.enter_context(nc.sbuf_tensor("rrep", [128, 512], F32))
                ob_sb = [cph.enter_context(nc.sbuf_tensor(f"ob_sb{i}", [128, 512], BF16)) for i in range(2)]
                pool = [cph.enter_context(nc.psum_tensor(f"poolC{i}", [128, 512], F32)) for i in range(4)]
                po = [cph.enter_context(nc.psum_tensor(f"poC{i}", [128, 512], F32)) for i in range(2)]
                pden = cph.enter_context(nc.psum_tensor("pdenC", [128, 512], F32))
                pmisc = cph.enter_context(nc.psum_tensor("pmiscC", [128, 512], F32))

                t_wb = [P.dma("PL", wkvb_sb[:, 4096 * c:4096 * (c + 1)], t_wkvb[c])
                        for c in range(4)]
                t_qp = P.dma("SP", qpos_sb[:, :], t_qpos)
                t_kp = P.dma("SP", kposc_sb[:, :], t_kposc)

                NG = 64
                def q_load(gi, waits):
                    hh, gg = divmod(gi, 4)
                    par = gi % 2
                    tn = P.dma("SP", qn_sb[par][:, :],
                               q_scr[128 * hh:128 * (hh + 1), 512 * gg:512 * (gg + 1)],
                               waits=waits)
                    P.dma("SP", qr_sb[par][0:64, :],
                          q_scr[2048 + 64 * hh:2048 + 64 * (hh + 1), 512 * gg:512 * (gg + 1)])
                    tr = P.dma("SP", qr_sb[par][64:128, :],
                               q_scr[2048 + 64 * hh:2048 + 64 * (hh + 1), 512 * gg:512 * (gg + 1)])
                    return tr

                pool_free = {i: [] for i in range(4)}
                pt_free = {i: [] for i in range(4)}
                po_free = {0: [], 1: []}
                pden_free = []
                pmisc_free = []
                pkn_free = {0: [], 1: []}
                ob_free = {0: [], 1: []}
                kn_free = []
                vt_free = []
                knC_free = []
                sqs_free = []
                q_loaded = {}
                q_loaded[0] = q_load(0, [])
                for h in range(16):
                    P.op("DVE", lambda e: e.memset(kmax[:, :], -1e30), waits=knC_free)
                    knC_free = []
                    kn_cp = None
                    sq_toks = {}
                    for tt in range(16):
                        bk = tt % 4
                        for cc in range(4):
                            waits = []
                            if cc == 0:
                                waits = list(pool_free[bk]) + ([t_wb[3]] if (h == 0 and tt == 0) else [])
                                waits += kn_free
                                kn_free = []
                                pool_free[bk] = []
                            P.op("PE", lambda e, cc=cc, tt=tt, bk=bk, h=h: e.matmul(
                                pool[bk][:, :],
                                wkvb_sb[:, 4096 * cc + 256 * h:4096 * cc + 256 * h + 128],
                                ckv_sb[:, S * cc + 512 * tt:S * cc + 512 * (tt + 1)],
                                start=(cc == 0), stop=(cc == 3)),
                                waits=waits, inc="pe" if cc == 3 else None)
                        tok_mm = P.cnt["pe"]
                        kn_cp = P.op("ACT", lambda e, bk=bk, tt=tt: e.copy(
                            knope_sb[:, 512 * tt:512 * (tt + 1)], pool[bk][:, :]),
                            waits=[("pe", tok_mm)], inc="act")
                        pool_free[bk] = [("act", kn_cp)]
                        sq_toks[tt] = P.op("DVE", lambda e, tt=tt: e.tensor_mul(
                            sqCs[:, 512 * tt:512 * (tt + 1)],
                            knope_sb[:, 512 * tt:512 * (tt + 1)],
                            knope_sb[:, 512 * tt:512 * (tt + 1)]),
                            waits=[("act", kn_cp)] + (sqs_free if tt == 0 else []), inc="dve")
                        if tt == 0:
                            sqs_free = []
                    for tt in range(16):
                        knb = tt % 2
                        knbank = pmisc if knb == 0 else pden
                        knw = [("dve", sq_toks[tt])] + pkn_free[knb]
                        if knb == 0:
                            knw += pmisc_free
                            pmisc_free = []
                        else:
                            knw += pden_free
                            pden_free = []
                        tok_kn = P.op("PE", lambda e, knbank=knbank, tt=tt: e.matmul(
                            knbank[:, :], ones_mat[:, :], sqCs[:, 512 * tt:512 * (tt + 1)],
                            start=True, stop=True),
                            waits=knw, inc="pe")
                        pkn_free[knb] = []
                        tok_red = P.op("DVE", lambda e, knbank=knbank, tt=tt: e.tensor_add(
                            den_row[:, :], knbank[0:1, :],
                            krn_sb[:, 512 * tt:512 * (tt + 1)]),
                            waits=[("pe", tok_kn)], inc="dve")
                        pkn_free[knb] = [("dve", tok_red)]
                        P.op("DVE", lambda e: e.reduce_max(km_t[:, :], den_row[:, :], AX.X))
                        P.op("DVE", lambda e: e.tensor_max(kmax[:, :], kmax[:, :], km_t[:, :]))
                    sqs_free = [("pe", P.cnt["pe"])]
                    tok_kmax = P.cnt["dve"]
                    knC_free = [("dve", tok_kmax)]
                    P.op("DVE", lambda e, h=h: e.tensor_add(
                        qsum_t[:, :], qmaxN[:, h:h + 1], qmaxR[:, h:h + 1]),
                        waits=[("dve", tok_kmax)])
                    tok_b0 = P.op("DVE", lambda e: e.tensor_mul(
                        km_t[:, :], qsum_t[:, :], kmax[:, :]), inc="dve")
                    P.op("ACT", lambda e: e.activation(bias_t[:, :], km_t[:, :], AF.Ln),
                         waits=[("dve", tok_b0)])
                    tok_b1 = P.op("ACT", lambda e: e.activation(
                        bias_t[:, :], bias_t[:, :], AF.Exp, scale=0.5), inc="act")
                    tok_b2 = P.op("DVE", lambda e: e.tensor_scalar_mul(
                        bias_t[:, :], bias_t[:, :], -SSCALE * BOUND_MARGIN),
                        waits=[("act", tok_b1)], inc="dve")
                    vt_cp = None
                    for tg in range(16):
                        bk = tg % 4
                        for kt4 in range(4):
                            kt = 4 * tg + kt4
                            for cc in range(4):
                                waits = []
                                if cc == 0 and kt4 == 0:
                                    waits = list(pool_free[bk]) + vt_free
                                    vt_free = []
                                    pool_free[bk] = []
                                P.op("PE", lambda e, cc=cc, kt=kt, kt4=kt4, bk=bk, h=h: e.matmul(
                                    pool[bk][:, 128 * kt4:128 * (kt4 + 1)],
                                    ckv_sb[:, S * cc + 128 * kt:S * cc + 128 * (kt + 1)],
                                    wkvb_sb[:, 4096 * cc + 256 * h + 128:4096 * cc + 256 * h + 256],
                                    start=(cc == 0), stop=(cc == 3)),
                                    waits=waits, inc="pe" if (cc == 3 and kt4 == 3) else None)
                        tok_mm = P.cnt["pe"]
                        vt_cp = P.op("ACT", lambda e, bk=bk, tg=tg: e.copy(
                            vtm_sb[:, 512 * tg:512 * (tg + 1)], pool[bk][:, :]),
                            waits=[("pe", tok_mm)], inc="act")
                        pool_free[bk] = [("act", vt_cp)]
                    tok_b3 = P.op("PE", lambda e: e.matmul(
                        pmisc[:, 0:1], ones1f[:, :], bias_t[:, :], start=True, stop=True),
                        waits=[("dve", tok_b2)] + pmisc_free + pkn_free[0], inc="pe")
                    pmisc_free = []
                    pkn_free[0] = []
                    tok_bias = P.op("DVE", lambda e: e.tensor_copy(
                        bias_sb[:, :], pmisc[:, 0:1]), waits=[("pe", tok_b3)], inc="dve")
                    pmisc_free = [("dve", tok_bias)]
                    for gq in range(4):
                        gi = 4 * h + gq
                        niter = 16 * (gq + 1)
                        t_q = q_loaded[gi]
                        par_g = gq % 2
                        s_tok = {}
                        den_tok = {}
                        pt_tok = {}

                        def s_mm_pair(p):
                            it0, it1 = 2 * p, 2 * p + 1
                            bk0, bk1 = it0 % 4, it1 % 4
                            waits = []
                            if it0 < 4:
                                waits += pool_free[bk0]
                            pool_free[bk0] = []
                            if it0 == 0:
                                waits += [t_q, ("act", kn_cp)]
                            P.op("PE", lambda e, it=it0, bk=bk0, par_g=par_g: e.matmul(
                                pool[bk][:, :], knope_sb[:, 128 * it:128 * (it + 1)],
                                qn_sb[par_g][:, :], start=True, stop=False), waits=waits)
                            waits1 = []
                            if it1 < 4:
                                waits1 += pool_free[bk1]
                            pool_free[bk1] = []
                            P.op("PE", lambda e, it=it1, bk=bk1, par_g=par_g: e.matmul(
                                pool[bk][:, :], knope_sb[:, 128 * it:128 * (it + 1)],
                                qn_sb[par_g][:, :], start=True, stop=False), waits=waits1)
                            P.op("PE", lambda e, it=it0, bk=bk0, par_g=par_g: e.matmul(
                                pool[bk][:, :], krope_sb[0:64, 128 * it:128 * (it + 1)],
                                qr_sb[par_g][0:64, :], start=False, stop=True,
                                tile_position=(0, 0)))
                            tok = P.op("PE", lambda e, it=it1, bk=bk1, par_g=par_g: e.matmul(
                                pool[bk][:, :], krope_sb[64:128, 128 * it:128 * (it + 1)],
                                qr_sb[par_g][64:128, :], start=False, stop=True,
                                tile_position=(64, 0)), inc="pe")
                            s_tok[it0] = tok
                            s_tok[it1] = tok

                        def s_exp(it, pe_tok):
                            bk = it % 4
                            tok_e = P.op("ACT", lambda e, bk=bk: e.activation(
                                ptb[bk][:, :], pool[bk][:, :], AF.Exp,
                                bias=bias_sb[:, 0:1], scale=SSCALE),
                                waits=[("pe", pe_tok), ("dve", tok_bias)] + pt_free[bk], inc="act")
                            pt_free[bk] = []
                            pool_free[bk] = [("act", tok_e)]
                            if it >= 16 * gq:
                                tok_m = P.op("DVE", lambda e, it=it, bk=bk, gq=gq: e.scalar_tensor_tensor(
                                    ptb[bk][:, :], qpos_sb[:, 512 * gq:512 * (gq + 1)],
                                    kposc_sb[:, it:it + 1], ptb[bk][:, :],
                                    ALU.is_ge, ALU.mult),
                                    waits=[("act", tok_e), t_qp, t_kp], inc="dve")
                                pt_tok[it] = ("dve", tok_m)
                            else:
                                pt_tok[it] = ("act", tok_e)

                        for p in range(2):
                            s_mm_pair(p)
                            s_exp(2 * p, s_tok[2 * p])
                            s_exp(2 * p + 1, s_tok[2 * p + 1])
                        if gi + 1 < NG:
                            q_loaded[gi + 1] = q_load(gi + 1, [("pe", P.cnt["pe"])])
                        pend_bk = None
                        for it in range(niter):
                            bk = it % 4
                            waits_pv = [pt_tok[it]]
                            if it == 0:
                                waits_pv += po_free[par_g]
                                po_free[par_g] = []
                            P.op("PE", lambda e, it=it, bk=bk, par_g=par_g, niter=niter: e.matmul(
                                po[par_g][:, :], vtm_sb[:, 128 * it:128 * (it + 1)],
                                ptb[bk][:, :], start=(it == 0), stop=(it == niter - 1)),
                                waits=waits_pv)
                            den_w = []
                            if it == 0:
                                den_w = pden_free + pkn_free[1]
                                pden_free = []
                                pkn_free[1] = []
                            do_inc = (it % 2 == 1) or (it == niter - 1)
                            tok_pv = P.op("PE", lambda e, it=it, bk=bk, niter=niter: e.matmul(
                                pden[:, :], ones_mat[:, :], ptb[bk][:, :],
                                start=(it == 0), stop=(it == niter - 1)),
                                waits=den_w, inc="pe" if do_inc else None)
                            den_tok[it] = tok_pv
                            if do_inc:
                                pt_free[bk] = [("pe", tok_pv)]
                                if pend_bk is not None:
                                    pt_free[pend_bk] = [("pe", tok_pv)]
                                    pend_bk = None
                            else:
                                pend_bk = bk
                            if it % 2 == 1 and it + 3 < niter:
                                s_mm_pair((it + 3) // 2)
                                s_exp(it + 3, s_tok[it + 3])
                                s_exp(it + 4, s_tok[it + 4])
                        tok_dc = P.op("DVE", lambda e: e.tensor_copy(
                            dfull[:, :], pden[:, :]), waits=[("pe", P.cnt["pe"])], inc="dve")
                        pden_free = [("dve", tok_dc)]
                        tok_r = P.op("DVE", lambda e: e.reciprocal(
                            rrep[:, :], dfull[:, :]), inc="dve")
                        tok_ob = P.op("DVE", lambda e, par_g=par_g: e.tensor_mul(
                            ob_sb[par_g][:, :], po[par_g][:, :], rrep[:, :]),
                            waits=ob_free[par_g], inc="dve")
                        ob_free[par_g] = []
                        po_free[par_g] = [("dve", tok_ob)]
                        t_ost = P.dma("SP", o_scr[gq, :, h, :],
                                      ob_sb[par_g][:, :], waits=[("dve", tok_ob)], cls="st")
                        ob_free[par_g] = [t_ost]
                    kn_free = [("pe", P.cnt["pe"])]
                    vt_free = [("pe", P.cnt["pe"])]
                    if h == 15 and d_kn15 is not None:
                        P.dma("SP", d_kn15, knope_sb[:, :], waits=[("pe", P.cnt["pe"])])
                        P.dma("SP", d_vt15, vtm_sb[:, :])
                P.op("SP", lambda e: e.nop(), waits=P.dma_all("SP") + P.dma_all("PL") + [("dve", P.cnt["dve"])])
                if d_oscr is not None:
                    P.dma("SP", d_oscr, o_scr)
                    P.op("SP", lambda e: e.nop(), waits=P.dma_all("SP") + P.dma_all("PL"))
                P.emit()

        if upto < 4:
            _final_wait(nc, P, t_out)
            return nc, dbg_aps

        P.phase()
        with ExitStack() as dph:
            wob = [dph.enter_context(nc.sbuf_tensor(f"wob{i}", [128, 2048], BF16)) for i in range(2)]
            osb = [dph.enter_context(nc.sbuf_tensor(f"osb{i}", [128, 16 * 512], BF16)) for i in range(2)]
            xtile = [dph.enter_context(nc.sbuf_tensor(f"xtile{i}", [128, 512], F32)) for i in range(2)]
            r1 = dph.enter_context(nc.sbuf_tensor("r1", [128, 16 * 512], F32))
            r1b = dph.enter_context(nc.sbuf_tensor("r1b", [128, 16 * 512], BF16))
            sqD = dph.enter_context(nc.sbuf_tensor("sqD", [128, 16 * 512], BF16))
            m2rep = dph.enter_context(nc.sbuf_tensor("m2rep", [128, 512], F32))
            mrep = dph.enter_context(nc.sbuf_tensor("mrep", [128, 512], F32))
            rrepD = dph.enter_context(nc.sbuf_tensor("rrepD", [128, 512], F32))
            h1t = [dph.enter_context(nc.sbuf_tensor(f"h1t{i}", [128, 512], F32)) for i in range(2)]
            tmpD = dph.enter_context(nc.sbuf_tensor("tmpD", [128, 512], F32))
            pwo = [dph.enter_context(nc.psum_tensor(f"pwo{i}", [128, 512], F32)) for i in range(4)]
            pmD = dph.enter_context(nc.psum_tensor("pmD", [128, 512], F32))
            pvD = dph.enter_context(nc.psum_tensor("pvD", [128, 512], F32))


            bank_free = {i: [] for i in range(4)}
            wob_free = {0: [], 1: []}
            osb_free = {0: [], 1: []}
            r1b_free = []
            t_o = {}
            t_o[0] = [P.dma("SP", osb[0][:, :], o_scr[0])]
            xt_free = {0: [], 1: []}
            o_free = []
            stat_free = []
            bc_free = []
            h1_free = {0: [], 1: []}
            h1_stores = []
            for gq in range(4):
                opar = gq % 2
                if gq + 1 < 4:
                    t_o[gq + 1] = [P.dma("SP", osb[(gq + 1) % 2][:, :],
                                         o_scr[gq + 1], waits=osb_free[(gq + 1) % 2])]
                    osb_free[(gq + 1) % 2] = []
                mm_toks = {}
                for oc in range(16):
                    bk = oc % 4
                    par = oc % 2
                    wpar = oc % 2
                    t_w = P.dma("PL", wob[wpar][:, :], wo_bf[oc], waits=wob_free[wpar])
                    wob_free[wpar] = []
                    t_x = P.dma("PL", xtile[par][:, :],
                                t_xq[128 * oc:128 * (oc + 1), 512 * gq:512 * (gq + 1)],
                                waits=xt_free[par])
                    xt_free[par] = []
                    for c in range(16):
                        waits = []
                        if c == 0:
                            waits = [t_o[gq][-1], t_w] + bank_free[bk]
                            bank_free[bk] = []
                        P.op("PE", lambda e, c=c, oc=oc, bk=bk, wpar=wpar, opar=opar: e.matmul(
                            pwo[bk][:, :], wob[wpar][:, 128 * c:128 * (c + 1)],
                            osb[opar][:, 512 * c:512 * (c + 1)],
                            start=(c == 0), stop=(c == 15)),
                            waits=waits, inc="pe" if c == 15 else None)
                    mm_toks[oc] = P.cnt["pe"]
                    wob_free[wpar] = [("pe", mm_toks[oc])]
                    tok_r1 = P.op("DVE", lambda e, oc=oc, bk=bk, par=par: e.tensor_add(
                        r1[:, 512 * oc:512 * (oc + 1)], pwo[bk][:, :], xtile[par][:, :]),
                        waits=[("pe", mm_toks[oc]), t_x], inc="dve")
                    bank_free[bk] = [("dve", tok_r1)]
                    xt_free[par] = [("dve", tok_r1)]
                    tok_cb = P.op("ACT", lambda e, oc=oc: e.copy(
                        r1b[:, 512 * oc:512 * (oc + 1)], r1[:, 512 * oc:512 * (oc + 1)]),
                        waits=[("dve", tok_r1)] + (r1b_free if oc == 0 else []), inc="act")
                    if oc == 0:
                        r1b_free = []
                    tok_sq = P.op("ACT", lambda e, oc=oc: e.activation(
                        sqD[:, 512 * oc:512 * (oc + 1)],
                        r1b[:, 512 * oc:512 * (oc + 1)], AF.Square), inc="act")
                    sq_tok_last = tok_sq
                for oc in range(16):
                    P.op("PE", lambda e, oc=oc: e.matmul(
                        pmD[:, :], ones_mat[:, :], r1b[:, 512 * oc:512 * (oc + 1)],
                        start=(oc == 0), stop=(oc == 15)),
                        waits=([("act", sq_tok_last)] + stat_free) if oc == 0 else [],
                        inc="pe" if oc == 15 else None)
                stat_free = []
                for oc in range(16):
                    P.op("PE", lambda e, oc=oc: e.matmul(
                        pvD[:, :], ones_mat[:, :], sqD[:, 512 * oc:512 * (oc + 1)],
                        start=(oc == 0), stop=(oc == 15)),
                        inc="pe" if oc == 15 else None)
                r1b_free = [("pe", P.cnt["pe"])]
                tok_stats = P.cnt["pe"]
                P.op("ACT", lambda e: e.activation(
                    mrep[:, :], pmD[:, :], AF.Copy, scale=1.0 / 2048.0),
                    waits=[("pe", tok_stats)] + bc_free)
                bc_free = []
                P.op("ACT", lambda e: e.activation(
                    rrepD[:, :], pvD[:, :], AF.Copy, scale=1.0 / 2048.0))
                tok_m2 = P.op("ACT", lambda e: e.activation(
                    m2rep[:, :], mrep[:, :], AF.Square), inc="act")
                stat_free = [("act", tok_m2)]
                tok_var = P.op("DVE", lambda e: e.tensor_sub(
                    rrepD[:, :], rrepD[:, :], m2rep[:, :]), waits=[("act", tok_m2)], inc="dve")
                P.op("ACT", lambda e: e.activation(
                    rrepD[:, :], rrepD[:, :], AF.Ln, bias=consts_sb[:, 1:2]),
                    waits=[("dve", tok_var)])
                tok_cr = P.op("ACT", lambda e: e.activation(
                    rrepD[:, :], rrepD[:, :], AF.Exp, scale=-0.5), inc="act")
                bc_free = [("act", tok_cr)]
                for oc in range(16):
                    par = oc % 2
                    P.op("DVE", lambda e, oc=oc: e.tensor_sub(
                        tmpD[:, :], r1[:, 512 * oc:512 * (oc + 1)], mrep[:, :]),
                        waits=([("act", tok_cr)] if oc == 0 else []) + h1_free[par])
                    h1_free[par] = []
                    tok_h1 = P.op("DVE", lambda e, par=par: e.tensor_mul(
                        h1t[par][:, :], tmpD[:, :], rrepD[:, :]), inc="dve")
                    t_st = P.dma("ACT", h1_scr[128 * oc:128 * (oc + 1), 512 * gq:512 * (gq + 1)],
                                 h1t[par][:, :], waits=[("dve", tok_h1)], cls="st")
                    h1_free[par] = [t_st]
                    h1_stores.append(t_st)
                bc_free = bc_free + [("dve", P.cnt["dve"])]
                osb_free[opar] = [("pe", mm_toks[15])]
            P.op("SP", lambda e: e.nop(), waits=P.dma_all("SP") + P.dma_all("PL") + P.dma_all("ACT") + [("dve", P.cnt["dve"])])
            if d_h1 is not None:
                P.dma("SP", d_h1, h1_scr)
                P.op("SP", lambda e: e.nop(), waits=P.dma_all("SP") + P.dma_all("PL"))
            P.emit()

        if upto < 5:
            _final_wait(nc, P, t_out)
            return nc, dbg_aps

        P.phase()
        with ExitStack() as eph:
            h1b = eph.enter_context(nc.sbuf_tensor("h1b", [128, 16 * 512], BF16))
            fuse = eph.enter_context(nc.sbuf_tensor("fuse", [128, 64 * 512], BF16))
            wgb = [eph.enter_context(nc.sbuf_tensor(f"wgb{i}", [128, 2048], BF16)) for i in range(2)]
            wub = [eph.enter_context(nc.sbuf_tensor(f"wub{i}", [128, 2048], BF16)) for i in range(2)]
            wdb = [eph.enter_context(nc.sbuf_tensor(f"wdb{i}", [128, 8192], BF16)) for i in range(2)]
            sg_sb = eph.enter_context(nc.sbuf_tensor("sg_sb", [128, 64], F32))
            su_sb = eph.enter_context(nc.sbuf_tensor("su_sb", [128, 64], F32))
            sd_sb = eph.enter_context(nc.sbuf_tensor("sd_sb", [128, 16], F32))
            tg = [eph.enter_context(nc.sbuf_tensor(f"tgE{i}", [128, 512], BF16)) for i in range(2)]
            tu = [eph.enter_context(nc.sbuf_tensor(f"tuE{i}", [128, 512], BF16)) for i in range(2)]
            h1f = [eph.enter_context(nc.sbuf_tensor(f"h1f{i}", [128, 512], F32)) for i in range(2)]
            r2 = eph.enter_context(nc.sbuf_tensor("r2", [128, 16 * 512], F32))
            identE = eph.enter_context(nc.sbuf_tensor("identE", [128, 128], F32))
            rtm = eph.enter_context(nc.sbuf_tensor("rtm", [128, 2048], F32))
            sq2 = eph.enter_context(nc.sbuf_tensor("sq2", [128, 2048], F32))
            st1 = eph.enter_context(nc.sbuf_tensor("st1", [128, 1], F32))
            st2 = eph.enter_context(nc.sbuf_tensor("st2", [128, 1], F32))
            stv = eph.enter_context(nc.sbuf_tensor("stv", [128, 1], F32))
            outb = eph.enter_context(nc.sbuf_tensor("outb", [128, 2048], F32))
            psG = [eph.enter_context(nc.psum_tensor(f"psG{i}", [128, 512], F32)) for i in range(2)]
            psU = [eph.enter_context(nc.psum_tensor(f"psU{i}", [128, 512], F32)) for i in range(2)]
            psD = [eph.enter_context(nc.psum_tensor(f"psD{i}", [128, 512], F32)) for i in range(2)]

            t_sg = P.dma("SP", sg_sb[:, :], t_sg)
            t_su = P.dma("SP", su_sb[:, :], t_su)
            t_sd = P.dma("SP", sd_sb[:, :], t_sd)
            P.op("PL", lambda e: e.memset(identE[:, :], 1.0))
            tok_id = P.op("PL", lambda e: e.affine_select(
                out=identE[:, :], in_=identE[:, :], compare_op=ALU.is_equal,
                fill=0.0, base=0, pattern=[[-1, 128]], channel_multiplier=1),
                inc="plc")

            wg_free = {0: [], 1: []}
            wu_free = {0: [], 1: []}
            wd_free = {0: [], 1: []}
            gu_free = {0: [], 1: []}
            d_free = {0: [], 1: []}
            tg_free = {0: [], 1: []}
            tu_free = {0: [], 1: []}
            fuse_free = []
            h1b_free = []
            h1f_free = {0: [], 1: []}
            r2_free = []
            rtm_free = []
            out_free = []
            out_stores = []
            def load_h1b(sw, waits=()):
                src = h1_scr[:, 512 * sw:512 * (sw + 1)].rearrange(
                    "(c p) q -> p c q", p=128)
                dst = h1b[:, :].rearrange("p (c q) -> p c q", c=16)
                return [P.dma("PL", dst, src, waits=waits)]
            t_h1map = {}
            t_h1map[0] = load_h1b(0)
            for sw in range(4):
                t_h1 = t_h1map[sw]
                for ic in range(64):
                    par = ic % 2
                    t_g = P.dma("PL", wgb[par][:, :], t_wg[ic], waits=wg_free[par])
                    t_u = P.dma("PL", wub[par][:, :], t_wu[ic], waits=wu_free[par])
                    wg_free[par], wu_free[par] = [], []
                    for c in range(16):
                        waits = []
                        if c == 0:
                            waits = [t_g, t_h1[-1]] + gu_free[par] + (fuse_free if ic == 0 else [])
                            gu_free[par] = []
                            if ic == 0:
                                fuse_free = []
                        P.op("PE", lambda e, c=c, par=par: e.matmul(
                            psG[par][:, :], wgb[par][:, 128 * c:128 * (c + 1)],
                            h1b[:, 512 * c:512 * (c + 1)],
                            start=(c == 0), stop=(c == 15)), waits=waits)
                    for c in range(16):
                        waits = [t_u] if c == 0 else []
                        P.op("PE", lambda e, c=c, par=par: e.matmul(
                            psU[par][:, :], wub[par][:, 128 * c:128 * (c + 1)],
                            h1b[:, 512 * c:512 * (c + 1)],
                            start=(c == 0), stop=(c == 15)),
                            waits=waits, inc="pe" if c == 15 else None)
                    tok_mm = P.cnt["pe"]
                    wg_free[par] = [("pe", tok_mm)]
                    wu_free[par] = [("pe", tok_mm)]
                    tok_g = P.op("ACT", lambda e, par=par, ic=ic: e.activation(
                        tg[par][:, :], psG[par][:, :], AF.Silu,
                        scale=sg_sb[:, ic:ic + 1]),
                        waits=[("pe", tok_mm), t_sg] + tg_free[par], inc="act")
                    tg_free[par] = []
                    tok_u = P.op("DVE", lambda e, par=par, ic=ic: e.tensor_scalar_mul(
                        tu[par][:, :], psU[par][:, :], su_sb[:, ic:ic + 1]),
                        waits=[("pe", tok_mm), t_su] + tu_free[par], inc="dve")
                    tu_free[par] = []
                    gu_free[par] = [("act", tok_g), ("dve", tok_u)]
                    tok_f = P.op("DVE", lambda e, par=par, ic=ic: e.tensor_mul(
                        fuse[:, 512 * ic:512 * (ic + 1)], tg[par][:, :], tu[par][:, :]),
                        waits=[("act", tok_g)], inc="dve")
                    tg_free[par] = [("dve", tok_f)]
                    tu_free[par] = [("dve", tok_f)]
                tok_fuse_all = P.cnt["dve"]
                if sw + 1 < 4:
                    t_h1map[sw + 1] = load_h1b(sw + 1, waits=[("pe", P.cnt["pe"])])
                for oc in range(16):
                    par = oc % 2
                    t_wd_ld = P.dma("PL", wdb[par][:, :], t_wd[oc], waits=wd_free[par])
                    wd_free[par] = []
                    t_h1f = P.dma("SP", h1f[par][:, :],
                                  h1_scr[128 * oc:128 * (oc + 1), 512 * sw:512 * (sw + 1)],
                                  waits=h1f_free[par])
                    h1f_free[par] = []
                    for c in range(64):
                        waits = []
                        if c == 0:
                            waits = [t_wd_ld, ("dve", tok_fuse_all)] + d_free[par] + (r2_free if oc == 0 else [])
                            d_free[par] = []
                            if oc == 0:
                                r2_free = []
                        P.op("PE", lambda e, c=c, par=par: e.matmul(
                            psD[par][:, :], wdb[par][:, 128 * c:128 * (c + 1)],
                            fuse[:, 512 * c:512 * (c + 1)],
                            start=(c == 0), stop=(c == 63)),
                            waits=waits, inc="pe" if c == 63 else None)
                    tok_mm = P.cnt["pe"]
                    wd_free[par] = [("pe", tok_mm)]
                    tok_r2 = P.op("DVE", lambda e, oc=oc, par=par: e.scalar_tensor_tensor(
                        r2[:, 512 * oc:512 * (oc + 1)], psD[par][:, :],
                        sd_sb[:, oc:oc + 1], h1f[par][:, :], ALU.mult, ALU.add),
                        waits=[("pe", tok_mm), t_h1f, t_sd], inc="dve")
                    d_free[par] = [("dve", tok_r2)]
                    h1f_free[par] = [("dve", tok_r2)]
                tok_r2_all = P.cnt["dve"]
                fuse_free = [("pe", P.cnt["pe"])]
                tr_free = []
                for q4 in range(4):
                    for oc in range(16):
                        bank = (psG + psU)[oc // 4]
                        waits = []
                        if oc == 0:
                            waits = [("dve", tok_r2_all), ("plc", tok_id)] + tr_free
                            tr_free = []
                            if q4 == 0:
                                waits += gu_free[0] + gu_free[1]
                                gu_free[0], gu_free[1] = [], []
                        P.op("PE", lambda e, oc=oc, q4=q4, bank=bank: e.transpose(
                            bank[:, 128 * (oc % 4):128 * (oc % 4 + 1)],
                            r2[:, 512 * oc + 128 * q4:512 * oc + 128 * q4 + 128],
                            identE[:, :]),
                            waits=waits, inc="pe" if oc % 4 == 3 else None)
                    tok_tr = P.cnt["pe"]
                    tok_rtm = None
                    for bk in range(4):
                        tok_rtm = P.op("DVE", lambda e, bk=bk: e.tensor_copy(
                            rtm[:, 512 * bk:512 * (bk + 1)], (psG + psU)[bk][:, :]),
                            waits=[("pe", tok_tr)] if bk == 0 else [],
                            inc="dve" if bk == 3 else None)
                    tr_free = [("dve", tok_rtm)]
                    gu_free[0] = [("dve", tok_rtm)]
                    gu_free[1] = [("dve", tok_rtm)]
                    P.op("DVE", lambda e: e.reduce_sum(st1[:, :], rtm[:, :], AX.X),
                         waits=[("dve", tok_rtm)])
                    tok_sq = P.op("ACT", lambda e: e.activation(
                        sq2[:, :], rtm[:, :], AF.Square), waits=[("dve", tok_rtm)], inc="act")
                    tok_s2 = P.op("DVE", lambda e: e.reduce_sum(st2[:, :], sq2[:, :], AX.X),
                                  waits=[("act", tok_sq)], inc="dve")
                    P.op("DVE", lambda e: e.tensor_scalar_mul(st1[:, :], st1[:, :], 1.0 / 2048.0))
                    tok_sts = P.op("DVE", lambda e: e.tensor_scalar_mul(
                        st2[:, :], st2[:, :], 1.0 / 2048.0), inc="dve")
                    tok_m2 = P.op("ACT", lambda e: e.activation(
                        stv[:, :], st1[:, :], AF.Square), waits=[("dve", tok_sts)], inc="act")
                    tok_var = P.op("DVE", lambda e: e.tensor_sub(
                        stv[:, :], st2[:, :], stv[:, :]), waits=[("act", tok_m2)], inc="dve")
                    P.op("ACT", lambda e: e.activation(
                        stv[:, :], stv[:, :], AF.Ln, bias=consts_sb[:, 1:2]),
                        waits=[("dve", tok_var)])
                    tok_rstd = P.op("ACT", lambda e: e.activation(
                        stv[:, :], stv[:, :], AF.Exp, scale=-0.5), inc="act")
                    tok_out = P.op("DVE", lambda e: e.tensor_scalar(
                        outb[:, :], rtm[:, :], st1[:, :], stv[:, :],
                        ALU.subtract, ALU.mult),
                        waits=[("act", tok_rstd)] + out_free, inc="dve")
                    out_free = []
                    rtm_free = [("dve", tok_out)]
                    t_st = P.dma("SP", t_out[128 * (4 * sw + q4):128 * (4 * sw + q4 + 1), :],
                                 outb[:, :], waits=[("dve", tok_out)], cls="st")
                    out_stores.append(t_st)
                    out_free = [t_st]
                r2_free = [("pe", P.cnt["pe"])]
            P.op("SP", lambda e: e.nop(), waits=P.dma_all("SP") + P.dma_all("PL"))
            P.emit()

    return nc, dbg_aps


def _final_wait(nc, P, t_out):
    P.phase()
    with nc.sbuf_tensor("zz", [128, 2048], F32) as zz:
        P.op("PL", lambda e: e.memset(zz[:, :], 0.0), inc="plc")
        for i in range(16):
            P.dma("SP", t_out[128 * i:128 * (i + 1), :], zz[:, :],
                  waits=[("plc", P.cnt["plc"])])
        P.op("SP", lambda e: e.nop(), waits=P.dma_all("SP") + P.dma_all("PL"))
        P.emit()


_NC_CACHE = {}


def _get_nc(upto=5, dbg=()):
    key = (upto, tuple(sorted(dbg)))
    if key not in _NC_CACHE:
        _NC_CACHE[key] = build_nc(upto, dbg)
    return _NC_CACHE[key]


def kernel(**inputs):
    in_maps, tiles_per_core = host_prep(inputs)
    nc, _ = _get_nc()
    res = run_bass_kernel_spmd(nc, in_maps, core_ids=list(range(8)))
    out = np.zeros((B, S, D), np.float32)
    for core in range(8):
        b, tiles = tiles_per_core[core]
        r = res.results[core]["out"]
        for s, t in enumerate(tiles):
            out[b, 128 * t:128 * t + 128, :] = r[128 * s:128 * s + 128, :]
    return out


# version 18
# speedup vs baseline: 1.1893x; 1.1893x over previous
import numpy as np
import ml_dtypes
from contextlib import ExitStack

import concourse.bass as bass
import concourse.mybir as mybir
from concourse.bass_utils import run_bass_kernel_spmd

F32 = mybir.dt.float32
BF16 = mybir.dt.bfloat16
I8 = mybir.dt.int8
AF = mybir.ActivationFunctionType
ALU = mybir.AluOpType
AX = mybir.AxisListType

B, S, D, H = 2, 8192, 2048, 16
DN, DR, DK, KVL, DV = 128, 64, 192, 512, 128
INTER = 8192
NSLOT = 16
OWN = NSLOT * 128
RMS_EPS = 1.1920928955078125e-07
LN_EPS = 1e-5
SSCALE = float(1.0 / np.sqrt(DK))
BOUND_MARGIN = 1.04

ENGS = ("PE", "ACT", "DVE", "SP", "PL")


class Prog:

    NPOOL = 4

    def __init__(self, nc, ctx):
        self.nc = nc
        self.sem = {}
        self.cnt = {}
        for name in ("pe", "act", "dve", "plc"):
            self.sem[name] = ctx.enter_context(nc.semaphore(name))
            self.cnt[name] = 0
        self.pool = {}
        self.pool_idx = {}
        self.open = {}
        self.all_pool_sems = set()
        for q in ("SP", "PL", "SPst", "PLst", "ACTst"):
            names = []
            for i in range(self.NPOOL):
                nm = f"{q.lower()}{i}"
                self.sem[nm] = ctx.enter_context(nc.semaphore(nm))
                self.cnt[nm] = 0
                names.append(nm)
                self.all_pool_sems.add(nm)
            self.pool[q] = names
            self.pool_idx[q] = -1
            self.open[q] = None
        self.qs = None

    def phase(self):
        self.qs = {e: [] for e in ENGS}

    _AUTO = {"DVE": "dve", "ACT": "act", "PL": "plc"}

    def op(self, eng, fn, waits=(), inc=None, n=1, noclose=()):
        auto = self._AUTO.get(eng)
        if auto is not None and (inc is None or inc == auto):
            waits = list(waits) + [(auto, self.cnt[auto])]
            inc = auto
        ws = []
        for (s, v) in waits:
            if s in self.all_pool_sems:
                if s not in noclose:
                    v = self.cnt[s]
                    for q in self.pool:
                        if self.open[q] == s:
                            self.open[q] = None
            if v > 0:
                ws.append((s, v))
        dedup = {}
        for (s, v) in ws:
            dedup[s] = max(v, dedup.get(s, 0))
        ws = tuple(dedup.items())
        self.qs[eng].append((fn, ws, inc, n))
        if inc is not None:
            self.cnt[inc] += n
            return self.cnt[inc]
        return None

    def dma(self, queue, out, in_, waits=(), cls="ld"):
        eng = queue
        if cls == "st":
            queue = queue + "st"
        elif queue == "ACT":
            queue = "ACTst"
        extra = []
        if self.open[queue] is None:
            self.pool_idx[queue] += 1
            sem = self.pool[queue][self.pool_idx[queue] % self.NPOOL]
            self.open[queue] = sem
            if self.cnt[sem] > 0:
                extra = [(sem, self.cnt[sem])]
        sem = self.open[queue]
        self.op(eng, lambda e, o=out, i=in_: e.dma_start(out=o, in_=i),
                waits=list(waits) + extra, inc=sem, n=16, noclose=(sem,))
        return (sem, self.cnt[sem])

    def dma_all(self, queue):
        toks = []
        qs = (queue, queue + "st") if queue != "ACT" else ("ACTst",)
        for q in qs:
            self.open[q] = None
            toks += [(s, self.cnt[s]) for s in self.pool[q] if self.cnt[s] > 0]
        return toks

    def emit(self):
        blk_ctx = self.nc.Block()
        blk = blk_ctx.__enter__()
        try:
            for engname, attr in (("PE", "tensor"), ("ACT", "scalar"),
                                  ("DVE", "vector"), ("SP", "sync"),
                                  ("PL", "gpsimd")):
                ops = self.qs[engname]
                if not ops:
                    continue

                def run(e, ops=ops):
                    for fn, waits, inc, n in ops:
                        for (s, v) in waits:
                            e.wait_ge(self.sem[s], v)
                        i = fn(e)
                        if inc is not None:
                            i.then_inc(self.sem[inc], n)
                getattr(blk, attr)(run)
        finally:
            blk_ctx.__exit__(None, None, None)



def _rope_tables():
    freqs = 1.0 / 10000.0 ** (np.arange(0, DR, 2, dtype=np.float64) / DR)
    t = np.arange(S, dtype=np.float64)[:, None] * freqs[None, :]
    return (np.cos(t).T.astype(np.float32), np.sin(t).T.astype(np.float32))


def _wq_col_order():
    nope = np.concatenate([192 * h + np.arange(128) for h in range(H)])
    rope = []
    for r in range(8):
        h0, h1 = 2 * r, 2 * r + 1
        rope += [192 * h0 + 128 + 2 * np.arange(32),
                 192 * h1 + 128 + 2 * np.arange(32),
                 192 * h0 + 128 + 2 * np.arange(32) + 1,
                 192 * h1 + 128 + 2 * np.arange(32) + 1]
    return np.concatenate([nope] + rope)


def host_prep(inputs):
    x = np.ascontiguousarray(np.asarray(inputs["L0_attn_x"]))
    Wq = np.asarray(inputs["L0_attn_Wq"])
    Wkva = np.asarray(inputs["L0_attn_Wkv_a"])
    kvnw = np.asarray(inputs["L0_attn_kv_norm_w"])
    Wkvb = np.asarray(inputs["L0_attn_Wkv_b"])
    Wo = np.asarray(inputs["L0_attn_Wo"])
    wg = np.asarray(inputs["L0_ffn_w_gate"])
    wu = np.asarray(inputs["L0_ffn_w_up"])
    wd = np.asarray(inputs["L0_ffn_w_down"])
    sg = np.asarray(inputs["L0_ffn_s_gate"])
    su = np.asarray(inputs["L0_ffn_s_up"])
    sd = np.asarray(inputs["L0_ffn_s_down"])

    cosf, sinf = _rope_tables()
    cos2 = np.ascontiguousarray(np.vstack([cosf, cosf]))
    sin2 = np.ascontiguousarray(np.vstack([sinf, sinf]))

    wq_p = np.ascontiguousarray(
        Wq[:, _wq_col_order()].reshape(16, 128, 24, 128).transpose(2, 1, 0, 3)
        .reshape(24, 128, 2048))
    rc = 512 + np.concatenate([2 * np.arange(32), 2 * np.arange(32) + 1])
    wkva_p = np.ascontiguousarray(
        np.concatenate([Wkva[:, :512], Wkva[:, rc]], axis=1)
        .reshape(16, 128, 576))
    kvnw_p = np.ascontiguousarray(kvnw.reshape(4, 128).T)
    wkvb_p = np.ascontiguousarray(Wkvb.reshape(4, 128, 4096))
    wo_p = np.ascontiguousarray(
        Wo.reshape(16, 128, 16, 128).transpose(2, 1, 0, 3).reshape(16, 128, 2048))
    wg_p = np.ascontiguousarray(
        wg.reshape(16, 128, 64, 128).transpose(2, 1, 0, 3))
    wu_p = np.ascontiguousarray(
        wu.reshape(16, 128, 64, 128).transpose(2, 1, 0, 3))
    wd_p = np.ascontiguousarray(
        wd.reshape(64, 128, 16, 128).transpose(2, 1, 0, 3))
    sg_p = np.ascontiguousarray(sg.reshape(64, 128).T)
    su_p = np.ascontiguousarray(su.reshape(64, 128).T)
    sd_p = np.ascontiguousarray(sd.reshape(16, 128).T)
    selv = np.zeros((128, 2), np.float32)
    selv[0:32, 0] = 1.0
    selv[64:96, 0] = 1.0
    selv[32:64, 1] = 1.0
    selv[96:128, 1] = 1.0
    sel = np.ascontiguousarray(np.repeat(selv, 128, axis=1))
    kposc = (np.arange(128, dtype=np.float32)[:, None]
             + 128.0 * np.arange(64, dtype=np.float32)[None, :])
    kposc = np.ascontiguousarray(kposc)
    consts = np.zeros((128, 4), np.float32)
    consts[:, 0] = RMS_EPS
    consts[:, 1] = LN_EPS
    consts = np.ascontiguousarray(consts)

    xf = [np.ascontiguousarray(x[b].T) for b in range(B)]

    in_maps = []
    tiles_per_core = []
    for core in range(8):
        b, j = divmod(core, 4)
        tiles = [4 * s + j for s in range(NSLOT)]
        tiles_per_core.append((b, tiles))
        qcols = np.concatenate([np.arange(128 * t, 128 * t + 128)
                                for t in tiles])
        xq = np.ascontiguousarray(xf[b][:, qcols])
        qpos_rep = np.ascontiguousarray(
            np.broadcast_to(qcols.astype(np.float32)[None, :], (128, OWN)))
        cos_own = np.ascontiguousarray(cos2[:, qcols])
        sin_own = np.ascontiguousarray(sin2[:, qcols])
        in_maps.append(dict(
            xf=xf[b], xq=xq, qpos=qpos_rep,
            cos_own=cos_own, sin_own=sin_own, cosf=cosf, sinf=sinf,
            kposc=kposc, sel=sel, consts=consts,
            wq=wq_p, wkva=wkva_p, kvnw=kvnw_p, wkvb=wkvb_p, wo=wo_p,
            wg=wg_p, wu=wu_p, wd=wd_p, sg=sg_p, su=su_p, sd=sd_p,
        ))
    return in_maps, tiles_per_core



def build_nc(upto=5, dbg=()):
    nc = bass.Bass("TRN2", target_bir_lowering=False)
    dbg = set(dbg)

    t_xf = nc.dram_tensor("xf", [D, S], F32, kind="ExternalInput").ap()
    t_xq = nc.dram_tensor("xq", [D, OWN], F32, kind="ExternalInput").ap()
    t_qpos = nc.dram_tensor("qpos", [128, OWN], F32, kind="ExternalInput").ap()
    t_cos_own = nc.dram_tensor("cos_own", [64, OWN], F32, kind="ExternalInput").ap()
    t_sin_own = nc.dram_tensor("sin_own", [64, OWN], F32, kind="ExternalInput").ap()
    t_cosf = nc.dram_tensor("cosf", [32, S], F32, kind="ExternalInput").ap()
    t_sinf = nc.dram_tensor("sinf", [32, S], F32, kind="ExternalInput").ap()
    t_kposc = nc.dram_tensor("kposc", [128, 64], F32, kind="ExternalInput").ap()
    t_sel = nc.dram_tensor("sel", [128, 256], F32, kind="ExternalInput").ap()
    t_consts = nc.dram_tensor("consts", [128, 4], F32, kind="ExternalInput").ap()
    t_wq = nc.dram_tensor("wq", [24, 128, 2048], F32, kind="ExternalInput").ap()
    t_wkva = nc.dram_tensor("wkva", [16, 128, 576], F32, kind="ExternalInput").ap()
    t_kvnw = nc.dram_tensor("kvnw", [128, 4], F32, kind="ExternalInput").ap()
    t_wkvb = nc.dram_tensor("wkvb", [4, 128, 4096], F32, kind="ExternalInput").ap()
    t_wo = nc.dram_tensor("wo", [16, 128, 2048], F32, kind="ExternalInput").ap()
    t_wg = nc.dram_tensor("wg", [64, 128, 16, 128], I8, kind="ExternalInput").ap()
    t_wu = nc.dram_tensor("wu", [64, 128, 16, 128], I8, kind="ExternalInput").ap()
    t_wd = nc.dram_tensor("wd", [16, 128, 64, 128], I8, kind="ExternalInput").ap()
    t_sg = nc.dram_tensor("sg", [128, 64], F32, kind="ExternalInput").ap()
    t_su = nc.dram_tensor("su", [128, 64], F32, kind="ExternalInput").ap()
    t_sd = nc.dram_tensor("sd", [128, 16], F32, kind="ExternalInput").ap()

    t_out = nc.dram_tensor("out", [OWN, D], F32, kind="ExternalOutput").ap()

    q_scr = nc.dram_tensor("q_scr", [3072, OWN], BF16, kind="Internal").ap()
    o_scr = nc.dram_tensor("o_scr", [4, 128, 16, 512], BF16, kind="Internal").ap()
    h1_scr = nc.dram_tensor("h1_scr", [D, OWN], F32, kind="Internal").ap()
    wo_bf = nc.dram_tensor("wo_bf", [16, 128, 2048], BF16, kind="Internal").ap()

    dbg_aps = {}
    def dbg_out(name, shape, dtype):
        if name in dbg:
            dbg_aps[name] = nc.dram_tensor(
                "dbg_" + name, shape, dtype, kind="ExternalOutput").ap()
            return dbg_aps[name]
        return None

    d_qscr = dbg_out("qscr", [3072, OWN], BF16)
    d_ckv = dbg_out("ckv", [128, 4 * S], BF16)
    d_krope = dbg_out("krope", [64, S], BF16)
    d_qmax = dbg_out("qmax", [1, 16], F32)
    d_krn = dbg_out("krn", [1, S], BF16)
    d_oscr = dbg_out("oscr", [4, 128, 16, 512], BF16)
    d_h1 = dbg_out("h1", [D, OWN], F32)
    d_kn15 = dbg_out("kn15", [128, S], BF16)
    d_vt15 = dbg_out("vt15", [128, S], BF16)

    with ExitStack() as g:
        P = Prog(nc, g)
        qmaxN = g.enter_context(nc.sbuf_tensor("qmaxN", [1, 16], F32))
        qmaxR = g.enter_context(nc.sbuf_tensor("qmaxR", [1, 16], F32))
        qsum_t = g.enter_context(nc.sbuf_tensor("qsum_t", [1, 1], F32))
        ones128 = g.enter_context(nc.sbuf_tensor("ones128", [128, 1], BF16))
        ones1 = g.enter_context(nc.sbuf_tensor("ones1", [1, 128], BF16))
        ones1f = g.enter_context(nc.sbuf_tensor("ones1f", [1, 128], F32))
        ones_mat = g.enter_context(nc.sbuf_tensor("ones_mat", [128, 128], BF16))
        consts_sb = g.enter_context(nc.sbuf_tensor("consts_sb", [128, 4], F32))

        P.phase()
        P.op("PL", lambda e: e.memset(ones128[:, :], 1.0), inc="plc")
        P.op("PL", lambda e: e.memset(ones1[:, :], 1.0), inc="plc")
        P.op("PL", lambda e: e.memset(ones1f[:, :], 1.0), inc="plc")
        P.op("PL", lambda e: e.memset(ones_mat[:, :], 1.0), inc="plc")
        P.op("PL", lambda e: e.memset(qmaxN[:, :], -1e30), inc="plc")
        P.op("PL", lambda e: e.memset(qmaxR[:, :], -1e30), inc="plc")
        t_c = P.dma("SP", consts_sb[:, :], t_consts)
        init_done = [("plc", P.cnt["plc"]), t_c]

        with ExitStack() as a:
            wqb = [a.enter_context(nc.sbuf_tensor(f"wqb{i}", [128, 2048], BF16)) for i in range(2)]
            xb = [a.enter_context(nc.sbuf_tensor(f"xbA{i}", [128, 16 * 512], BF16)) for i in range(4)]
            cosA = a.enter_context(nc.sbuf_tensor("cosA", [64, OWN], F32))
            sinA = a.enter_context(nc.sbuf_tensor("sinA", [64, OWN], F32))
            sel_sb = a.enter_context(nc.sbuf_tensor("sel_sb", [128, 256], BF16))
            qev = [a.enter_context(nc.sbuf_tensor(f"qev{i}", [128, 512], BF16)) for i in range(2)]
            qevr = [a.enter_context(nc.sbuf_tensor(f"qevr{i}", [128, 512], BF16)) for i in range(2)]
            sqb = [a.enter_context(nc.sbuf_tensor(f"sqbA{i}", [128, 512], BF16)) for i in range(4)]
            rm1 = a.enter_context(nc.sbuf_tensor("rm1", [64, 512], F32))
            rm2 = a.enter_context(nc.sbuf_tensor("rm2", [64, 512], F32))
            qm_t = a.enter_context(nc.sbuf_tensor("qm_t", [2, 1], F32))
            pq = [a.enter_context(nc.psum_tensor(f"pqA{i}", [128, 512], F32)) for i in range(4)]
            pn = [a.enter_context(nc.psum_tensor(f"pnA{i}", [128, 512], F32)) for i in range(2)]
            pnR = [a.enter_context(nc.psum_tensor(f"pnR{i}", [128, 512], F32)) for i in range(2)]

            def load_xb(tt):
                src = t_xq[:, 512 * tt:512 * (tt + 1)].rearrange(
                    "(c p) q -> p c q", p=128)
                dst = xb[tt][:, :].rearrange("p (c q) -> p c q", c=16)
                return [P.dma("PL", dst, src)]
            t_xb = {0: load_xb(0)}
            P.dma("SP", cosA[:, :], t_cos_own)
            t_trig = P.dma("SP", sinA[:, :], t_sin_own)
            t_sel = P.dma("PL", sel_sb[:, :], t_sel)
            t_wocast = [P.dma("PL", wo_bf[c], t_wo[c], cls="st") for c in range(16)]

            bank_free = {i: [] for i in range(4)}
            wqb_free = {0: [], 1: []}
            qev_free = {0: [], 1: []}
            qevr_free = {0: [], 1: []}
            pn_free = {0: [], 1: []}
            sq_free = {i: [] for i in range(4)}
            t_stores = []
            pending_qn = []

            def flush_qn():
                nonlocal_state = None
                for (kind, ocb2, tt2, tok_sq2) in pending_qn:
                    par = tt2 % 2
                    spar = tt2
                    if kind == "nope":
                        h = ocb2
                        tok_n = P.op("PE", lambda e, par=par, spar=spar: e.matmul(
                            pn[par][:, :], ones_mat[:, :], sqb[spar][:, :],
                            start=True, stop=True),
                            waits=[("act", tok_sq2), ("plc", P.cnt["plc"])] + pn_free[par], inc="pe")
                        pn_free[par] = []
                        P.op("DVE", lambda e, par=par: e.reduce_max(
                            qm_t[0:1, :], pn[par][0:1, :], AX.X), waits=[("pe", tok_n)])
                        tok_mx = P.op("DVE", lambda e, h=h: e.tensor_max(
                            qmaxN[:, h:h + 1], qmaxN[:, h:h + 1], qm_t[0:1, :]), inc="dve")
                        pn_free[par] = [("dve", tok_mx)]
                    else:
                        r = ocb2 - 16
                        P.op("PE", lambda e, par=par, spar=spar: e.matmul(
                            pn[par][:, :], sel_sb[:, 0:128], sqb[spar][:, :],
                            start=True, stop=True),
                            waits=[("act", tok_sq2)] + pn_free[par])
                        tok_n = P.op("PE", lambda e, par=par, spar=spar: e.matmul(
                            pnR[par][:, :], sel_sb[:, 128:256], sqb[spar][:, :],
                            start=True, stop=True), inc="pe")
                        pn_free[par] = []
                        P.op("DVE", lambda e, par=par: e.reduce_max(
                            qm_t[0:1, :], pn[par][0:1, :], AX.X), waits=[("pe", tok_n)])
                        P.op("DVE", lambda e, r=r: e.tensor_max(
                            qmaxR[:, 2 * r:2 * r + 1], qmaxR[:, 2 * r:2 * r + 1], qm_t[0:1, :]))
                        P.op("DVE", lambda e, par=par: e.reduce_max(
                            qm_t[0:1, :], pnR[par][0:1, :], AX.X))
                        tok_mx = P.op("DVE", lambda e, r=r: e.tensor_max(
                            qmaxR[:, 2 * r + 1:2 * r + 2], qmaxR[:, 2 * r + 1:2 * r + 2], qm_t[0:1, :]), inc="dve")
                        pn_free[par] = [("dve", tok_mx)]
                    sq_free[spar] = [("pe", tok_n)]
                pending_qn.clear()

            for ocb in range(24):
                wpar = ocb % 2
                t_w = P.dma("PL", wqb[wpar][:, :], t_wq[ocb], waits=wqb_free[wpar])
                wqb_free[wpar] = []
                flush_qn()
                for tt in range(4):
                    bk = (4 * ocb + tt) % 4
                    par = tt % 2
                    for c in range(16):
                        waits = []
                        if c == 0:
                            waits = [t_w, t_xb[tt][-1]] + bank_free[bk] + ([t_sel] if ocb == 0 else [])
                            bank_free[bk] = []
                        P.op("PE", lambda e, c=c, bk=bk, tt=tt, wpar=wpar: e.matmul(
                            pq[bk][:, :], wqb[wpar][:, 128 * c:128 * (c + 1)],
                            xb[tt][:, 512 * c:512 * (c + 1)],
                            start=(c == 0), stop=(c == 15)),
                            waits=waits, inc="pe" if c == 15 else None)
                    tok_mm = P.cnt["pe"]
                    if ocb == 0 and tt + 1 < 4:
                        t_xb[tt + 1] = load_xb(tt + 1)
                    if ocb < 16:
                        h = ocb
                        tok_cp = P.op("ACT", lambda e, par=par, bk=bk: e.copy(
                            qev[par][:, :], pq[bk][:, :]),
                            waits=[("pe", tok_mm)] + qev_free[par], inc="act")
                        qev_free[par] = []
                        tok_sq = P.op("ACT", lambda e, tt=tt, bk=bk: e.activation(
                            sqb[tt][:, :], pq[bk][:, :], AF.Square),
                            waits=sq_free[tt], inc="act")
                        sq_free[tt] = []
                        bank_free[bk] = [("act", tok_sq)]
                        t_st = P.dma("SP", q_scr[128 * h:128 * (h + 1),
                                               512 * tt:512 * (tt + 1)],
                                     qev[par][:, :], waits=[("act", tok_cp)], cls="st")
                        qev_free[par] = [t_st]
                        t_stores.append(t_st)
                        pending_qn.append(("nope", ocb, tt, tok_sq))
                    else:
                        r = ocb - 16
                        tok_sq = P.op("ACT", lambda e, tt=tt, bk=bk: e.activation(
                            sqb[tt][:, :], pq[bk][:, :], AF.Square),
                            waits=[("pe", tok_mm)] + sq_free[tt], inc="act")
                        sq_free[tt] = []
                        pending_qn.append(("rope", ocb, tt, tok_sq))
                        cs = cosA[:, 512 * tt:512 * (tt + 1)]
                        sn = sinA[:, 512 * tt:512 * (tt + 1)]
                        e_ap = pq[bk][0:64, :]
                        o_ap = pq[bk][64:128, :]
                        wts = [("act", tok_sq), t_trig] + qevr_free[par]
                        qevr_free[par] = []
                        P.op("DVE", lambda e, a1=e_ap, c1=cs: e.tensor_mul(rm1[:, :], a1, c1), waits=wts)
                        P.op("DVE", lambda e, a1=o_ap, s1=sn: e.tensor_mul(rm2[:, :], a1, s1))
                        P.op("DVE", lambda e, par=par: e.tensor_sub(qevr[par][0:64, :], rm1[:, :], rm2[:, :]))
                        P.op("DVE", lambda e, a1=o_ap, c1=cs: e.tensor_mul(rm1[:, :], a1, c1))
                        P.op("DVE", lambda e, a1=e_ap, s1=sn: e.tensor_mul(rm2[:, :], a1, s1))
                        tok_rope = P.op("DVE", lambda e, par=par: e.tensor_add(
                            qevr[par][64:128, :], rm1[:, :], rm2[:, :]), inc="dve")
                        bank_free[bk] = [("dve", tok_rope)]
                        h0, h1 = 2 * r, 2 * r + 1
                        for (rows_s, rows_d) in (((0, 32), 2048 + 64 * h0),
                                                 ((32, 64), 2048 + 64 * h1),
                                                 ((64, 96), 2048 + 64 * h0 + 32),
                                                 ((96, 128), 2048 + 64 * h1 + 32)):
                            t_st = P.dma("SP",
                                         q_scr[rows_d:rows_d + 32, 512 * tt:512 * (tt + 1)],
                                         qevr[par][rows_s[0]:rows_s[1], :],
                                         waits=[("dve", tok_rope)], cls="st")
                            t_stores.append(t_st)
                        qevr_free[par] = [t_stores[-1]]
                wqb_free[wpar] = [("pe", P.cnt["pe"])]
            flush_qn()
            P.op("SP", lambda e: e.nop(), waits=P.dma_all("SP") + P.dma_all("PL") + [("dve", P.cnt["dve"])])
            if d_qscr is not None:
                P.dma("SP", d_qscr, q_scr)
                P.op("SP", lambda e: e.nop(), waits=P.dma_all("SP") + P.dma_all("PL"))
            P.emit()


        if upto < 2:
            _final_wait(nc, P, t_out)
            return nc, dbg_aps

        with ExitStack() as bc:
            ckv_sb = bc.enter_context(nc.sbuf_tensor("ckv_sb", [128, 4 * S], BF16))
            krope_sb = bc.enter_context(nc.sbuf_tensor("krope_sb", [128, S], BF16))
            krn_sb = bc.enter_context(nc.sbuf_tensor("krn_sb", [1, S], BF16))

            P.phase()
            with ExitStack() as b:
                wkva_sb = b.enter_context(nc.sbuf_tensor("wkva_sb", [128, 16 * 576], BF16))
                kvnw_sb = b.enter_context(nc.sbuf_tensor("kvnw_sb", [128, 4], F32))
                xb2 = [b.enter_context(nc.sbuf_tensor(f"xbB{i}", [128, 16 * 512], BF16)) for i in range(2)]
                cosB = [b.enter_context(nc.sbuf_tensor(f"cosB{i}", [32, 512], F32)) for i in range(2)]
                sinB = [b.enter_context(nc.sbuf_tensor(f"sinB{i}", [32, 512], F32)) for i in range(2)]
                ckvraw = [b.enter_context(nc.sbuf_tensor(f"ckvraw{i}", [128, 4 * 512], F32)) for i in range(2)]
                sqB = [b.enter_context(nc.sbuf_tensor(f"sqB{i}", [128, 4 * 512], BF16)) for i in range(2)]
                sqRB = [b.enter_context(nc.sbuf_tensor(f"sqRB{i}", [64, 512], BF16)) for i in range(2)]
                rstdB = b.enter_context(nc.sbuf_tensor("rstdB", [128, 512], F32))
                rmB1 = b.enter_context(nc.sbuf_tensor("rmB1", [32, 512], F32))
                rmB2 = b.enter_context(nc.sbuf_tensor("rmB2", [32, 512], F32))
                pkv = [b.enter_context(nc.psum_tensor(f"pkvB{i}", [128, 512], F32)) for i in range(3)]
                prope = b.enter_context(nc.psum_tensor("propeB", [64, 512], F32))
                pss = b.enter_context(nc.psum_tensor("pssB", [128, 512], F32))
                pkr = b.enter_context(nc.psum_tensor("pkrB", [128, 512], F32))


                t_wk_ld = [P.dma("PL", wkva_sb[:, 576 * c:576 * (c + 1)], t_wkva[c])
                           for c in range(16)]
                t_kvnw = P.dma("SP", kvnw_sb[:, :], t_kvnw)

                bank_free = {0: [], 1: [], 2: []}
                rope_free = []
                ss_free = []
                rstd_free = []
                kr_free = []
                raw_free = {0: [], 1: []}
                sq_freeB = {0: [], 1: []}
                xb_free = {0: [], 1: []}
                cos_free = {0: [], 1: []}
                pending_B = None

                def flush_B(pend):
                    nonlocal ss_free, rstd_free, kr_free, sq_freeB, raw_free
                    tt2 = pend["tt"]
                    rp = tt2 % 2
                    for oc in range(4):
                        P.op("PE", lambda e, oc=oc, rp=rp: e.matmul(
                            pss[:, :], ones_mat[:, :], sqB[rp][:, 512 * oc:512 * (oc + 1)],
                            start=(oc == 0), stop=(oc == 3)),
                            waits=([("act", pend["sq3"])] + ss_free) if oc == 0 else [],
                            inc="pe" if oc == 3 else None)
                    tok_ss = P.cnt["pe"]
                    ss_free = []
                    tok_kr = P.op("PE", lambda e, rp=rp: e.matmul(
                        pkr[:, :], ones_mat[0:64, :], sqRB[rp][:, :],
                        start=True, stop=True), waits=[("act", pend["sqr"])] + kr_free, inc="pe")
                    tok_krc = P.op("DVE", lambda e, tt2=tt2: e.tensor_copy(
                        krn_sb[:, 512 * tt2:512 * (tt2 + 1)], pkr[0:1, :]),
                        waits=[("pe", tok_kr)], inc="dve")
                    kr_free = [("dve", tok_krc)]
                    tok_ln = P.op("ACT", lambda e: e.activation(
                        rstdB[:, :], pss[:, :], AF.Ln,
                        bias=consts_sb[:, 0:1], scale=1.0 / 512.0),
                        waits=[("pe", tok_ss), t_c] + rstd_free)
                    rstd_free = []
                    tok_ex = P.op("ACT", lambda e: e.activation(
                        rstdB[:, :], rstdB[:, :], AF.Exp, scale=-0.5), inc="act")
                    ss_free = [("act", tok_ex)]
                    for oc in range(4):
                        P.op("DVE", lambda e, oc=oc, tt2=tt2, rp=rp: e.scalar_tensor_tensor(
                            ckv_sb[:, S * oc + 512 * tt2: S * oc + 512 * (tt2 + 1)],
                            ckvraw[rp][:, 512 * oc:512 * (oc + 1)],
                            kvnw_sb[:, oc:oc + 1], rstdB[:, :],
                            ALU.mult, ALU.mult),
                            waits=[("act", tok_ex), t_kvnw] if oc == 0 else [],
                            inc="dve")
                    ss_free = ss_free + [("dve", P.cnt["dve"])]
                    rstd_free = [("dve", P.cnt["dve"])]
                    sq_freeB[rp] = [("pe", tok_ss)]
                    raw_free[rp] = [("dve", P.cnt["dve"])]

                for tt in range(16):
                    par_x = tt % 2
                    t_x = [P.dma("PL", xb2[par_x][:, :].rearrange("p (c q) -> p c q", c=16),
                                 t_xf[:, 512 * tt:512 * (tt + 1)].rearrange(
                                     "(c p) q -> p c q", p=128),
                                 waits=xb_free[par_x])]
                    xb_free[par_x] = []
                    t_cs = P.dma("SP", cosB[par_x][:, :],
                                 t_cosf[:, 512 * tt:512 * (tt + 1)], waits=cos_free[par_x])
                    t_sn = P.dma("SP", sinB[par_x][:, :],
                                 t_sinf[:, 512 * tt:512 * (tt + 1)])
                    cos_free[par_x] = []
                    mm_tok = {}
                    for oc in range(5):
                        par = oc % 3
                        if oc < 4:
                            dst, lo, w = (pkv[par], 128 * oc, 128)
                        else:
                            dst, lo, w = (prope, 512, 64)
                        def mk(c, dst=dst, lo=lo, w=w, par_x=par_x):
                            def f(e):
                                return e.matmul(
                                    dst[:, :],
                                    wkva_sb[:, 576 * c + lo:576 * c + lo + w],
                                    xb2[par_x][:, 512 * c:512 * (c + 1)],
                                    start=(c == 0), stop=(c == 15))
                            return f
                        for c in range(16):
                            waits = []
                            if c == 0:
                                waits = [t_x[-1], t_wk_ld[15]]
                                waits += (bank_free[par] if oc < 4 else rope_free)
                                if oc < 4:
                                    bank_free[par] = []
                                else:
                                    rope_free = []
                            P.op("PE", mk(c), waits=waits, inc="pe" if c == 15 else None)
                        mm_tok[oc] = P.cnt["pe"]
                        if oc < 4:
                            tok_cp = P.op("ACT", lambda e, par=par, oc=oc, rp=tt % 2: e.copy(
                                ckvraw[rp][:, 512 * oc:512 * (oc + 1)], pkv[par][:, :]),
                                waits=[("pe", mm_tok[oc])] + (raw_free[tt % 2] if oc == 0 else []), inc="act")
                            if oc == 0:
                                raw_free[tt % 2] = []
                            tok_sq = P.op("ACT", lambda e, par=par, oc=oc, rp=tt % 2: e.activation(
                                sqB[rp][:, 512 * oc:512 * (oc + 1)], pkv[par][:, :], AF.Square),
                                waits=(sq_freeB[tt % 2] if oc == 0 else []), inc="act")
                            if oc == 0:
                                sq_freeB[tt % 2] = []
                            bank_free[par] = [("act", tok_sq)]
                            mm_tok["sq", oc] = tok_sq
                        else:
                            tok_sqr = P.op("ACT", lambda e, rp=tt % 2: e.activation(
                                sqRB[rp][:, :], prope[:, :], AF.Square),
                                waits=[("pe", mm_tok[oc])] + kr_free, inc="act")
                            kr_free = []
                            mm_tok["sqr"] = tok_sqr
                            wts = [("pe", mm_tok[oc]), ("act", tok_sqr), t_sn, t_cs]
                            P.op("DVE", lambda e, p=par_x: e.tensor_mul(rmB1[:, :], prope[0:32, :], cosB[p][:, :]), waits=wts)
                            P.op("DVE", lambda e, p=par_x: e.tensor_mul(rmB2[:, :], prope[32:64, :], sinB[p][:, :]))
                            P.op("DVE", lambda e, tt=tt: e.tensor_sub(
                                krope_sb[0:32, 512 * tt:512 * (tt + 1)], rmB1[:, :], rmB2[:, :]))
                            P.op("DVE", lambda e, p=par_x: e.tensor_mul(rmB1[:, :], prope[32:64, :], cosB[p][:, :]))
                            P.op("DVE", lambda e, p=par_x: e.tensor_mul(rmB2[:, :], prope[0:32, :], sinB[p][:, :]))
                            tok_rope = P.op("DVE", lambda e, tt=tt: e.tensor_add(
                                krope_sb[32:64, 512 * tt:512 * (tt + 1)], rmB1[:, :], rmB2[:, :]), inc="dve")
                            cos_free[par_x] = [("dve", tok_rope)]
                            rope_free = [("dve", tok_rope), ("act", tok_sqr)]
                    if pending_B is not None:
                        flush_B(pending_B)
                    pending_B = dict(tt=tt, sq3=mm_tok["sq", 3], sqr=mm_tok["sqr"])
                    xb_free[par_x] = [("pe", mm_tok[4])]
                flush_B(pending_B)
                P.dma("SP", krope_sb[64:128, :], krope_sb[0:64, :],
                      waits=[("dve", P.cnt["dve"])], cls="st")
                P.op("SP", lambda e: e.nop(),
                     waits=P.dma_all("SP") + P.dma_all("PL") + [("dve", P.cnt["dve"])])
                if d_ckv is not None:
                    tk = P.op("DVE", lambda e: e.nop(), waits=[("dve", P.cnt["dve"])], inc="dve")
                    P.dma("SP", d_ckv, ckv_sb[:, :], waits=[("dve", tk)])
                    P.dma("SP", d_krope, krope_sb[:, :])
                    P.dma("SP", d_krn, krn_sb[:, :])
                    P.dma("SP", d_qmax, qmaxN[:, :])
                    P.op("SP", lambda e: e.nop(), waits=P.dma_all("SP") + P.dma_all("PL"))
                P.emit()

            if upto < 3:
                _final_wait(nc, P, t_out)
                return nc, dbg_aps

            P.phase()
            with ExitStack() as cph:
                wkvb_sb = cph.enter_context(nc.sbuf_tensor("wkvb_sb", [128, 4 * 4096], BF16))
                qpos_sb = cph.enter_context(nc.sbuf_tensor("qpos_sb", [128, OWN], F32))
                kposc_sb = cph.enter_context(nc.sbuf_tensor("kposc_sb", [128, 64], F32))
                knope_sb = cph.enter_context(nc.sbuf_tensor("knope_sb", [128, S], BF16))
                vtm_sb = cph.enter_context(nc.sbuf_tensor("vtm_sb", [128, S], BF16))
                qn_sb = [cph.enter_context(nc.sbuf_tensor(f"qn_sb{i}", [128, 512], BF16)) for i in range(2)]
                qr_sb = [cph.enter_context(nc.sbuf_tensor(f"qr_sb{i}", [128, 512], BF16)) for i in range(2)]
                ptb = [cph.enter_context(nc.sbuf_tensor(f"ptb{i}", [128, 512], BF16)) for i in range(4)]
                sqCs = cph.enter_context(nc.sbuf_tensor("sqCs", [128, 16 * 512], BF16))
                km_t = cph.enter_context(nc.sbuf_tensor("km_t", [1, 1], F32))
                kmax = cph.enter_context(nc.sbuf_tensor("kmax", [1, 1], F32))
                bias_sb = cph.enter_context(nc.sbuf_tensor("bias_sb", [128, 1], F32))
                bias_t = cph.enter_context(nc.sbuf_tensor("bias_t", [1, 1], F32))
                den_row = cph.enter_context(nc.sbuf_tensor("den_row", [1, 512], F32))
                dfull = cph.enter_context(nc.sbuf_tensor("dfull", [128, 512], F32))
                rrep = cph.enter_context(nc.sbuf_tensor("rrep", [128, 512], F32))
                ob_sb = [cph.enter_context(nc.sbuf_tensor(f"ob_sb{i}", [128, 512], BF16)) for i in range(2)]
                pool = [cph.enter_context(nc.psum_tensor(f"poolC{i}", [128, 512], F32)) for i in range(4)]
                po = [cph.enter_context(nc.psum_tensor(f"poC{i}", [128, 512], F32)) for i in range(2)]
                pden = cph.enter_context(nc.psum_tensor("pdenC", [128, 512], F32))
                pmisc = cph.enter_context(nc.psum_tensor("pmiscC", [128, 512], F32))

                t_wb = [P.dma("PL", wkvb_sb[:, 4096 * c:4096 * (c + 1)], t_wkvb[c])
                        for c in range(4)]
                t_qp = P.dma("SP", qpos_sb[:, :], t_qpos)
                t_kp = P.dma("SP", kposc_sb[:, :], t_kposc)

                NG = 64
                def q_load(gi, waits):
                    hh, gg = divmod(gi, 4)
                    par = gi % 2
                    tn = P.dma("SP", qn_sb[par][:, :],
                               q_scr[128 * hh:128 * (hh + 1), 512 * gg:512 * (gg + 1)],
                               waits=waits)
                    P.dma("SP", qr_sb[par][0:64, :],
                          q_scr[2048 + 64 * hh:2048 + 64 * (hh + 1), 512 * gg:512 * (gg + 1)])
                    tr = P.dma("SP", qr_sb[par][64:128, :],
                               q_scr[2048 + 64 * hh:2048 + 64 * (hh + 1), 512 * gg:512 * (gg + 1)])
                    return tr

                pool_free = {i: [] for i in range(4)}
                pt_free = {i: [] for i in range(4)}
                po_free = {0: [], 1: []}
                pden_free = []
                pmisc_free = []
                pkn_free = {0: [], 1: []}
                ob_free = {0: [], 1: []}
                kn_free = []
                vt_free = []
                knC_free = []
                sqs_free = []
                q_loaded = {}
                q_loaded[0] = q_load(0, [])
                for h in range(16):
                    P.op("DVE", lambda e: e.memset(kmax[:, :], -1e30), waits=knC_free)
                    knC_free = []
                    kn_cp = None
                    sq_toks = {}
                    for tt in range(16):
                        bk = tt % 4
                        for cc in range(4):
                            waits = []
                            if cc == 0:
                                waits = list(pool_free[bk]) + ([t_wb[3]] if (h == 0 and tt == 0) else [])
                                waits += kn_free
                                kn_free = []
                                pool_free[bk] = []
                            P.op("PE", lambda e, cc=cc, tt=tt, bk=bk, h=h: e.matmul(
                                pool[bk][:, :],
                                wkvb_sb[:, 4096 * cc + 256 * h:4096 * cc + 256 * h + 128],
                                ckv_sb[:, S * cc + 512 * tt:S * cc + 512 * (tt + 1)],
                                start=(cc == 0), stop=(cc == 3)),
                                waits=waits, inc="pe" if cc == 3 else None)
                        tok_mm = P.cnt["pe"]
                        kn_cp = P.op("ACT", lambda e, bk=bk, tt=tt: e.copy(
                            knope_sb[:, 512 * tt:512 * (tt + 1)], pool[bk][:, :]),
                            waits=[("pe", tok_mm)], inc="act")
                        pool_free[bk] = [("act", kn_cp)]
                        sq_toks[tt] = P.op("DVE", lambda e, tt=tt: e.tensor_mul(
                            sqCs[:, 512 * tt:512 * (tt + 1)],
                            knope_sb[:, 512 * tt:512 * (tt + 1)],
                            knope_sb[:, 512 * tt:512 * (tt + 1)]),
                            waits=[("act", kn_cp)] + (sqs_free if tt == 0 else []), inc="dve")
                        if tt == 0:
                            sqs_free = []
                    for tt in range(16):
                        knb = tt % 2
                        knbank = pmisc if knb == 0 else pden
                        knw = [("dve", sq_toks[tt])] + pkn_free[knb]
                        if knb == 0:
                            knw += pmisc_free
                            pmisc_free = []
                        else:
                            knw += pden_free
                            pden_free = []
                        tok_kn = P.op("PE", lambda e, knbank=knbank, tt=tt: e.matmul(
                            knbank[:, :], ones_mat[:, :], sqCs[:, 512 * tt:512 * (tt + 1)],
                            start=True, stop=True),
                            waits=knw, inc="pe")
                        pkn_free[knb] = []
                        tok_red = P.op("DVE", lambda e, knbank=knbank, tt=tt: e.tensor_add(
                            den_row[:, :], knbank[0:1, :],
                            krn_sb[:, 512 * tt:512 * (tt + 1)]),
                            waits=[("pe", tok_kn)], inc="dve")
                        pkn_free[knb] = [("dve", tok_red)]
                        P.op("DVE", lambda e: e.reduce_max(km_t[:, :], den_row[:, :], AX.X))
                        P.op("DVE", lambda e: e.tensor_max(kmax[:, :], kmax[:, :], km_t[:, :]))
                    sqs_free = [("pe", P.cnt["pe"])]
                    tok_kmax = P.cnt["dve"]
                    knC_free = [("dve", tok_kmax)]
                    P.op("DVE", lambda e, h=h: e.tensor_add(
                        qsum_t[:, :], qmaxN[:, h:h + 1], qmaxR[:, h:h + 1]),
                        waits=[("dve", tok_kmax)])
                    tok_b0 = P.op("DVE", lambda e: e.tensor_mul(
                        km_t[:, :], qsum_t[:, :], kmax[:, :]), inc="dve")
                    P.op("ACT", lambda e: e.activation(bias_t[:, :], km_t[:, :], AF.Ln),
                         waits=[("dve", tok_b0)])
                    tok_b1 = P.op("ACT", lambda e: e.activation(
                        bias_t[:, :], bias_t[:, :], AF.Exp, scale=0.5), inc="act")
                    tok_b2 = P.op("DVE", lambda e: e.tensor_scalar_mul(
                        bias_t[:, :], bias_t[:, :], -SSCALE * BOUND_MARGIN),
                        waits=[("act", tok_b1)], inc="dve")
                    vt_cp = None
                    for tg in range(16):
                        bk = tg % 4
                        for kt4 in range(4):
                            kt = 4 * tg + kt4
                            for cc in range(4):
                                waits = []
                                if cc == 0 and kt4 == 0:
                                    waits = list(pool_free[bk]) + vt_free
                                    vt_free = []
                                    pool_free[bk] = []
                                P.op("PE", lambda e, cc=cc, kt=kt, kt4=kt4, bk=bk, h=h: e.matmul(
                                    pool[bk][:, 128 * kt4:128 * (kt4 + 1)],
                                    ckv_sb[:, S * cc + 128 * kt:S * cc + 128 * (kt + 1)],
                                    wkvb_sb[:, 4096 * cc + 256 * h + 128:4096 * cc + 256 * h + 256],
                                    start=(cc == 0), stop=(cc == 3)),
                                    waits=waits, inc="pe" if (cc == 3 and kt4 == 3) else None)
                        tok_mm = P.cnt["pe"]
                        vt_cp = P.op("ACT", lambda e, bk=bk, tg=tg: e.copy(
                            vtm_sb[:, 512 * tg:512 * (tg + 1)], pool[bk][:, :]),
                            waits=[("pe", tok_mm)], inc="act")
                        pool_free[bk] = [("act", vt_cp)]
                    tok_b3 = P.op("PE", lambda e: e.matmul(
                        pmisc[:, 0:1], ones1f[:, :], bias_t[:, :], start=True, stop=True),
                        waits=[("dve", tok_b2)] + pmisc_free + pkn_free[0], inc="pe")
                    pmisc_free = []
                    pkn_free[0] = []
                    tok_bias = P.op("DVE", lambda e: e.tensor_copy(
                        bias_sb[:, :], pmisc[:, 0:1]), waits=[("pe", tok_b3)], inc="dve")
                    pmisc_free = [("dve", tok_bias)]
                    for gq in range(4):
                        gi = 4 * h + gq
                        niter = 16 * (gq + 1)
                        t_q = q_loaded[gi]
                        par_g = gq % 2
                        s_tok = {}
                        den_tok = {}
                        pt_tok = {}

                        def s_mm_pair(p):
                            it0, it1 = 2 * p, 2 * p + 1
                            bk0, bk1 = it0 % 4, it1 % 4
                            waits = []
                            if it0 < 4:
                                waits += pool_free[bk0]
                            pool_free[bk0] = []
                            if it0 == 0:
                                waits += [t_q, ("act", kn_cp)]
                            P.op("PE", lambda e, it=it0, bk=bk0, par_g=par_g: e.matmul(
                                pool[bk][:, :], knope_sb[:, 128 * it:128 * (it + 1)],
                                qn_sb[par_g][:, :], start=True, stop=False), waits=waits)
                            waits1 = []
                            if it1 < 4:
                                waits1 += pool_free[bk1]
                            pool_free[bk1] = []
                            P.op("PE", lambda e, it=it1, bk=bk1, par_g=par_g: e.matmul(
                                pool[bk][:, :], knope_sb[:, 128 * it:128 * (it + 1)],
                                qn_sb[par_g][:, :], start=True, stop=False), waits=waits1)
                            P.op("PE", lambda e, it=it0, bk=bk0, par_g=par_g: e.matmul(
                                pool[bk][:, :], krope_sb[0:64, 128 * it:128 * (it + 1)],
                                qr_sb[par_g][0:64, :], start=False, stop=True,
                                tile_position=(0, 0)))
                            tok = P.op("PE", lambda e, it=it1, bk=bk1, par_g=par_g: e.matmul(
                                pool[bk][:, :], krope_sb[64:128, 128 * it:128 * (it + 1)],
                                qr_sb[par_g][64:128, :], start=False, stop=True,
                                tile_position=(64, 0)), inc="pe")
                            s_tok[it0] = tok
                            s_tok[it1] = tok

                        def s_exp(it, pe_tok):
                            bk = it % 4
                            tok_e = P.op("ACT", lambda e, bk=bk: e.activation(
                                ptb[bk][:, :], pool[bk][:, :], AF.Exp,
                                bias=bias_sb[:, 0:1], scale=SSCALE),
                                waits=[("pe", pe_tok), ("dve", tok_bias)] + pt_free[bk], inc="act")
                            pt_free[bk] = []
                            pool_free[bk] = [("act", tok_e)]
                            if it >= 16 * gq:
                                tok_m = P.op("DVE", lambda e, it=it, bk=bk, gq=gq: e.scalar_tensor_tensor(
                                    ptb[bk][:, :], qpos_sb[:, 512 * gq:512 * (gq + 1)],
                                    kposc_sb[:, it:it + 1], ptb[bk][:, :],
                                    ALU.is_ge, ALU.mult),
                                    waits=[("act", tok_e), t_qp, t_kp], inc="dve")
                                pt_tok[it] = ("dve", tok_m)
                            else:
                                pt_tok[it] = ("act", tok_e)

                        for p in range(2):
                            s_mm_pair(p)
                            s_exp(2 * p, s_tok[2 * p])
                            s_exp(2 * p + 1, s_tok[2 * p + 1])
                        if gi + 1 < NG:
                            q_loaded[gi + 1] = q_load(gi + 1, [("pe", P.cnt["pe"])])
                        pend_bk = None
                        for it in range(niter):
                            bk = it % 4
                            waits_pv = [pt_tok[it]]
                            if it == 0:
                                waits_pv += po_free[par_g]
                                po_free[par_g] = []
                            P.op("PE", lambda e, it=it, bk=bk, par_g=par_g, niter=niter: e.matmul(
                                po[par_g][:, :], vtm_sb[:, 128 * it:128 * (it + 1)],
                                ptb[bk][:, :], start=(it == 0), stop=(it == niter - 1)),
                                waits=waits_pv)
                            den_w = []
                            if it == 0:
                                den_w = pden_free + pkn_free[1]
                                pden_free = []
                                pkn_free[1] = []
                            do_inc = (it % 2 == 1) or (it == niter - 1)
                            tok_pv = P.op("PE", lambda e, it=it, bk=bk, niter=niter: e.matmul(
                                pden[:, :], ones_mat[:, :], ptb[bk][:, :],
                                start=(it == 0), stop=(it == niter - 1)),
                                waits=den_w, inc="pe" if do_inc else None)
                            den_tok[it] = tok_pv
                            if do_inc:
                                pt_free[bk] = [("pe", tok_pv)]
                                if pend_bk is not None:
                                    pt_free[pend_bk] = [("pe", tok_pv)]
                                    pend_bk = None
                            else:
                                pend_bk = bk
                            if it % 2 == 1 and it + 3 < niter:
                                s_mm_pair((it + 3) // 2)
                                s_exp(it + 3, s_tok[it + 3])
                                s_exp(it + 4, s_tok[it + 4])
                        tok_dc = P.op("DVE", lambda e: e.tensor_copy(
                            dfull[:, :], pden[:, :]), waits=[("pe", P.cnt["pe"])], inc="dve")
                        pden_free = [("dve", tok_dc)]
                        tok_r = P.op("DVE", lambda e: e.reciprocal(
                            rrep[:, :], dfull[:, :]), inc="dve")
                        tok_ob = P.op("DVE", lambda e, par_g=par_g: e.tensor_mul(
                            ob_sb[par_g][:, :], po[par_g][:, :], rrep[:, :]),
                            waits=ob_free[par_g], inc="dve")
                        ob_free[par_g] = []
                        po_free[par_g] = [("dve", tok_ob)]
                        t_ost = P.dma("SP", o_scr[gq, :, h, :],
                                      ob_sb[par_g][:, :], waits=[("dve", tok_ob)], cls="st")
                        ob_free[par_g] = [t_ost]
                    kn_free = [("pe", P.cnt["pe"])]
                    vt_free = [("pe", P.cnt["pe"])]
                    if h == 15 and d_kn15 is not None:
                        P.dma("SP", d_kn15, knope_sb[:, :], waits=[("pe", P.cnt["pe"])])
                        P.dma("SP", d_vt15, vtm_sb[:, :])
                P.op("SP", lambda e: e.nop(), waits=P.dma_all("SP") + P.dma_all("PL") + [("dve", P.cnt["dve"])])
                if d_oscr is not None:
                    P.dma("SP", d_oscr, o_scr)
                    P.op("SP", lambda e: e.nop(), waits=P.dma_all("SP") + P.dma_all("PL"))
                P.emit()

        if upto < 4:
            _final_wait(nc, P, t_out)
            return nc, dbg_aps

        P.phase()
        with ExitStack() as dph:
            wob = [dph.enter_context(nc.sbuf_tensor(f"wob{i}", [128, 2048], BF16)) for i in range(2)]
            osb = [dph.enter_context(nc.sbuf_tensor(f"osb{i}", [128, 16 * 512], BF16)) for i in range(2)]
            xtile = [dph.enter_context(nc.sbuf_tensor(f"xtile{i}", [128, 512], F32)) for i in range(2)]
            r1 = [dph.enter_context(nc.sbuf_tensor(f"r1_{i}", [128, 16 * 512], F32)) for i in range(2)]
            r1b = dph.enter_context(nc.sbuf_tensor("r1b", [128, 16 * 512], BF16))
            sqD = dph.enter_context(nc.sbuf_tensor("sqD", [128, 16 * 512], BF16))
            m2rep = dph.enter_context(nc.sbuf_tensor("m2rep", [128, 512], F32))
            mrep = [dph.enter_context(nc.sbuf_tensor(f"mrep{i}", [128, 512], F32)) for i in range(2)]
            rrepD = [dph.enter_context(nc.sbuf_tensor(f"rrepD{i}", [128, 512], F32)) for i in range(2)]
            h1t = [dph.enter_context(nc.sbuf_tensor(f"h1t{i}", [128, 512], F32)) for i in range(4)]
            tmpD = [dph.enter_context(nc.sbuf_tensor(f"tmpD{i}", [128, 512], F32)) for i in range(2)]
            pwo = [dph.enter_context(nc.psum_tensor(f"pwo{i}", [128, 512], F32)) for i in range(4)]
            pmD = dph.enter_context(nc.psum_tensor("pmD", [128, 512], F32))
            pvD = dph.enter_context(nc.psum_tensor("pvD", [128, 512], F32))


            bank_free = {i: [] for i in range(4)}
            wob_free = {0: [], 1: []}
            osb_free = {0: [], 1: []}
            r1b_free = []
            r1_free = {0: [], 1: []}
            bc_free = {0: [], 1: []}
            pending_D = []

            def flush_D(pend):
                gq2, opar2, tok_cr2 = pend["gq"], pend["opar"], pend["tok_cr"]
                toks = []
                for oc in range(16):
                    par = oc % 2
                    eng = "DVE" if oc % 2 == 0 else "PL"
                    P.op(eng, lambda e, oc=oc, opar2=opar2: e.tensor_sub(
                        tmpD[par][:, :] if False else tmpD[oc % 2][:, :],
                        r1[opar2][:, 512 * oc:512 * (oc + 1)], mrep[opar2][:, :]),
                        waits=([("act", tok_cr2)] if oc < 2 else []) + h1_free[oc % 4])
                    tok_h1 = P.op(eng, lambda e, oc=oc, opar2=opar2: e.tensor_mul(
                        h1t[oc % 4][:, :], tmpD[oc % 2][:, :], rrepD[opar2][:, :]),
                        inc="dve" if eng == "DVE" else "plc")
                    h1_free[oc % 4] = []
                    t_st = P.dma("ACT", h1_scr[128 * oc:128 * (oc + 1), 512 * gq2:512 * (gq2 + 1)],
                                 h1t[oc % 4][:, :],
                                 waits=[("dve" if eng == "DVE" else "plc", tok_h1)], cls="st")
                    h1_free[oc % 4] = [t_st]
                    h1_stores.append(t_st)
                    toks.append(("dve" if eng == "DVE" else "plc", tok_h1))
                r1_free[opar2] = toks[-2:]
                bc_free[opar2] = toks[-2:]
            t_o = {}
            t_o[0] = [P.dma("SP", osb[0][:, :], o_scr[0])]
            xt_free = {0: [], 1: []}
            o_free = []
            stat_free = []
            h1_free = {0: [], 1: [], 2: [], 3: []}
            h1_stores = []
            for gq in range(4):
                opar = gq % 2
                if gq + 1 < 4:
                    t_o[gq + 1] = [P.dma("SP", osb[(gq + 1) % 2][:, :],
                                         o_scr[gq + 1], waits=osb_free[(gq + 1) % 2])]
                    osb_free[(gq + 1) % 2] = []
                mm_toks = {}
                for oc in range(16):
                    bk = oc % 4
                    par = oc % 2
                    wpar = oc % 2
                    t_w = P.dma("PL", wob[wpar][:, :], wo_bf[oc], waits=wob_free[wpar])
                    wob_free[wpar] = []
                    t_x = P.dma("PL", xtile[par][:, :],
                                t_xq[128 * oc:128 * (oc + 1), 512 * gq:512 * (gq + 1)],
                                waits=xt_free[par])
                    xt_free[par] = []
                    for c in range(16):
                        waits = []
                        if c == 0:
                            waits = [t_o[gq][-1], t_w] + bank_free[bk]
                            bank_free[bk] = []
                        P.op("PE", lambda e, c=c, oc=oc, bk=bk, wpar=wpar, opar=opar: e.matmul(
                            pwo[bk][:, :], wob[wpar][:, 128 * c:128 * (c + 1)],
                            osb[opar][:, 512 * c:512 * (c + 1)],
                            start=(c == 0), stop=(c == 15)),
                            waits=waits, inc="pe" if c == 15 else None)
                    mm_toks[oc] = P.cnt["pe"]
                    wob_free[wpar] = [("pe", mm_toks[oc])]
                    tok_r1 = P.op("DVE", lambda e, oc=oc, bk=bk, par=par, opar=opar: e.tensor_add(
                        r1[opar][:, 512 * oc:512 * (oc + 1)], pwo[bk][:, :], xtile[par][:, :]),
                        waits=[("pe", mm_toks[oc]), t_x] + (r1_free[opar] if oc == 0 else []), inc="dve")
                    if oc == 0:
                        r1_free[opar] = []
                    bank_free[bk] = [("dve", tok_r1)]
                    xt_free[par] = [("dve", tok_r1)]
                    tok_cb = P.op("ACT", lambda e, oc=oc, opar=opar: e.copy(
                        r1b[:, 512 * oc:512 * (oc + 1)], r1[opar][:, 512 * oc:512 * (oc + 1)]),
                        waits=[("dve", tok_r1)] + (r1b_free if oc == 0 else []), inc="act")
                    if oc == 0:
                        r1b_free = []
                    tok_sq = P.op("ACT", lambda e, oc=oc: e.activation(
                        sqD[:, 512 * oc:512 * (oc + 1)],
                        r1b[:, 512 * oc:512 * (oc + 1)], AF.Square), inc="act")
                    sq_tok_last = tok_sq
                for oc in range(16):
                    P.op("PE", lambda e, oc=oc: e.matmul(
                        pmD[:, :], ones_mat[:, :], r1b[:, 512 * oc:512 * (oc + 1)],
                        start=(oc == 0), stop=(oc == 15)),
                        waits=([("act", sq_tok_last)] + stat_free) if oc == 0 else [],
                        inc="pe" if oc == 15 else None)
                stat_free = []
                for oc in range(16):
                    P.op("PE", lambda e, oc=oc: e.matmul(
                        pvD[:, :], ones_mat[:, :], sqD[:, 512 * oc:512 * (oc + 1)],
                        start=(oc == 0), stop=(oc == 15)),
                        inc="pe" if oc == 15 else None)
                r1b_free = [("pe", P.cnt["pe"])]
                tok_stats = P.cnt["pe"]
                P.op("ACT", lambda e, opar=opar: e.activation(
                    mrep[opar][:, :], pmD[:, :], AF.Copy, scale=1.0 / 2048.0),
                    waits=[("pe", tok_stats)] + bc_free[opar])
                bc_free[opar] = []
                P.op("ACT", lambda e, opar=opar: e.activation(
                    rrepD[opar][:, :], pvD[:, :], AF.Copy, scale=1.0 / 2048.0))
                tok_m2 = P.op("ACT", lambda e, opar=opar: e.activation(
                    m2rep[:, :], mrep[opar][:, :], AF.Square), inc="act")
                stat_free = [("act", tok_m2)]
                tok_var = P.op("DVE", lambda e, opar=opar: e.tensor_sub(
                    rrepD[opar][:, :], rrepD[opar][:, :], m2rep[:, :]), waits=[("act", tok_m2)], inc="dve")
                P.op("ACT", lambda e, opar=opar: e.activation(
                    rrepD[opar][:, :], rrepD[opar][:, :], AF.Ln, bias=consts_sb[:, 1:2]),
                    waits=[("dve", tok_var)])
                tok_cr = P.op("ACT", lambda e, opar=opar: e.activation(
                    rrepD[opar][:, :], rrepD[opar][:, :], AF.Exp, scale=-0.5), inc="act")
                pending_D.append(dict(gq=gq, opar=opar, tok_cr=tok_cr))
                if len(pending_D) > 1:
                    flush_D(pending_D.pop(0))
                osb_free[opar] = [("pe", mm_toks[15])]
            flush_D(pending_D.pop(0))
            P.op("SP", lambda e: e.nop(), waits=P.dma_all("SP") + P.dma_all("PL") + P.dma_all("ACT") + [("dve", P.cnt["dve"])])
            if d_h1 is not None:
                P.dma("SP", d_h1, h1_scr)
                P.op("SP", lambda e: e.nop(), waits=P.dma_all("SP") + P.dma_all("PL"))
            P.emit()

        if upto < 5:
            _final_wait(nc, P, t_out)
            return nc, dbg_aps

        P.phase()
        with ExitStack() as eph:
            h1b = eph.enter_context(nc.sbuf_tensor("h1b", [128, 16 * 512], BF16))
            fuse = eph.enter_context(nc.sbuf_tensor("fuse", [128, 64 * 512], BF16))
            wgb = [eph.enter_context(nc.sbuf_tensor(f"wgb{i}", [128, 2048], BF16)) for i in range(2)]
            wub = [eph.enter_context(nc.sbuf_tensor(f"wub{i}", [128, 2048], BF16)) for i in range(2)]
            wdb = [eph.enter_context(nc.sbuf_tensor(f"wdb{i}", [128, 8192], BF16)) for i in range(2)]
            sg_sb = eph.enter_context(nc.sbuf_tensor("sg_sb", [128, 64], F32))
            su_sb = eph.enter_context(nc.sbuf_tensor("su_sb", [128, 64], F32))
            sd_sb = eph.enter_context(nc.sbuf_tensor("sd_sb", [128, 16], F32))
            tg = [eph.enter_context(nc.sbuf_tensor(f"tgE{i}", [128, 512], BF16)) for i in range(2)]
            tu = [eph.enter_context(nc.sbuf_tensor(f"tuE{i}", [128, 512], BF16)) for i in range(2)]
            h1f = [eph.enter_context(nc.sbuf_tensor(f"h1f{i}", [128, 512], F32)) for i in range(2)]
            r2 = eph.enter_context(nc.sbuf_tensor("r2", [128, 16 * 512], F32))
            identE = eph.enter_context(nc.sbuf_tensor("identE", [128, 128], F32))
            rtm = eph.enter_context(nc.sbuf_tensor("rtm", [128, 2048], F32))
            sq2 = eph.enter_context(nc.sbuf_tensor("sq2", [128, 2048], F32))
            st1 = eph.enter_context(nc.sbuf_tensor("st1", [128, 1], F32))
            st2 = eph.enter_context(nc.sbuf_tensor("st2", [128, 1], F32))
            stv = eph.enter_context(nc.sbuf_tensor("stv", [128, 1], F32))
            outb = eph.enter_context(nc.sbuf_tensor("outb", [128, 2048], F32))
            psG = [eph.enter_context(nc.psum_tensor(f"psG{i}", [128, 512], F32)) for i in range(2)]
            psU = [eph.enter_context(nc.psum_tensor(f"psU{i}", [128, 512], F32)) for i in range(2)]
            psD = [eph.enter_context(nc.psum_tensor(f"psD{i}", [128, 512], F32)) for i in range(2)]

            t_sg = P.dma("SP", sg_sb[:, :], t_sg)
            t_su = P.dma("SP", su_sb[:, :], t_su)
            t_sd = P.dma("SP", sd_sb[:, :], t_sd)
            P.op("PL", lambda e: e.memset(identE[:, :], 1.0))
            tok_id = P.op("PL", lambda e: e.affine_select(
                out=identE[:, :], in_=identE[:, :], compare_op=ALU.is_equal,
                fill=0.0, base=0, pattern=[[-1, 128]], channel_multiplier=1),
                inc="plc")

            wg_free = {0: [], 1: []}
            wu_free = {0: [], 1: []}
            wd_free = {0: [], 1: []}
            gu_free = {0: [], 1: []}
            d_free = {0: [], 1: []}
            tg_free = {0: [], 1: []}
            tu_free = {0: [], 1: []}
            fuse_free = []
            h1b_free = []
            h1f_free = {0: [], 1: []}
            r2_free = []
            rtm_free = []
            out_free = []
            out_stores = []
            def load_h1b(sw, waits=()):
                src = h1_scr[:, 512 * sw:512 * (sw + 1)].rearrange(
                    "(c p) q -> p c q", p=128)
                dst = h1b[:, :].rearrange("p (c q) -> p c q", c=16)
                return [P.dma("PL", dst, src, waits=waits)]
            t_h1map = {}
            t_h1map[0] = load_h1b(0)
            for sw in range(4):
                t_h1 = t_h1map[sw]
                for ic in range(64):
                    par = ic % 2
                    t_g = P.dma("PL", wgb[par][:, :], t_wg[ic], waits=wg_free[par])
                    t_u = P.dma("PL", wub[par][:, :], t_wu[ic], waits=wu_free[par])
                    wg_free[par], wu_free[par] = [], []
                    for c in range(16):
                        waits = []
                        if c == 0:
                            waits = [t_g, t_h1[-1]] + gu_free[par] + (fuse_free if ic == 0 else [])
                            gu_free[par] = []
                            if ic == 0:
                                fuse_free = []
                        P.op("PE", lambda e, c=c, par=par: e.matmul(
                            psG[par][:, :], wgb[par][:, 128 * c:128 * (c + 1)],
                            h1b[:, 512 * c:512 * (c + 1)],
                            start=(c == 0), stop=(c == 15)), waits=waits)
                    for c in range(16):
                        waits = [t_u] if c == 0 else []
                        P.op("PE", lambda e, c=c, par=par: e.matmul(
                            psU[par][:, :], wub[par][:, 128 * c:128 * (c + 1)],
                            h1b[:, 512 * c:512 * (c + 1)],
                            start=(c == 0), stop=(c == 15)),
                            waits=waits, inc="pe" if c == 15 else None)
                    tok_mm = P.cnt["pe"]
                    wg_free[par] = [("pe", tok_mm)]
                    wu_free[par] = [("pe", tok_mm)]
                    tok_g = P.op("ACT", lambda e, par=par, ic=ic: e.activation(
                        tg[par][:, :], psG[par][:, :], AF.Silu,
                        scale=sg_sb[:, ic:ic + 1]),
                        waits=[("pe", tok_mm), t_sg] + tg_free[par], inc="act")
                    tg_free[par] = []
                    tok_u = P.op("DVE", lambda e, par=par, ic=ic: e.tensor_scalar_mul(
                        tu[par][:, :], psU[par][:, :], su_sb[:, ic:ic + 1]),
                        waits=[("pe", tok_mm), t_su] + tu_free[par], inc="dve")
                    tu_free[par] = []
                    gu_free[par] = [("act", tok_g), ("dve", tok_u)]
                    tok_f = P.op("DVE", lambda e, par=par, ic=ic: e.tensor_mul(
                        fuse[:, 512 * ic:512 * (ic + 1)], tg[par][:, :], tu[par][:, :]),
                        waits=[("act", tok_g)], inc="dve")
                    tg_free[par] = [("dve", tok_f)]
                    tu_free[par] = [("dve", tok_f)]
                tok_fuse_all = P.cnt["dve"]
                if sw + 1 < 4:
                    t_h1map[sw + 1] = load_h1b(sw + 1, waits=[("pe", P.cnt["pe"])])
                for oc in range(16):
                    par = oc % 2
                    t_wd_ld = P.dma("PL", wdb[par][:, :], t_wd[oc], waits=wd_free[par])
                    wd_free[par] = []
                    t_h1f = P.dma("SP", h1f[par][:, :],
                                  h1_scr[128 * oc:128 * (oc + 1), 512 * sw:512 * (sw + 1)],
                                  waits=h1f_free[par])
                    h1f_free[par] = []
                    for c in range(64):
                        waits = []
                        if c == 0:
                            waits = [t_wd_ld, ("dve", tok_fuse_all)] + d_free[par] + (r2_free if oc == 0 else [])
                            d_free[par] = []
                            if oc == 0:
                                r2_free = []
                        P.op("PE", lambda e, c=c, par=par: e.matmul(
                            psD[par][:, :], wdb[par][:, 128 * c:128 * (c + 1)],
                            fuse[:, 512 * c:512 * (c + 1)],
                            start=(c == 0), stop=(c == 63)),
                            waits=waits, inc="pe" if c == 63 else None)
                    tok_mm = P.cnt["pe"]
                    wd_free[par] = [("pe", tok_mm)]
                    tok_r2 = P.op("DVE", lambda e, oc=oc, par=par: e.scalar_tensor_tensor(
                        r2[:, 512 * oc:512 * (oc + 1)], psD[par][:, :],
                        sd_sb[:, oc:oc + 1], h1f[par][:, :], ALU.mult, ALU.add),
                        waits=[("pe", tok_mm), t_h1f, t_sd], inc="dve")
                    d_free[par] = [("dve", tok_r2)]
                    h1f_free[par] = [("dve", tok_r2)]
                tok_r2_all = P.cnt["dve"]
                fuse_free = [("pe", P.cnt["pe"])]
                tr_free = []
                for q4 in range(4):
                    for oc in range(16):
                        bank = (psG + psU)[oc // 4]
                        waits = []
                        if oc == 0:
                            waits = [("dve", tok_r2_all), ("plc", tok_id)] + tr_free
                            tr_free = []
                            if q4 == 0:
                                waits += gu_free[0] + gu_free[1]
                                gu_free[0], gu_free[1] = [], []
                        P.op("PE", lambda e, oc=oc, q4=q4, bank=bank: e.transpose(
                            bank[:, 128 * (oc % 4):128 * (oc % 4 + 1)],
                            r2[:, 512 * oc + 128 * q4:512 * oc + 128 * q4 + 128],
                            identE[:, :]),
                            waits=waits, inc="pe" if oc % 4 == 3 else None)
                    tok_tr = P.cnt["pe"]
                    tok_rtm = None
                    for bk in range(4):
                        tok_rtm = P.op("DVE", lambda e, bk=bk: e.tensor_copy(
                            rtm[:, 512 * bk:512 * (bk + 1)], (psG + psU)[bk][:, :]),
                            waits=[("pe", tok_tr)] if bk == 0 else [],
                            inc="dve" if bk == 3 else None)
                    tr_free = [("dve", tok_rtm)]
                    gu_free[0] = [("dve", tok_rtm)]
                    gu_free[1] = [("dve", tok_rtm)]
                    P.op("DVE", lambda e: e.reduce_sum(st1[:, :], rtm[:, :], AX.X),
                         waits=[("dve", tok_rtm)])
                    tok_sq = P.op("ACT", lambda e: e.activation(
                        sq2[:, :], rtm[:, :], AF.Square), waits=[("dve", tok_rtm)], inc="act")
                    tok_s2 = P.op("DVE", lambda e: e.reduce_sum(st2[:, :], sq2[:, :], AX.X),
                                  waits=[("act", tok_sq)], inc="dve")
                    P.op("DVE", lambda e: e.tensor_scalar_mul(st1[:, :], st1[:, :], 1.0 / 2048.0))
                    tok_sts = P.op("DVE", lambda e: e.tensor_scalar_mul(
                        st2[:, :], st2[:, :], 1.0 / 2048.0), inc="dve")
                    tok_m2 = P.op("ACT", lambda e: e.activation(
                        stv[:, :], st1[:, :], AF.Square), waits=[("dve", tok_sts)], inc="act")
                    tok_var = P.op("DVE", lambda e: e.tensor_sub(
                        stv[:, :], st2[:, :], stv[:, :]), waits=[("act", tok_m2)], inc="dve")
                    P.op("ACT", lambda e: e.activation(
                        stv[:, :], stv[:, :], AF.Ln, bias=consts_sb[:, 1:2]),
                        waits=[("dve", tok_var)])
                    tok_rstd = P.op("ACT", lambda e: e.activation(
                        stv[:, :], stv[:, :], AF.Exp, scale=-0.5), inc="act")
                    tok_out = P.op("DVE", lambda e: e.tensor_scalar(
                        outb[:, :], rtm[:, :], st1[:, :], stv[:, :],
                        ALU.subtract, ALU.mult),
                        waits=[("act", tok_rstd)] + out_free, inc="dve")
                    out_free = []
                    rtm_free = [("dve", tok_out)]
                    t_st = P.dma("SP", t_out[128 * (4 * sw + q4):128 * (4 * sw + q4 + 1), :],
                                 outb[:, :], waits=[("dve", tok_out)], cls="st")
                    out_stores.append(t_st)
                    out_free = [t_st]
                r2_free = [("pe", P.cnt["pe"])]
            P.op("SP", lambda e: e.nop(), waits=P.dma_all("SP") + P.dma_all("PL"))
            P.emit()

    return nc, dbg_aps


def _final_wait(nc, P, t_out):
    P.phase()
    with nc.sbuf_tensor("zz", [128, 2048], F32) as zz:
        P.op("PL", lambda e: e.memset(zz[:, :], 0.0), inc="plc")
        for i in range(16):
            P.dma("SP", t_out[128 * i:128 * (i + 1), :], zz[:, :],
                  waits=[("plc", P.cnt["plc"])])
        P.op("SP", lambda e: e.nop(), waits=P.dma_all("SP") + P.dma_all("PL"))
        P.emit()


_NC_CACHE = {}


def _get_nc(upto=5, dbg=()):
    key = (upto, tuple(sorted(dbg)))
    if key not in _NC_CACHE:
        _NC_CACHE[key] = build_nc(upto, dbg)
    return _NC_CACHE[key]


def kernel(**inputs):
    in_maps, tiles_per_core = host_prep(inputs)
    nc, _ = _get_nc()
    res = run_bass_kernel_spmd(nc, in_maps, core_ids=list(range(8)))
    out = np.zeros((B, S, D), np.float32)
    for core in range(8):
        b, tiles = tiles_per_core[core]
        r = res.results[core]["out"]
        for s, t in enumerate(tiles):
            out[b, 128 * t:128 * t + 128, :] = r[128 * s:128 * s + 128, :]
    return out


# version 19 (final)
# speedup vs baseline: 1.1941x; 1.0040x over previous
import numpy as np
import ml_dtypes
from contextlib import ExitStack

import concourse.bass as bass
import concourse.mybir as mybir
from concourse.bass_utils import run_bass_kernel_spmd

F32 = mybir.dt.float32
BF16 = mybir.dt.bfloat16
I8 = mybir.dt.int8
AF = mybir.ActivationFunctionType
ALU = mybir.AluOpType
AX = mybir.AxisListType

B, S, D, H = 2, 8192, 2048, 16
DN, DR, DK, KVL, DV = 128, 64, 192, 512, 128
INTER = 8192
NSLOT = 16
OWN = NSLOT * 128
RMS_EPS = 1.1920928955078125e-07
LN_EPS = 1e-5
SSCALE = float(1.0 / np.sqrt(DK))
BOUND_MARGIN = 1.04

ENGS = ("PE", "ACT", "DVE", "SP", "PL")


class Prog:

    NPOOL = 4

    def __init__(self, nc, ctx):
        self.nc = nc
        self.sem = {}
        self.cnt = {}
        for name in ("pe", "act", "dve", "plc"):
            self.sem[name] = ctx.enter_context(nc.semaphore(name))
            self.cnt[name] = 0
        self.pool = {}
        self.pool_idx = {}
        self.open = {}
        self.all_pool_sems = set()
        for q in ("SP", "PL", "SPst", "PLst", "ACTst"):
            names = []
            for i in range(self.NPOOL):
                nm = f"{q.lower()}{i}"
                self.sem[nm] = ctx.enter_context(nc.semaphore(nm))
                self.cnt[nm] = 0
                names.append(nm)
                self.all_pool_sems.add(nm)
            self.pool[q] = names
            self.pool_idx[q] = -1
            self.open[q] = None
        self.qs = None

    def phase(self):
        self.qs = {e: [] for e in ENGS}

    _AUTO = {"DVE": "dve", "ACT": "act", "PL": "plc"}

    def op(self, eng, fn, waits=(), inc=None, n=1, noclose=()):
        auto = self._AUTO.get(eng)
        if auto is not None and (inc is None or inc == auto):
            waits = list(waits) + [(auto, self.cnt[auto])]
            inc = auto
        ws = []
        for (s, v) in waits:
            if s in self.all_pool_sems:
                if s not in noclose:
                    v = self.cnt[s]
                    for q in self.pool:
                        if self.open[q] == s:
                            self.open[q] = None
            if v > 0:
                ws.append((s, v))
        dedup = {}
        for (s, v) in ws:
            dedup[s] = max(v, dedup.get(s, 0))
        ws = tuple(dedup.items())
        self.qs[eng].append((fn, ws, inc, n))
        if inc is not None:
            self.cnt[inc] += n
            return self.cnt[inc]
        return None

    def dma(self, queue, out, in_, waits=(), cls="ld"):
        eng = queue
        if cls == "st":
            queue = queue + "st"
        elif queue == "ACT":
            queue = "ACTst"
        extra = []
        if self.open[queue] is None:
            self.pool_idx[queue] += 1
            sem = self.pool[queue][self.pool_idx[queue] % self.NPOOL]
            self.open[queue] = sem
            if self.cnt[sem] > 0:
                extra = [(sem, self.cnt[sem])]
        sem = self.open[queue]
        self.op(eng, lambda e, o=out, i=in_: e.dma_start(out=o, in_=i),
                waits=list(waits) + extra, inc=sem, n=16, noclose=(sem,))
        return (sem, self.cnt[sem])

    def dma_all(self, queue):
        toks = []
        qs = (queue, queue + "st") if queue != "ACT" else ("ACTst",)
        for q in qs:
            self.open[q] = None
            toks += [(s, self.cnt[s]) for s in self.pool[q] if self.cnt[s] > 0]
        return toks

    def emit(self):
        blk_ctx = self.nc.Block()
        blk = blk_ctx.__enter__()
        try:
            for engname, attr in (("PE", "tensor"), ("ACT", "scalar"),
                                  ("DVE", "vector"), ("SP", "sync"),
                                  ("PL", "gpsimd")):
                ops = self.qs[engname]
                if not ops:
                    continue

                def run(e, ops=ops):
                    for fn, waits, inc, n in ops:
                        for (s, v) in waits:
                            e.wait_ge(self.sem[s], v)
                        i = fn(e)
                        if inc is not None:
                            i.then_inc(self.sem[inc], n)
                getattr(blk, attr)(run)
        finally:
            blk_ctx.__exit__(None, None, None)



def _rope_tables():
    freqs = 1.0 / 10000.0 ** (np.arange(0, DR, 2, dtype=np.float64) / DR)
    t = np.arange(S, dtype=np.float64)[:, None] * freqs[None, :]
    return (np.cos(t).T.astype(np.float32), np.sin(t).T.astype(np.float32))


def _wq_col_order():
    nope = np.concatenate([192 * h + np.arange(128) for h in range(H)])
    rope = []
    for r in range(8):
        h0, h1 = 2 * r, 2 * r + 1
        rope += [192 * h0 + 128 + 2 * np.arange(32),
                 192 * h1 + 128 + 2 * np.arange(32),
                 192 * h0 + 128 + 2 * np.arange(32) + 1,
                 192 * h1 + 128 + 2 * np.arange(32) + 1]
    return np.concatenate([nope] + rope)


def host_prep(inputs):
    x = np.ascontiguousarray(np.asarray(inputs["L0_attn_x"]))
    Wq = np.asarray(inputs["L0_attn_Wq"])
    Wkva = np.asarray(inputs["L0_attn_Wkv_a"])
    kvnw = np.asarray(inputs["L0_attn_kv_norm_w"])
    Wkvb = np.asarray(inputs["L0_attn_Wkv_b"])
    Wo = np.asarray(inputs["L0_attn_Wo"])
    wg = np.asarray(inputs["L0_ffn_w_gate"])
    wu = np.asarray(inputs["L0_ffn_w_up"])
    wd = np.asarray(inputs["L0_ffn_w_down"])
    sg = np.asarray(inputs["L0_ffn_s_gate"])
    su = np.asarray(inputs["L0_ffn_s_up"])
    sd = np.asarray(inputs["L0_ffn_s_down"])

    cosf, sinf = _rope_tables()
    cos2 = np.ascontiguousarray(np.vstack([cosf, cosf]))
    sin2 = np.ascontiguousarray(np.vstack([sinf, sinf]))

    wq_p = np.ascontiguousarray(
        Wq[:, _wq_col_order()].reshape(16, 128, 24, 128).transpose(2, 1, 0, 3)
        .reshape(24, 128, 2048))
    rc = 512 + np.concatenate([2 * np.arange(32), 2 * np.arange(32) + 1])
    wkva_p = np.ascontiguousarray(
        np.concatenate([Wkva[:, :512], Wkva[:, rc]], axis=1)
        .reshape(16, 128, 576))
    kvnw_p = np.ascontiguousarray(kvnw.reshape(4, 128).T)
    wkvb_p = np.ascontiguousarray(Wkvb.reshape(4, 128, 4096))
    wo_p = np.ascontiguousarray(
        Wo.reshape(16, 128, 16, 128).transpose(2, 1, 0, 3).reshape(16, 128, 2048))
    wg_p = np.ascontiguousarray(
        wg.reshape(16, 128, 64, 128).transpose(2, 1, 0, 3))
    wu_p = np.ascontiguousarray(
        wu.reshape(16, 128, 64, 128).transpose(2, 1, 0, 3))
    wd_p = np.ascontiguousarray(
        wd.reshape(64, 128, 16, 128).transpose(2, 1, 0, 3))
    sg_p = np.ascontiguousarray(sg.reshape(64, 128).T)
    su_p = np.ascontiguousarray(su.reshape(64, 128).T)
    sd_p = np.ascontiguousarray(sd.reshape(16, 128).T)
    selv = np.zeros((128, 2), np.float32)
    selv[0:32, 0] = 1.0
    selv[64:96, 0] = 1.0
    selv[32:64, 1] = 1.0
    selv[96:128, 1] = 1.0
    sel = np.ascontiguousarray(np.repeat(selv, 128, axis=1))
    kposc = (np.arange(128, dtype=np.float32)[:, None]
             + 128.0 * np.arange(64, dtype=np.float32)[None, :])
    kposc = np.ascontiguousarray(kposc)
    consts = np.zeros((128, 4), np.float32)
    consts[:, 0] = RMS_EPS
    consts[:, 1] = LN_EPS
    consts = np.ascontiguousarray(consts)

    xf = [np.ascontiguousarray(x[b].T) for b in range(B)]

    in_maps = []
    tiles_per_core = []
    for core in range(8):
        b, j = divmod(core, 4)
        tiles = [4 * s + j for s in range(NSLOT)]
        tiles_per_core.append((b, tiles))
        qcols = np.concatenate([np.arange(128 * t, 128 * t + 128)
                                for t in tiles])
        xq = np.ascontiguousarray(xf[b][:, qcols])
        qpos_rep = np.ascontiguousarray(
            np.broadcast_to(qcols.astype(np.float32)[None, :], (128, OWN)))
        cos_own = np.ascontiguousarray(cos2[:, qcols])
        sin_own = np.ascontiguousarray(sin2[:, qcols])
        in_maps.append(dict(
            xf=xf[b], xq=xq, qpos=qpos_rep,
            cos_own=cos_own, sin_own=sin_own, cosf=cosf, sinf=sinf,
            kposc=kposc, sel=sel, consts=consts,
            wq=wq_p, wkva=wkva_p, kvnw=kvnw_p, wkvb=wkvb_p, wo=wo_p,
            wg=wg_p, wu=wu_p, wd=wd_p, sg=sg_p, su=su_p, sd=sd_p,
        ))
    return in_maps, tiles_per_core



def build_nc(upto=5, dbg=()):
    nc = bass.Bass("TRN2", target_bir_lowering=False)
    dbg = set(dbg)

    t_xf = nc.dram_tensor("xf", [D, S], F32, kind="ExternalInput").ap()
    t_xq = nc.dram_tensor("xq", [D, OWN], F32, kind="ExternalInput").ap()
    t_qpos = nc.dram_tensor("qpos", [128, OWN], F32, kind="ExternalInput").ap()
    t_cos_own = nc.dram_tensor("cos_own", [64, OWN], F32, kind="ExternalInput").ap()
    t_sin_own = nc.dram_tensor("sin_own", [64, OWN], F32, kind="ExternalInput").ap()
    t_cosf = nc.dram_tensor("cosf", [32, S], F32, kind="ExternalInput").ap()
    t_sinf = nc.dram_tensor("sinf", [32, S], F32, kind="ExternalInput").ap()
    t_kposc = nc.dram_tensor("kposc", [128, 64], F32, kind="ExternalInput").ap()
    t_sel = nc.dram_tensor("sel", [128, 256], F32, kind="ExternalInput").ap()
    t_consts = nc.dram_tensor("consts", [128, 4], F32, kind="ExternalInput").ap()
    t_wq = nc.dram_tensor("wq", [24, 128, 2048], F32, kind="ExternalInput").ap()
    t_wkva = nc.dram_tensor("wkva", [16, 128, 576], F32, kind="ExternalInput").ap()
    t_kvnw = nc.dram_tensor("kvnw", [128, 4], F32, kind="ExternalInput").ap()
    t_wkvb = nc.dram_tensor("wkvb", [4, 128, 4096], F32, kind="ExternalInput").ap()
    t_wo = nc.dram_tensor("wo", [16, 128, 2048], F32, kind="ExternalInput").ap()
    t_wg = nc.dram_tensor("wg", [64, 128, 16, 128], I8, kind="ExternalInput").ap()
    t_wu = nc.dram_tensor("wu", [64, 128, 16, 128], I8, kind="ExternalInput").ap()
    t_wd = nc.dram_tensor("wd", [16, 128, 64, 128], I8, kind="ExternalInput").ap()
    t_sg = nc.dram_tensor("sg", [128, 64], F32, kind="ExternalInput").ap()
    t_su = nc.dram_tensor("su", [128, 64], F32, kind="ExternalInput").ap()
    t_sd = nc.dram_tensor("sd", [128, 16], F32, kind="ExternalInput").ap()

    t_out = nc.dram_tensor("out", [OWN, D], F32, kind="ExternalOutput").ap()

    q_scr = nc.dram_tensor("q_scr", [3072, OWN], BF16, kind="Internal").ap()
    o_scr = nc.dram_tensor("o_scr", [4, 128, 16, 512], BF16, kind="Internal").ap()
    h1_scr = nc.dram_tensor("h1_scr", [D, OWN], F32, kind="Internal").ap()
    wo_bf = nc.dram_tensor("wo_bf", [16, 128, 2048], BF16, kind="Internal").ap()

    dbg_aps = {}
    def dbg_out(name, shape, dtype):
        if name in dbg:
            dbg_aps[name] = nc.dram_tensor(
                "dbg_" + name, shape, dtype, kind="ExternalOutput").ap()
            return dbg_aps[name]
        return None

    d_qscr = dbg_out("qscr", [3072, OWN], BF16)
    d_ckv = dbg_out("ckv", [128, 4 * S], BF16)
    d_krope = dbg_out("krope", [64, S], BF16)
    d_qmax = dbg_out("qmax", [1, 16], F32)
    d_krn = dbg_out("krn", [1, S], BF16)
    d_oscr = dbg_out("oscr", [4, 128, 16, 512], BF16)
    d_h1 = dbg_out("h1", [D, OWN], F32)
    d_kn15 = dbg_out("kn15", [128, S], BF16)
    d_vt15 = dbg_out("vt15", [128, S], BF16)

    with ExitStack() as g:
        P = Prog(nc, g)
        qmaxN = g.enter_context(nc.sbuf_tensor("qmaxN", [1, 16], F32))
        qmaxR = g.enter_context(nc.sbuf_tensor("qmaxR", [1, 16], F32))
        qsum_t = g.enter_context(nc.sbuf_tensor("qsum_t", [1, 1], F32))
        ones128 = g.enter_context(nc.sbuf_tensor("ones128", [128, 1], BF16))
        ones1 = g.enter_context(nc.sbuf_tensor("ones1", [1, 128], BF16))
        ones1f = g.enter_context(nc.sbuf_tensor("ones1f", [1, 128], F32))
        ones_mat = g.enter_context(nc.sbuf_tensor("ones_mat", [128, 128], BF16))
        consts_sb = g.enter_context(nc.sbuf_tensor("consts_sb", [128, 4], F32))

        P.phase()
        P.op("PL", lambda e: e.memset(ones128[:, :], 1.0), inc="plc")
        P.op("PL", lambda e: e.memset(ones1[:, :], 1.0), inc="plc")
        P.op("PL", lambda e: e.memset(ones1f[:, :], 1.0), inc="plc")
        P.op("PL", lambda e: e.memset(ones_mat[:, :], 1.0), inc="plc")
        P.op("PL", lambda e: e.memset(qmaxN[:, :], -1e30), inc="plc")
        P.op("PL", lambda e: e.memset(qmaxR[:, :], -1e30), inc="plc")
        t_c = P.dma("SP", consts_sb[:, :], t_consts)
        init_done = [("plc", P.cnt["plc"]), t_c]

        with ExitStack() as a:
            wqb = [a.enter_context(nc.sbuf_tensor(f"wqb{i}", [128, 2048], BF16)) for i in range(2)]
            xb = [a.enter_context(nc.sbuf_tensor(f"xbA{i}", [128, 16 * 512], BF16)) for i in range(4)]
            cosA = a.enter_context(nc.sbuf_tensor("cosA", [64, OWN], F32))
            sinA = a.enter_context(nc.sbuf_tensor("sinA", [64, OWN], F32))
            sel_sb = a.enter_context(nc.sbuf_tensor("sel_sb", [128, 256], BF16))
            qev = [a.enter_context(nc.sbuf_tensor(f"qev{i}", [128, 512], BF16)) for i in range(2)]
            qevr = [a.enter_context(nc.sbuf_tensor(f"qevr{i}", [128, 512], BF16)) for i in range(2)]
            sqb = [a.enter_context(nc.sbuf_tensor(f"sqbA{i}", [128, 512], BF16)) for i in range(4)]
            rm1 = a.enter_context(nc.sbuf_tensor("rm1", [64, 512], F32))
            rm2 = a.enter_context(nc.sbuf_tensor("rm2", [64, 512], F32))
            qm_t = a.enter_context(nc.sbuf_tensor("qm_t", [2, 1], F32))
            pq = [a.enter_context(nc.psum_tensor(f"pqA{i}", [128, 512], F32)) for i in range(4)]
            pn = [a.enter_context(nc.psum_tensor(f"pnA{i}", [128, 512], F32)) for i in range(2)]
            pnR = [a.enter_context(nc.psum_tensor(f"pnR{i}", [128, 512], F32)) for i in range(2)]

            def load_xb(tt):
                src = t_xq[:, 512 * tt:512 * (tt + 1)].rearrange(
                    "(c p) q -> p c q", p=128)
                dst = xb[tt][:, :].rearrange("p (c q) -> p c q", c=16)
                return [P.dma("PL", dst, src)]
            t_xb = {0: load_xb(0)}
            P.dma("SP", cosA[:, :], t_cos_own)
            t_trig = P.dma("SP", sinA[:, :], t_sin_own)
            t_sel = P.dma("PL", sel_sb[:, :], t_sel)
            t_wocast = [P.dma("PL", wo_bf[c], t_wo[c], cls="st") for c in range(16)]

            bank_free = {i: [] for i in range(4)}
            wqb_free = {0: [], 1: []}
            qev_free = {0: [], 1: []}
            qevr_free = {0: [], 1: []}
            pn_free = {0: [], 1: []}
            sq_free = {i: [] for i in range(4)}
            t_stores = []
            pending_qn = []

            def flush_qn():
                nonlocal_state = None
                for (kind, ocb2, tt2, tok_sq2) in pending_qn:
                    par = tt2 % 2
                    spar = tt2
                    if kind == "nope":
                        h = ocb2
                        tok_n = P.op("PE", lambda e, par=par, spar=spar: e.matmul(
                            pn[par][:, :], ones_mat[:, :], sqb[spar][:, :],
                            start=True, stop=True),
                            waits=[("act", tok_sq2), ("plc", P.cnt["plc"])] + pn_free[par], inc="pe")
                        pn_free[par] = []
                        P.op("DVE", lambda e, par=par: e.reduce_max(
                            qm_t[0:1, :], pn[par][0:1, :], AX.X), waits=[("pe", tok_n)])
                        tok_mx = P.op("DVE", lambda e, h=h: e.tensor_max(
                            qmaxN[:, h:h + 1], qmaxN[:, h:h + 1], qm_t[0:1, :]), inc="dve")
                        pn_free[par] = [("dve", tok_mx)]
                    else:
                        r = ocb2 - 16
                        P.op("PE", lambda e, par=par, spar=spar: e.matmul(
                            pn[par][:, :], sel_sb[:, 0:128], sqb[spar][:, :],
                            start=True, stop=True),
                            waits=[("act", tok_sq2)] + pn_free[par])
                        tok_n = P.op("PE", lambda e, par=par, spar=spar: e.matmul(
                            pnR[par][:, :], sel_sb[:, 128:256], sqb[spar][:, :],
                            start=True, stop=True), inc="pe")
                        pn_free[par] = []
                        P.op("DVE", lambda e, par=par: e.reduce_max(
                            qm_t[0:1, :], pn[par][0:1, :], AX.X), waits=[("pe", tok_n)])
                        P.op("DVE", lambda e, r=r: e.tensor_max(
                            qmaxR[:, 2 * r:2 * r + 1], qmaxR[:, 2 * r:2 * r + 1], qm_t[0:1, :]))
                        P.op("DVE", lambda e, par=par: e.reduce_max(
                            qm_t[0:1, :], pnR[par][0:1, :], AX.X))
                        tok_mx = P.op("DVE", lambda e, r=r: e.tensor_max(
                            qmaxR[:, 2 * r + 1:2 * r + 2], qmaxR[:, 2 * r + 1:2 * r + 2], qm_t[0:1, :]), inc="dve")
                        pn_free[par] = [("dve", tok_mx)]
                    sq_free[spar] = [("pe", tok_n)]
                pending_qn.clear()

            for ocb in range(24):
                wpar = ocb % 2
                t_w = P.dma("PL", wqb[wpar][:, :], t_wq[ocb], waits=wqb_free[wpar])
                wqb_free[wpar] = []
                flush_qn()
                for tt in range(4):
                    bk = (4 * ocb + tt) % 4
                    par = tt % 2
                    for c in range(16):
                        waits = []
                        if c == 0:
                            waits = [t_w, t_xb[tt][-1]] + bank_free[bk] + ([t_sel] if ocb == 0 else [])
                            bank_free[bk] = []
                        P.op("PE", lambda e, c=c, bk=bk, tt=tt, wpar=wpar: e.matmul(
                            pq[bk][:, :], wqb[wpar][:, 128 * c:128 * (c + 1)],
                            xb[tt][:, 512 * c:512 * (c + 1)],
                            start=(c == 0), stop=(c == 15)),
                            waits=waits, inc="pe" if c == 15 else None)
                    tok_mm = P.cnt["pe"]
                    if ocb == 0 and tt + 1 < 4:
                        t_xb[tt + 1] = load_xb(tt + 1)
                    if ocb < 16:
                        h = ocb
                        tok_cp = P.op("ACT", lambda e, par=par, bk=bk: e.copy(
                            qev[par][:, :], pq[bk][:, :]),
                            waits=[("pe", tok_mm)] + qev_free[par], inc="act")
                        qev_free[par] = []
                        tok_sq = P.op("ACT", lambda e, tt=tt, bk=bk: e.activation(
                            sqb[tt][:, :], pq[bk][:, :], AF.Square),
                            waits=sq_free[tt], inc="act")
                        sq_free[tt] = []
                        bank_free[bk] = [("act", tok_sq)]
                        t_st = P.dma("SP", q_scr[128 * h:128 * (h + 1),
                                               512 * tt:512 * (tt + 1)],
                                     qev[par][:, :], waits=[("act", tok_cp)], cls="st")
                        qev_free[par] = [t_st]
                        t_stores.append(t_st)
                        pending_qn.append(("nope", ocb, tt, tok_sq))
                    else:
                        r = ocb - 16
                        tok_sq = P.op("ACT", lambda e, tt=tt, bk=bk: e.activation(
                            sqb[tt][:, :], pq[bk][:, :], AF.Square),
                            waits=[("pe", tok_mm)] + sq_free[tt], inc="act")
                        sq_free[tt] = []
                        pending_qn.append(("rope", ocb, tt, tok_sq))
                        cs = cosA[:, 512 * tt:512 * (tt + 1)]
                        sn = sinA[:, 512 * tt:512 * (tt + 1)]
                        e_ap = pq[bk][0:64, :]
                        o_ap = pq[bk][64:128, :]
                        wts = [("act", tok_sq), t_trig] + qevr_free[par]
                        qevr_free[par] = []
                        P.op("DVE", lambda e, a1=e_ap, c1=cs: e.tensor_mul(rm1[:, :], a1, c1), waits=wts)
                        P.op("DVE", lambda e, a1=o_ap, s1=sn: e.tensor_mul(rm2[:, :], a1, s1))
                        P.op("DVE", lambda e, par=par: e.tensor_sub(qevr[par][0:64, :], rm1[:, :], rm2[:, :]))
                        P.op("DVE", lambda e, a1=o_ap, c1=cs: e.tensor_mul(rm1[:, :], a1, c1))
                        P.op("DVE", lambda e, a1=e_ap, s1=sn: e.tensor_mul(rm2[:, :], a1, s1))
                        tok_rope = P.op("DVE", lambda e, par=par: e.tensor_add(
                            qevr[par][64:128, :], rm1[:, :], rm2[:, :]), inc="dve")
                        bank_free[bk] = [("dve", tok_rope)]
                        h0, h1 = 2 * r, 2 * r + 1
                        for (rows_s, rows_d) in (((0, 32), 2048 + 64 * h0),
                                                 ((32, 64), 2048 + 64 * h1),
                                                 ((64, 96), 2048 + 64 * h0 + 32),
                                                 ((96, 128), 2048 + 64 * h1 + 32)):
                            t_st = P.dma("SP",
                                         q_scr[rows_d:rows_d + 32, 512 * tt:512 * (tt + 1)],
                                         qevr[par][rows_s[0]:rows_s[1], :],
                                         waits=[("dve", tok_rope)], cls="st")
                            t_stores.append(t_st)
                        qevr_free[par] = [t_stores[-1]]
                wqb_free[wpar] = [("pe", P.cnt["pe"])]
            flush_qn()
            P.op("SP", lambda e: e.nop(), waits=P.dma_all("SP") + P.dma_all("PL") + [("dve", P.cnt["dve"])])
            if d_qscr is not None:
                P.dma("SP", d_qscr, q_scr)
                P.op("SP", lambda e: e.nop(), waits=P.dma_all("SP") + P.dma_all("PL"))
            P.emit()


        if upto < 2:
            _final_wait(nc, P, t_out)
            return nc, dbg_aps

        with ExitStack() as bc:
            ckv_sb = bc.enter_context(nc.sbuf_tensor("ckv_sb", [128, 4 * S], BF16))
            krope_sb = bc.enter_context(nc.sbuf_tensor("krope_sb", [128, S], BF16))
            krn_sb = bc.enter_context(nc.sbuf_tensor("krn_sb", [1, S], BF16))

            P.phase()
            with ExitStack() as b:
                wkva_sb = b.enter_context(nc.sbuf_tensor("wkva_sb", [128, 16 * 576], BF16))
                kvnw_sb = b.enter_context(nc.sbuf_tensor("kvnw_sb", [128, 4], F32))
                xb2 = [b.enter_context(nc.sbuf_tensor(f"xbB{i}", [128, 16 * 512], BF16)) for i in range(2)]
                cosB = [b.enter_context(nc.sbuf_tensor(f"cosB{i}", [32, 512], F32)) for i in range(2)]
                sinB = [b.enter_context(nc.sbuf_tensor(f"sinB{i}", [32, 512], F32)) for i in range(2)]
                ckvraw = [b.enter_context(nc.sbuf_tensor(f"ckvraw{i}", [128, 4 * 512], F32)) for i in range(2)]
                sqB = [b.enter_context(nc.sbuf_tensor(f"sqB{i}", [128, 4 * 512], BF16)) for i in range(2)]
                sqRB = [b.enter_context(nc.sbuf_tensor(f"sqRB{i}", [64, 512], BF16)) for i in range(2)]
                rstdB = b.enter_context(nc.sbuf_tensor("rstdB", [128, 512], F32))
                rmB1 = b.enter_context(nc.sbuf_tensor("rmB1", [32, 512], F32))
                rmB2 = b.enter_context(nc.sbuf_tensor("rmB2", [32, 512], F32))
                pkv = [b.enter_context(nc.psum_tensor(f"pkvB{i}", [128, 512], F32)) for i in range(3)]
                prope = b.enter_context(nc.psum_tensor("propeB", [64, 512], F32))
                pss = b.enter_context(nc.psum_tensor("pssB", [128, 512], F32))
                pkr = b.enter_context(nc.psum_tensor("pkrB", [128, 512], F32))


                t_wk_ld = [P.dma("PL", wkva_sb[:, 576 * c:576 * (c + 1)], t_wkva[c])
                           for c in range(16)]
                t_kvnw = P.dma("SP", kvnw_sb[:, :], t_kvnw)

                bank_free = {0: [], 1: [], 2: []}
                rope_free = []
                ss_free = []
                rstd_free = []
                kr_free = []
                raw_free = {0: [], 1: []}
                sq_freeB = {0: [], 1: []}
                xb_free = {0: [], 1: []}
                cos_free = {0: [], 1: []}
                pending_B = None

                def flush_B(pend):
                    nonlocal ss_free, rstd_free, kr_free, sq_freeB, raw_free
                    tt2 = pend["tt"]
                    rp = tt2 % 2
                    for oc in range(4):
                        P.op("PE", lambda e, oc=oc, rp=rp: e.matmul(
                            pss[:, :], ones_mat[:, :], sqB[rp][:, 512 * oc:512 * (oc + 1)],
                            start=(oc == 0), stop=(oc == 3)),
                            waits=([("act", pend["sq3"])] + ss_free) if oc == 0 else [],
                            inc="pe" if oc == 3 else None)
                    tok_ss = P.cnt["pe"]
                    ss_free = []
                    tok_kr = P.op("PE", lambda e, rp=rp: e.matmul(
                        pkr[:, :], ones_mat[0:64, :], sqRB[rp][:, :],
                        start=True, stop=True), waits=[("act", pend["sqr"])] + kr_free, inc="pe")
                    tok_krc = P.op("DVE", lambda e, tt2=tt2: e.tensor_copy(
                        krn_sb[:, 512 * tt2:512 * (tt2 + 1)], pkr[0:1, :]),
                        waits=[("pe", tok_kr)], inc="dve")
                    kr_free = [("dve", tok_krc)]
                    tok_ln = P.op("ACT", lambda e: e.activation(
                        rstdB[:, :], pss[:, :], AF.Ln,
                        bias=consts_sb[:, 0:1], scale=1.0 / 512.0),
                        waits=[("pe", tok_ss), t_c] + rstd_free)
                    rstd_free = []
                    tok_ex = P.op("ACT", lambda e: e.activation(
                        rstdB[:, :], rstdB[:, :], AF.Exp, scale=-0.5), inc="act")
                    ss_free = [("act", tok_ex)]
                    for oc in range(4):
                        P.op("DVE", lambda e, oc=oc, tt2=tt2, rp=rp: e.scalar_tensor_tensor(
                            ckv_sb[:, S * oc + 512 * tt2: S * oc + 512 * (tt2 + 1)],
                            ckvraw[rp][:, 512 * oc:512 * (oc + 1)],
                            kvnw_sb[:, oc:oc + 1], rstdB[:, :],
                            ALU.mult, ALU.mult),
                            waits=[("act", tok_ex), t_kvnw] if oc == 0 else [],
                            inc="dve")
                    ss_free = ss_free + [("dve", P.cnt["dve"])]
                    rstd_free = [("dve", P.cnt["dve"])]
                    sq_freeB[rp] = [("pe", tok_ss)]
                    raw_free[rp] = [("dve", P.cnt["dve"])]

                for tt in range(16):
                    par_x = tt % 2
                    t_x = [P.dma("PL", xb2[par_x][:, :].rearrange("p (c q) -> p c q", c=16),
                                 t_xf[:, 512 * tt:512 * (tt + 1)].rearrange(
                                     "(c p) q -> p c q", p=128),
                                 waits=xb_free[par_x])]
                    xb_free[par_x] = []
                    t_cs = P.dma("SP", cosB[par_x][:, :],
                                 t_cosf[:, 512 * tt:512 * (tt + 1)], waits=cos_free[par_x])
                    t_sn = P.dma("SP", sinB[par_x][:, :],
                                 t_sinf[:, 512 * tt:512 * (tt + 1)])
                    cos_free[par_x] = []
                    mm_tok = {}
                    for oc in range(5):
                        par = oc % 3
                        if oc < 4:
                            dst, lo, w = (pkv[par], 128 * oc, 128)
                        else:
                            dst, lo, w = (prope, 512, 64)
                        def mk(c, dst=dst, lo=lo, w=w, par_x=par_x):
                            def f(e):
                                return e.matmul(
                                    dst[:, :],
                                    wkva_sb[:, 576 * c + lo:576 * c + lo + w],
                                    xb2[par_x][:, 512 * c:512 * (c + 1)],
                                    start=(c == 0), stop=(c == 15))
                            return f
                        for c in range(16):
                            waits = []
                            if c == 0:
                                waits = [t_x[-1], t_wk_ld[15]]
                                waits += (bank_free[par] if oc < 4 else rope_free)
                                if oc < 4:
                                    bank_free[par] = []
                                else:
                                    rope_free = []
                            P.op("PE", mk(c), waits=waits, inc="pe" if c == 15 else None)
                        mm_tok[oc] = P.cnt["pe"]
                        if oc < 4:
                            tok_cp = P.op("ACT", lambda e, par=par, oc=oc, rp=tt % 2: e.copy(
                                ckvraw[rp][:, 512 * oc:512 * (oc + 1)], pkv[par][:, :]),
                                waits=[("pe", mm_tok[oc])] + (raw_free[tt % 2] if oc == 0 else []), inc="act")
                            if oc == 0:
                                raw_free[tt % 2] = []
                            tok_sq = P.op("ACT", lambda e, par=par, oc=oc, rp=tt % 2: e.activation(
                                sqB[rp][:, 512 * oc:512 * (oc + 1)], pkv[par][:, :], AF.Square),
                                waits=(sq_freeB[tt % 2] if oc == 0 else []), inc="act")
                            if oc == 0:
                                sq_freeB[tt % 2] = []
                            bank_free[par] = [("act", tok_sq)]
                            mm_tok["sq", oc] = tok_sq
                        else:
                            tok_sqr = P.op("ACT", lambda e, rp=tt % 2: e.activation(
                                sqRB[rp][:, :], prope[:, :], AF.Square),
                                waits=[("pe", mm_tok[oc])] + kr_free, inc="act")
                            kr_free = []
                            mm_tok["sqr"] = tok_sqr
                            wts = [("pe", mm_tok[oc]), ("act", tok_sqr), t_sn, t_cs]
                            P.op("DVE", lambda e, p=par_x: e.tensor_mul(rmB1[:, :], prope[0:32, :], cosB[p][:, :]), waits=wts)
                            P.op("DVE", lambda e, p=par_x: e.tensor_mul(rmB2[:, :], prope[32:64, :], sinB[p][:, :]))
                            P.op("DVE", lambda e, tt=tt: e.tensor_sub(
                                krope_sb[0:32, 512 * tt:512 * (tt + 1)], rmB1[:, :], rmB2[:, :]))
                            P.op("DVE", lambda e, p=par_x: e.tensor_mul(rmB1[:, :], prope[32:64, :], cosB[p][:, :]))
                            P.op("DVE", lambda e, p=par_x: e.tensor_mul(rmB2[:, :], prope[0:32, :], sinB[p][:, :]))
                            tok_rope = P.op("DVE", lambda e, tt=tt: e.tensor_add(
                                krope_sb[32:64, 512 * tt:512 * (tt + 1)], rmB1[:, :], rmB2[:, :]), inc="dve")
                            cos_free[par_x] = [("dve", tok_rope)]
                            rope_free = [("dve", tok_rope), ("act", tok_sqr)]
                    if pending_B is not None:
                        flush_B(pending_B)
                    pending_B = dict(tt=tt, sq3=mm_tok["sq", 3], sqr=mm_tok["sqr"])
                    xb_free[par_x] = [("pe", mm_tok[4])]
                flush_B(pending_B)
                P.dma("SP", krope_sb[64:128, :], krope_sb[0:64, :],
                      waits=[("dve", P.cnt["dve"])], cls="st")
                P.op("SP", lambda e: e.nop(),
                     waits=P.dma_all("SP") + P.dma_all("PL") + [("dve", P.cnt["dve"])])
                if d_ckv is not None:
                    tk = P.op("DVE", lambda e: e.nop(), waits=[("dve", P.cnt["dve"])], inc="dve")
                    P.dma("SP", d_ckv, ckv_sb[:, :], waits=[("dve", tk)])
                    P.dma("SP", d_krope, krope_sb[:, :])
                    P.dma("SP", d_krn, krn_sb[:, :])
                    P.dma("SP", d_qmax, qmaxN[:, :])
                    P.op("SP", lambda e: e.nop(), waits=P.dma_all("SP") + P.dma_all("PL"))
                P.emit()

            if upto < 3:
                _final_wait(nc, P, t_out)
                return nc, dbg_aps

            P.phase()
            with ExitStack() as cph:
                wkvb_sb = cph.enter_context(nc.sbuf_tensor("wkvb_sb", [128, 4 * 4096], BF16))
                qpos_sb = cph.enter_context(nc.sbuf_tensor("qpos_sb", [128, OWN], F32))
                kposc_sb = cph.enter_context(nc.sbuf_tensor("kposc_sb", [128, 64], F32))
                knope_sb = cph.enter_context(nc.sbuf_tensor("knope_sb", [128, S], BF16))
                vtm_sb = cph.enter_context(nc.sbuf_tensor("vtm_sb", [128, S], BF16))
                qn_sb = [cph.enter_context(nc.sbuf_tensor(f"qn_sb{i}", [128, 512], BF16)) for i in range(2)]
                qr_sb = [cph.enter_context(nc.sbuf_tensor(f"qr_sb{i}", [128, 512], BF16)) for i in range(2)]
                ptb = [cph.enter_context(nc.sbuf_tensor(f"ptb{i}", [128, 512], BF16)) for i in range(4)]
                sqCs = cph.enter_context(nc.sbuf_tensor("sqCs", [128, 16 * 512], BF16))
                km_t = cph.enter_context(nc.sbuf_tensor("km_t", [1, 1], F32))
                kmax = cph.enter_context(nc.sbuf_tensor("kmax", [1, 1], F32))
                bias_sb = cph.enter_context(nc.sbuf_tensor("bias_sb", [128, 1], F32))
                bias_t = cph.enter_context(nc.sbuf_tensor("bias_t", [1, 1], F32))
                den_row = cph.enter_context(nc.sbuf_tensor("den_row", [1, 512], F32))
                dfull = cph.enter_context(nc.sbuf_tensor("dfull", [128, 512], F32))
                rrep = cph.enter_context(nc.sbuf_tensor("rrep", [128, 512], F32))
                ob_sb = [cph.enter_context(nc.sbuf_tensor(f"ob_sb{i}", [128, 512], BF16)) for i in range(2)]
                pool = [cph.enter_context(nc.psum_tensor(f"poolC{i}", [128, 512], F32)) for i in range(4)]
                po = [cph.enter_context(nc.psum_tensor(f"poC{i}", [128, 512], F32)) for i in range(2)]
                pden = cph.enter_context(nc.psum_tensor("pdenC", [128, 512], F32))
                pmisc = cph.enter_context(nc.psum_tensor("pmiscC", [128, 512], F32))

                t_wb = [P.dma("PL", wkvb_sb[:, 4096 * c:4096 * (c + 1)], t_wkvb[c])
                        for c in range(4)]
                t_qp = P.dma("SP", qpos_sb[:, :], t_qpos)
                t_kp = P.dma("SP", kposc_sb[:, :], t_kposc)

                NG = 64
                def q_load(gi, waits):
                    hh, gg = divmod(gi, 4)
                    par = gi % 2
                    tn = P.dma("SP", qn_sb[par][:, :],
                               q_scr[128 * hh:128 * (hh + 1), 512 * gg:512 * (gg + 1)],
                               waits=waits)
                    P.dma("SP", qr_sb[par][0:64, :],
                          q_scr[2048 + 64 * hh:2048 + 64 * (hh + 1), 512 * gg:512 * (gg + 1)])
                    tr = P.dma("SP", qr_sb[par][64:128, :],
                               q_scr[2048 + 64 * hh:2048 + 64 * (hh + 1), 512 * gg:512 * (gg + 1)])
                    return tr

                pool_free = {i: [] for i in range(4)}
                pt_free = {i: [] for i in range(4)}
                po_free = {0: [], 1: []}
                pden_free = []
                pmisc_free = []
                pkn_free = {0: [], 1: []}
                ob_free = {0: [], 1: []}
                kn_free = []
                vt_free = []
                knC_free = []
                sqs_free = []
                q_loaded = {}
                q_loaded[0] = q_load(0, [])
                for h in range(16):
                    P.op("DVE", lambda e: e.memset(kmax[:, :], -1e30), waits=knC_free)
                    knC_free = []
                    kn_cp = None
                    sq_toks = {}
                    for tt in range(16):
                        bk = tt % 4
                        for cc in range(4):
                            waits = []
                            if cc == 0:
                                waits = list(pool_free[bk]) + ([t_wb[3]] if (h == 0 and tt == 0) else [])
                                waits += kn_free
                                kn_free = []
                                pool_free[bk] = []
                            P.op("PE", lambda e, cc=cc, tt=tt, bk=bk, h=h: e.matmul(
                                pool[bk][:, :],
                                wkvb_sb[:, 4096 * cc + 256 * h:4096 * cc + 256 * h + 128],
                                ckv_sb[:, S * cc + 512 * tt:S * cc + 512 * (tt + 1)],
                                start=(cc == 0), stop=(cc == 3)),
                                waits=waits, inc="pe" if cc == 3 else None)
                        tok_mm = P.cnt["pe"]
                        kn_cp = P.op("ACT", lambda e, bk=bk, tt=tt: e.copy(
                            knope_sb[:, 512 * tt:512 * (tt + 1)], pool[bk][:, :]),
                            waits=[("pe", tok_mm)], inc="act")
                        pool_free[bk] = [("act", kn_cp)]
                        sq_toks[tt] = P.op("DVE", lambda e, tt=tt: e.tensor_mul(
                            sqCs[:, 512 * tt:512 * (tt + 1)],
                            knope_sb[:, 512 * tt:512 * (tt + 1)],
                            knope_sb[:, 512 * tt:512 * (tt + 1)]),
                            waits=[("act", kn_cp)] + (sqs_free if tt == 0 else []), inc="dve")
                        if tt == 0:
                            sqs_free = []
                    for tt in range(16):
                        knb = tt % 2
                        knbank = pmisc if knb == 0 else pden
                        knw = [("dve", sq_toks[tt])] + pkn_free[knb]
                        if knb == 0:
                            knw += pmisc_free
                            pmisc_free = []
                        else:
                            knw += pden_free
                            pden_free = []
                        tok_kn = P.op("PE", lambda e, knbank=knbank, tt=tt: e.matmul(
                            knbank[:, :], ones_mat[:, :], sqCs[:, 512 * tt:512 * (tt + 1)],
                            start=True, stop=True),
                            waits=knw, inc="pe")
                        pkn_free[knb] = []
                        tok_red = P.op("DVE", lambda e, knbank=knbank, tt=tt: e.tensor_add(
                            den_row[:, :], knbank[0:1, :],
                            krn_sb[:, 512 * tt:512 * (tt + 1)]),
                            waits=[("pe", tok_kn)], inc="dve")
                        pkn_free[knb] = [("dve", tok_red)]
                        P.op("DVE", lambda e: e.reduce_max(km_t[:, :], den_row[:, :], AX.X))
                        P.op("DVE", lambda e: e.tensor_max(kmax[:, :], kmax[:, :], km_t[:, :]))
                    sqs_free = [("pe", P.cnt["pe"])]
                    tok_kmax = P.cnt["dve"]
                    knC_free = [("dve", tok_kmax)]
                    P.op("DVE", lambda e, h=h: e.tensor_add(
                        qsum_t[:, :], qmaxN[:, h:h + 1], qmaxR[:, h:h + 1]),
                        waits=[("dve", tok_kmax)])
                    tok_b0 = P.op("DVE", lambda e: e.tensor_mul(
                        km_t[:, :], qsum_t[:, :], kmax[:, :]), inc="dve")
                    P.op("ACT", lambda e: e.activation(bias_t[:, :], km_t[:, :], AF.Ln),
                         waits=[("dve", tok_b0)])
                    tok_b1 = P.op("ACT", lambda e: e.activation(
                        bias_t[:, :], bias_t[:, :], AF.Exp, scale=0.5), inc="act")
                    tok_b2 = P.op("DVE", lambda e: e.tensor_scalar_mul(
                        bias_t[:, :], bias_t[:, :], -SSCALE * BOUND_MARGIN),
                        waits=[("act", tok_b1)], inc="dve")
                    vt_cp = None
                    for tg in range(16):
                        bk = tg % 4
                        for kt4 in range(4):
                            kt = 4 * tg + kt4
                            for cc in range(4):
                                waits = []
                                if cc == 0 and kt4 == 0:
                                    waits = list(pool_free[bk]) + vt_free
                                    vt_free = []
                                    pool_free[bk] = []
                                P.op("PE", lambda e, cc=cc, kt=kt, kt4=kt4, bk=bk, h=h: e.matmul(
                                    pool[bk][:, 128 * kt4:128 * (kt4 + 1)],
                                    ckv_sb[:, S * cc + 128 * kt:S * cc + 128 * (kt + 1)],
                                    wkvb_sb[:, 4096 * cc + 256 * h + 128:4096 * cc + 256 * h + 256],
                                    start=(cc == 0), stop=(cc == 3)),
                                    waits=waits, inc="pe" if (cc == 3 and kt4 == 3) else None)
                        tok_mm = P.cnt["pe"]
                        vt_cp = P.op("ACT", lambda e, bk=bk, tg=tg: e.copy(
                            vtm_sb[:, 512 * tg:512 * (tg + 1)], pool[bk][:, :]),
                            waits=[("pe", tok_mm)], inc="act")
                        pool_free[bk] = [("act", vt_cp)]
                    tok_b3 = P.op("PE", lambda e: e.matmul(
                        pmisc[:, 0:1], ones1f[:, :], bias_t[:, :], start=True, stop=True),
                        waits=[("dve", tok_b2)] + pmisc_free + pkn_free[0], inc="pe")
                    pmisc_free = []
                    pkn_free[0] = []
                    tok_bias = P.op("DVE", lambda e: e.tensor_copy(
                        bias_sb[:, :], pmisc[:, 0:1]), waits=[("pe", tok_b3)], inc="dve")
                    pmisc_free = [("dve", tok_bias)]
                    for gq in range(4):
                        gi = 4 * h + gq
                        niter = 16 * (gq + 1)
                        t_q = q_loaded[gi]
                        par_g = gq % 2
                        s_tok = {}
                        den_tok = {}
                        pt_tok = {}

                        def s_mm_pair(p):
                            it0, it1 = 2 * p, 2 * p + 1
                            bk0, bk1 = it0 % 4, it1 % 4
                            waits = []
                            if it0 < 4:
                                waits += pool_free[bk0]
                            pool_free[bk0] = []
                            if it0 == 0:
                                waits += [t_q, ("act", kn_cp)]
                            P.op("PE", lambda e, it=it0, bk=bk0, par_g=par_g: e.matmul(
                                pool[bk][:, :], knope_sb[:, 128 * it:128 * (it + 1)],
                                qn_sb[par_g][:, :], start=True, stop=False), waits=waits)
                            waits1 = []
                            if it1 < 4:
                                waits1 += pool_free[bk1]
                            pool_free[bk1] = []
                            P.op("PE", lambda e, it=it1, bk=bk1, par_g=par_g: e.matmul(
                                pool[bk][:, :], knope_sb[:, 128 * it:128 * (it + 1)],
                                qn_sb[par_g][:, :], start=True, stop=False), waits=waits1)
                            P.op("PE", lambda e, it=it0, bk=bk0, par_g=par_g: e.matmul(
                                pool[bk][:, :], krope_sb[0:64, 128 * it:128 * (it + 1)],
                                qr_sb[par_g][0:64, :], start=False, stop=True,
                                tile_position=(0, 0)))
                            tok = P.op("PE", lambda e, it=it1, bk=bk1, par_g=par_g: e.matmul(
                                pool[bk][:, :], krope_sb[64:128, 128 * it:128 * (it + 1)],
                                qr_sb[par_g][64:128, :], start=False, stop=True,
                                tile_position=(64, 0)), inc="pe")
                            s_tok[it0] = tok
                            s_tok[it1] = tok

                        def s_exp(it, pe_tok):
                            bk = it % 4
                            tok_e = P.op("ACT", lambda e, bk=bk: e.activation(
                                ptb[bk][:, :], pool[bk][:, :], AF.Exp,
                                bias=bias_sb[:, 0:1], scale=SSCALE),
                                waits=[("pe", pe_tok), ("dve", tok_bias)] + pt_free[bk], inc="act")
                            pt_free[bk] = []
                            pool_free[bk] = [("act", tok_e)]
                            if it >= 16 * gq:
                                tok_m = P.op("DVE", lambda e, it=it, bk=bk, gq=gq: e.scalar_tensor_tensor(
                                    ptb[bk][:, :], qpos_sb[:, 512 * gq:512 * (gq + 1)],
                                    kposc_sb[:, it:it + 1], ptb[bk][:, :],
                                    ALU.is_ge, ALU.mult),
                                    waits=[("act", tok_e), t_qp, t_kp], inc="dve")
                                pt_tok[it] = ("dve", tok_m)
                            else:
                                pt_tok[it] = ("act", tok_e)

                        for p in range(2):
                            s_mm_pair(p)
                            s_exp(2 * p, s_tok[2 * p])
                            s_exp(2 * p + 1, s_tok[2 * p + 1])
                        if gi + 1 < NG:
                            q_loaded[gi + 1] = q_load(gi + 1, [("pe", P.cnt["pe"])])
                        pend_bk = None
                        for it in range(niter):
                            bk = it % 4
                            waits_pv = [pt_tok[it]]
                            if it == 0:
                                waits_pv += po_free[par_g]
                                po_free[par_g] = []
                            P.op("PE", lambda e, it=it, bk=bk, par_g=par_g, niter=niter: e.matmul(
                                po[par_g][:, :], vtm_sb[:, 128 * it:128 * (it + 1)],
                                ptb[bk][:, :], start=(it == 0), stop=(it == niter - 1)),
                                waits=waits_pv)
                            den_w = []
                            if it == 0:
                                den_w = pden_free + pkn_free[1]
                                pden_free = []
                                pkn_free[1] = []
                            do_inc = (it % 2 == 1) or (it == niter - 1)
                            tok_pv = P.op("PE", lambda e, it=it, bk=bk, niter=niter: e.matmul(
                                pden[:, :], ones_mat[:, :], ptb[bk][:, :],
                                start=(it == 0), stop=(it == niter - 1)),
                                waits=den_w, inc="pe" if do_inc else None)
                            den_tok[it] = tok_pv
                            if do_inc:
                                pt_free[bk] = [("pe", tok_pv)]
                                if pend_bk is not None:
                                    pt_free[pend_bk] = [("pe", tok_pv)]
                                    pend_bk = None
                            else:
                                pend_bk = bk
                            if it % 2 == 1 and it + 3 < niter:
                                s_mm_pair((it + 3) // 2)
                                s_exp(it + 3, s_tok[it + 3])
                                s_exp(it + 4, s_tok[it + 4])
                        tok_dc = P.op("DVE", lambda e: e.tensor_copy(
                            dfull[:, :], pden[:, :]), waits=[("pe", P.cnt["pe"])], inc="dve")
                        pden_free = [("dve", tok_dc)]
                        tok_r = P.op("DVE", lambda e: e.reciprocal(
                            rrep[:, :], dfull[:, :]), inc="dve")
                        tok_ob = P.op("DVE", lambda e, par_g=par_g: e.tensor_mul(
                            ob_sb[par_g][:, :], po[par_g][:, :], rrep[:, :]),
                            waits=ob_free[par_g], inc="dve")
                        ob_free[par_g] = []
                        po_free[par_g] = [("dve", tok_ob)]
                        t_ost = P.dma("SP", o_scr[gq, :, h, :],
                                      ob_sb[par_g][:, :], waits=[("dve", tok_ob)], cls="st")
                        ob_free[par_g] = [t_ost]
                    kn_free = [("pe", P.cnt["pe"])]
                    vt_free = [("pe", P.cnt["pe"])]
                    if h == 15 and d_kn15 is not None:
                        P.dma("SP", d_kn15, knope_sb[:, :], waits=[("pe", P.cnt["pe"])])
                        P.dma("SP", d_vt15, vtm_sb[:, :])
                P.op("SP", lambda e: e.nop(), waits=P.dma_all("SP") + P.dma_all("PL") + [("dve", P.cnt["dve"])])
                if d_oscr is not None:
                    P.dma("SP", d_oscr, o_scr)
                    P.op("SP", lambda e: e.nop(), waits=P.dma_all("SP") + P.dma_all("PL"))
                P.emit()

        if upto < 4:
            _final_wait(nc, P, t_out)
            return nc, dbg_aps

        P.phase()
        with ExitStack() as dph:
            wob = [dph.enter_context(nc.sbuf_tensor(f"wob{i}", [128, 2048], BF16)) for i in range(2)]
            osb = [dph.enter_context(nc.sbuf_tensor(f"osb{i}", [128, 16 * 512], BF16)) for i in range(2)]
            xtile = [dph.enter_context(nc.sbuf_tensor(f"xtile{i}", [128, 512], F32)) for i in range(2)]
            r1 = [dph.enter_context(nc.sbuf_tensor(f"r1_{i}", [128, 16 * 512], F32)) for i in range(2)]
            r1b = dph.enter_context(nc.sbuf_tensor("r1b", [128, 16 * 512], BF16))
            sqD = dph.enter_context(nc.sbuf_tensor("sqD", [128, 16 * 512], BF16))
            m2rep = dph.enter_context(nc.sbuf_tensor("m2rep", [128, 512], F32))
            mrep = [dph.enter_context(nc.sbuf_tensor(f"mrep{i}", [128, 512], F32)) for i in range(2)]
            rrepD = [dph.enter_context(nc.sbuf_tensor(f"rrepD{i}", [128, 512], F32)) for i in range(2)]
            h1t = dph.enter_context(nc.sbuf_tensor("h1t", [128, 16 * 512], F32))
            tmpD = [dph.enter_context(nc.sbuf_tensor(f"tmpD{i}", [128, 512], F32)) for i in range(2)]
            pwo = [dph.enter_context(nc.psum_tensor(f"pwo{i}", [128, 512], F32)) for i in range(4)]
            pmD = dph.enter_context(nc.psum_tensor("pmD", [128, 512], F32))
            pvD = dph.enter_context(nc.psum_tensor("pvD", [128, 512], F32))


            bank_free = {i: [] for i in range(4)}
            wob_free = {0: [], 1: []}
            osb_free = {0: [], 1: []}
            r1b_free = []
            r1_free = {0: [], 1: []}
            bc_free = {0: [], 1: []}
            h1_free = []
            h1_stores = []
            pending_D = []

            def flush_D(pend):
                nonlocal r1_free, bc_free, h1_free
                gq2, opar2, tok_cr2 = pend["gq"], pend["opar"], pend["tok_cr"]
                last = {"DVE": None, "PL": None}
                for oc in range(16):
                    eng = "DVE" if oc % 2 == 0 else "PL"
                    P.op(eng, lambda e, oc=oc, opar2=opar2: e.tensor_sub(
                        tmpD[oc % 2][:, :],
                        r1[opar2][:, 512 * oc:512 * (oc + 1)], mrep[opar2][:, :]),
                        waits=([("act", tok_cr2)] if oc < 2 else []) + (h1_free if oc < 2 else []))
                    last[eng] = P.op(eng, lambda e, oc=oc, opar2=opar2: e.tensor_mul(
                        h1t[:, 512 * oc:512 * (oc + 1)], tmpD[oc % 2][:, :],
                        rrepD[opar2][:, :]),
                        inc="dve" if eng == "DVE" else "plc")
                h1_free = []
                toks = [("dve", last["DVE"]), ("plc", last["PL"])]
                st = None
                for oc in range(16):
                    st = P.dma("SP", h1_scr[128 * oc:128 * (oc + 1), 512 * gq2:512 * (gq2 + 1)],
                               h1t[:, 512 * oc:512 * (oc + 1)],
                               waits=toks if oc == 0 else (), cls="st")
                    h1_stores.append(st)
                h1_free = [st]
                r1_free[opar2] = toks
                bc_free[opar2] = toks

            t_o = {}
            t_o[0] = [P.dma("SP", osb[0][:, :], o_scr[0])]
            xt_free = {0: [], 1: []}
            o_free = []
            stat_free = []
            for gq in range(4):
                opar = gq % 2
                if gq + 1 < 4:
                    t_o[gq + 1] = [P.dma("SP", osb[(gq + 1) % 2][:, :],
                                         o_scr[gq + 1], waits=osb_free[(gq + 1) % 2])]
                    osb_free[(gq + 1) % 2] = []
                mm_toks = {}
                for oc in range(16):
                    bk = oc % 4
                    par = oc % 2
                    wpar = oc % 2
                    t_w = P.dma("PL", wob[wpar][:, :], wo_bf[oc], waits=wob_free[wpar])
                    wob_free[wpar] = []
                    t_x = P.dma("PL", xtile[par][:, :],
                                t_xq[128 * oc:128 * (oc + 1), 512 * gq:512 * (gq + 1)],
                                waits=xt_free[par])
                    xt_free[par] = []
                    for c in range(16):
                        waits = []
                        if c == 0:
                            waits = [t_o[gq][-1], t_w] + bank_free[bk]
                            bank_free[bk] = []
                        P.op("PE", lambda e, c=c, oc=oc, bk=bk, wpar=wpar, opar=opar: e.matmul(
                            pwo[bk][:, :], wob[wpar][:, 128 * c:128 * (c + 1)],
                            osb[opar][:, 512 * c:512 * (c + 1)],
                            start=(c == 0), stop=(c == 15)),
                            waits=waits, inc="pe" if c == 15 else None)
                    mm_toks[oc] = P.cnt["pe"]
                    wob_free[wpar] = [("pe", mm_toks[oc])]
                    tok_r1 = P.op("DVE", lambda e, oc=oc, bk=bk, par=par, opar=opar: e.tensor_add(
                        r1[opar][:, 512 * oc:512 * (oc + 1)], pwo[bk][:, :], xtile[par][:, :]),
                        waits=[("pe", mm_toks[oc]), t_x] + (r1_free[opar] if oc == 0 else []), inc="dve")
                    if oc == 0:
                        r1_free[opar] = []
                    bank_free[bk] = [("dve", tok_r1)]
                    xt_free[par] = [("dve", tok_r1)]
                    tok_cb = P.op("ACT", lambda e, oc=oc, opar=opar: e.copy(
                        r1b[:, 512 * oc:512 * (oc + 1)], r1[opar][:, 512 * oc:512 * (oc + 1)]),
                        waits=[("dve", tok_r1)] + (r1b_free if oc == 0 else []), inc="act")
                    if oc == 0:
                        r1b_free = []
                    tok_sq = P.op("ACT", lambda e, oc=oc: e.activation(
                        sqD[:, 512 * oc:512 * (oc + 1)],
                        r1b[:, 512 * oc:512 * (oc + 1)], AF.Square), inc="act")
                    sq_tok_last = tok_sq
                for oc in range(16):
                    P.op("PE", lambda e, oc=oc: e.matmul(
                        pmD[:, :], ones_mat[:, :], r1b[:, 512 * oc:512 * (oc + 1)],
                        start=(oc == 0), stop=(oc == 15)),
                        waits=([("act", sq_tok_last)] + stat_free) if oc == 0 else [],
                        inc="pe" if oc == 15 else None)
                stat_free = []
                for oc in range(16):
                    P.op("PE", lambda e, oc=oc: e.matmul(
                        pvD[:, :], ones_mat[:, :], sqD[:, 512 * oc:512 * (oc + 1)],
                        start=(oc == 0), stop=(oc == 15)),
                        inc="pe" if oc == 15 else None)
                r1b_free = [("pe", P.cnt["pe"])]
                tok_stats = P.cnt["pe"]
                P.op("ACT", lambda e, opar=opar: e.activation(
                    mrep[opar][:, :], pmD[:, :], AF.Copy, scale=1.0 / 2048.0),
                    waits=[("pe", tok_stats)] + bc_free[opar])
                bc_free[opar] = []
                P.op("ACT", lambda e, opar=opar: e.activation(
                    rrepD[opar][:, :], pvD[:, :], AF.Copy, scale=1.0 / 2048.0))
                tok_m2 = P.op("ACT", lambda e, opar=opar: e.activation(
                    m2rep[:, :], mrep[opar][:, :], AF.Square), inc="act")
                stat_free = [("act", tok_m2)]
                tok_var = P.op("DVE", lambda e, opar=opar: e.tensor_sub(
                    rrepD[opar][:, :], rrepD[opar][:, :], m2rep[:, :]), waits=[("act", tok_m2)], inc="dve")
                P.op("ACT", lambda e, opar=opar: e.activation(
                    rrepD[opar][:, :], rrepD[opar][:, :], AF.Ln, bias=consts_sb[:, 1:2]),
                    waits=[("dve", tok_var)])
                tok_cr = P.op("ACT", lambda e, opar=opar: e.activation(
                    rrepD[opar][:, :], rrepD[opar][:, :], AF.Exp, scale=-0.5), inc="act")
                pending_D.append(dict(gq=gq, opar=opar, tok_cr=tok_cr))
                if len(pending_D) > 1:
                    flush_D(pending_D.pop(0))
                osb_free[opar] = [("pe", mm_toks[15])]
            flush_D(pending_D.pop(0))
            P.op("SP", lambda e: e.nop(), waits=P.dma_all("SP") + P.dma_all("PL") + P.dma_all("ACT") + [("dve", P.cnt["dve"])])
            if d_h1 is not None:
                P.dma("SP", d_h1, h1_scr)
                P.op("SP", lambda e: e.nop(), waits=P.dma_all("SP") + P.dma_all("PL"))
            P.emit()

        if upto < 5:
            _final_wait(nc, P, t_out)
            return nc, dbg_aps

        P.phase()
        with ExitStack() as eph:
            h1b = eph.enter_context(nc.sbuf_tensor("h1b", [128, 16 * 512], BF16))
            fuse = eph.enter_context(nc.sbuf_tensor("fuse", [128, 64 * 512], BF16))
            wgb = [eph.enter_context(nc.sbuf_tensor(f"wgb{i}", [128, 2048], BF16)) for i in range(2)]
            wub = [eph.enter_context(nc.sbuf_tensor(f"wub{i}", [128, 2048], BF16)) for i in range(2)]
            wdb = [eph.enter_context(nc.sbuf_tensor(f"wdb{i}", [128, 8192], BF16)) for i in range(2)]
            sg_sb = eph.enter_context(nc.sbuf_tensor("sg_sb", [128, 64], F32))
            su_sb = eph.enter_context(nc.sbuf_tensor("su_sb", [128, 64], F32))
            sd_sb = eph.enter_context(nc.sbuf_tensor("sd_sb", [128, 16], F32))
            tg = [eph.enter_context(nc.sbuf_tensor(f"tgE{i}", [128, 512], BF16)) for i in range(2)]
            tu = [eph.enter_context(nc.sbuf_tensor(f"tuE{i}", [128, 512], BF16)) for i in range(2)]
            h1f = [eph.enter_context(nc.sbuf_tensor(f"h1f{i}", [128, 512], F32)) for i in range(2)]
            r2 = eph.enter_context(nc.sbuf_tensor("r2", [128, 16 * 512], F32))
            identE = eph.enter_context(nc.sbuf_tensor("identE", [128, 128], F32))
            rtm = eph.enter_context(nc.sbuf_tensor("rtm", [128, 2048], F32))
            sq2 = eph.enter_context(nc.sbuf_tensor("sq2", [128, 2048], F32))
            st1 = eph.enter_context(nc.sbuf_tensor("st1", [128, 1], F32))
            st2 = eph.enter_context(nc.sbuf_tensor("st2", [128, 1], F32))
            stv = eph.enter_context(nc.sbuf_tensor("stv", [128, 1], F32))
            outb = eph.enter_context(nc.sbuf_tensor("outb", [128, 2048], F32))
            psG = [eph.enter_context(nc.psum_tensor(f"psG{i}", [128, 512], F32)) for i in range(2)]
            psU = [eph.enter_context(nc.psum_tensor(f"psU{i}", [128, 512], F32)) for i in range(2)]
            psD = [eph.enter_context(nc.psum_tensor(f"psD{i}", [128, 512], F32)) for i in range(2)]

            t_sg = P.dma("SP", sg_sb[:, :], t_sg)
            t_su = P.dma("SP", su_sb[:, :], t_su)
            t_sd = P.dma("SP", sd_sb[:, :], t_sd)
            P.op("PL", lambda e: e.memset(identE[:, :], 1.0))
            tok_id = P.op("PL", lambda e: e.affine_select(
                out=identE[:, :], in_=identE[:, :], compare_op=ALU.is_equal,
                fill=0.0, base=0, pattern=[[-1, 128]], channel_multiplier=1),
                inc="plc")

            wg_free = {0: [], 1: []}
            wu_free = {0: [], 1: []}
            wd_free = {0: [], 1: []}
            gu_free = {0: [], 1: []}
            d_free = {0: [], 1: []}
            tg_free = {0: [], 1: []}
            tu_free = {0: [], 1: []}
            fuse_free = []
            h1b_free = []
            h1f_free = {0: [], 1: []}
            r2_free = []
            rtm_free = []
            out_free = []
            out_stores = []
            def load_h1b(sw, waits=()):
                src = h1_scr[:, 512 * sw:512 * (sw + 1)].rearrange(
                    "(c p) q -> p c q", p=128)
                dst = h1b[:, :].rearrange("p (c q) -> p c q", c=16)
                return [P.dma("PL", dst, src, waits=waits)]
            t_h1map = {}
            t_h1map[0] = load_h1b(0)
            for sw in range(4):
                t_h1 = t_h1map[sw]
                for ic in range(64):
                    par = ic % 2
                    t_g = P.dma("PL", wgb[par][:, :], t_wg[ic], waits=wg_free[par])
                    t_u = P.dma("PL", wub[par][:, :], t_wu[ic], waits=wu_free[par])
                    wg_free[par], wu_free[par] = [], []
                    for c in range(16):
                        waits = []
                        if c == 0:
                            waits = [t_g, t_h1[-1]] + gu_free[par] + (fuse_free if ic == 0 else [])
                            gu_free[par] = []
                            if ic == 0:
                                fuse_free = []
                        P.op("PE", lambda e, c=c, par=par: e.matmul(
                            psG[par][:, :], wgb[par][:, 128 * c:128 * (c + 1)],
                            h1b[:, 512 * c:512 * (c + 1)],
                            start=(c == 0), stop=(c == 15)), waits=waits)
                    for c in range(16):
                        waits = [t_u] if c == 0 else []
                        P.op("PE", lambda e, c=c, par=par: e.matmul(
                            psU[par][:, :], wub[par][:, 128 * c:128 * (c + 1)],
                            h1b[:, 512 * c:512 * (c + 1)],
                            start=(c == 0), stop=(c == 15)),
                            waits=waits, inc="pe" if c == 15 else None)
                    tok_mm = P.cnt["pe"]
                    wg_free[par] = [("pe", tok_mm)]
                    wu_free[par] = [("pe", tok_mm)]
                    tok_g = P.op("ACT", lambda e, par=par, ic=ic: e.activation(
                        tg[par][:, :], psG[par][:, :], AF.Silu,
                        scale=sg_sb[:, ic:ic + 1]),
                        waits=[("pe", tok_mm), t_sg] + tg_free[par], inc="act")
                    tg_free[par] = []
                    tok_u = P.op("DVE", lambda e, par=par, ic=ic: e.tensor_scalar_mul(
                        tu[par][:, :], psU[par][:, :], su_sb[:, ic:ic + 1]),
                        waits=[("pe", tok_mm), t_su] + tu_free[par], inc="dve")
                    tu_free[par] = []
                    gu_free[par] = [("act", tok_g), ("dve", tok_u)]
                    tok_f = P.op("DVE", lambda e, par=par, ic=ic: e.tensor_mul(
                        fuse[:, 512 * ic:512 * (ic + 1)], tg[par][:, :], tu[par][:, :]),
                        waits=[("act", tok_g)], inc="dve")
                    tg_free[par] = [("dve", tok_f)]
                    tu_free[par] = [("dve", tok_f)]
                tok_fuse_all = P.cnt["dve"]
                if sw + 1 < 4:
                    t_h1map[sw + 1] = load_h1b(sw + 1, waits=[("pe", P.cnt["pe"])])
                for oc in range(16):
                    par = oc % 2
                    t_wd_ld = P.dma("PL", wdb[par][:, :], t_wd[oc], waits=wd_free[par])
                    wd_free[par] = []
                    t_h1f = P.dma("SP", h1f[par][:, :],
                                  h1_scr[128 * oc:128 * (oc + 1), 512 * sw:512 * (sw + 1)],
                                  waits=h1f_free[par])
                    h1f_free[par] = []
                    for c in range(64):
                        waits = []
                        if c == 0:
                            waits = [t_wd_ld, ("dve", tok_fuse_all)] + d_free[par] + (r2_free if oc == 0 else [])
                            d_free[par] = []
                            if oc == 0:
                                r2_free = []
                        P.op("PE", lambda e, c=c, par=par: e.matmul(
                            psD[par][:, :], wdb[par][:, 128 * c:128 * (c + 1)],
                            fuse[:, 512 * c:512 * (c + 1)],
                            start=(c == 0), stop=(c == 63)),
                            waits=waits, inc="pe" if c == 63 else None)
                    tok_mm = P.cnt["pe"]
                    wd_free[par] = [("pe", tok_mm)]
                    tok_r2 = P.op("DVE", lambda e, oc=oc, par=par: e.scalar_tensor_tensor(
                        r2[:, 512 * oc:512 * (oc + 1)], psD[par][:, :],
                        sd_sb[:, oc:oc + 1], h1f[par][:, :], ALU.mult, ALU.add),
                        waits=[("pe", tok_mm), t_h1f, t_sd], inc="dve")
                    d_free[par] = [("dve", tok_r2)]
                    h1f_free[par] = [("dve", tok_r2)]
                tok_r2_all = P.cnt["dve"]
                fuse_free = [("pe", P.cnt["pe"])]
                tr_free = []
                for q4 in range(4):
                    for oc in range(16):
                        bank = (psG + psU)[oc // 4]
                        waits = []
                        if oc == 0:
                            waits = [("dve", tok_r2_all), ("plc", tok_id)] + tr_free
                            tr_free = []
                            if q4 == 0:
                                waits += gu_free[0] + gu_free[1]
                                gu_free[0], gu_free[1] = [], []
                        P.op("PE", lambda e, oc=oc, q4=q4, bank=bank: e.transpose(
                            bank[:, 128 * (oc % 4):128 * (oc % 4 + 1)],
                            r2[:, 512 * oc + 128 * q4:512 * oc + 128 * q4 + 128],
                            identE[:, :]),
                            waits=waits, inc="pe" if oc % 4 == 3 else None)
                    tok_tr = P.cnt["pe"]
                    tok_rtm = None
                    for bk in range(4):
                        tok_rtm = P.op("DVE", lambda e, bk=bk: e.tensor_copy(
                            rtm[:, 512 * bk:512 * (bk + 1)], (psG + psU)[bk][:, :]),
                            waits=[("pe", tok_tr)] if bk == 0 else [],
                            inc="dve" if bk == 3 else None)
                    tr_free = [("dve", tok_rtm)]
                    gu_free[0] = [("dve", tok_rtm)]
                    gu_free[1] = [("dve", tok_rtm)]
                    P.op("DVE", lambda e: e.reduce_sum(st1[:, :], rtm[:, :], AX.X),
                         waits=[("dve", tok_rtm)])
                    tok_sq = P.op("ACT", lambda e: e.activation(
                        sq2[:, :], rtm[:, :], AF.Square), waits=[("dve", tok_rtm)], inc="act")
                    tok_s2 = P.op("DVE", lambda e: e.reduce_sum(st2[:, :], sq2[:, :], AX.X),
                                  waits=[("act", tok_sq)], inc="dve")
                    P.op("DVE", lambda e: e.tensor_scalar_mul(st1[:, :], st1[:, :], 1.0 / 2048.0))
                    tok_sts = P.op("DVE", lambda e: e.tensor_scalar_mul(
                        st2[:, :], st2[:, :], 1.0 / 2048.0), inc="dve")
                    tok_m2 = P.op("ACT", lambda e: e.activation(
                        stv[:, :], st1[:, :], AF.Square), waits=[("dve", tok_sts)], inc="act")
                    tok_var = P.op("DVE", lambda e: e.tensor_sub(
                        stv[:, :], st2[:, :], stv[:, :]), waits=[("act", tok_m2)], inc="dve")
                    P.op("ACT", lambda e: e.activation(
                        stv[:, :], stv[:, :], AF.Ln, bias=consts_sb[:, 1:2]),
                        waits=[("dve", tok_var)])
                    tok_rstd = P.op("ACT", lambda e: e.activation(
                        stv[:, :], stv[:, :], AF.Exp, scale=-0.5), inc="act")
                    tok_out = P.op("DVE", lambda e: e.tensor_scalar(
                        outb[:, :], rtm[:, :], st1[:, :], stv[:, :],
                        ALU.subtract, ALU.mult),
                        waits=[("act", tok_rstd)] + out_free, inc="dve")
                    out_free = []
                    rtm_free = [("dve", tok_out)]
                    t_st = P.dma("SP", t_out[128 * (4 * sw + q4):128 * (4 * sw + q4 + 1), :],
                                 outb[:, :], waits=[("dve", tok_out)], cls="st")
                    out_stores.append(t_st)
                    out_free = [t_st]
                r2_free = [("pe", P.cnt["pe"])]
            P.op("SP", lambda e: e.nop(), waits=P.dma_all("SP") + P.dma_all("PL"))
            P.emit()

    return nc, dbg_aps


def _final_wait(nc, P, t_out):
    P.phase()
    with nc.sbuf_tensor("zz", [128, 2048], F32) as zz:
        P.op("PL", lambda e: e.memset(zz[:, :], 0.0), inc="plc")
        for i in range(16):
            P.dma("SP", t_out[128 * i:128 * (i + 1), :], zz[:, :],
                  waits=[("plc", P.cnt["plc"])])
        P.op("SP", lambda e: e.nop(), waits=P.dma_all("SP") + P.dma_all("PL"))
        P.emit()


_NC_CACHE = {}


def _get_nc(upto=5, dbg=()):
    key = (upto, tuple(sorted(dbg)))
    if key not in _NC_CACHE:
        _NC_CACHE[key] = build_nc(upto, dbg)
    return _NC_CACHE[key]


def kernel(**inputs):
    in_maps, tiles_per_core = host_prep(inputs)
    nc, _ = _get_nc()
    res = run_bass_kernel_spmd(nc, in_maps, core_ids=list(range(8)))
    out = np.zeros((B, S, D), np.float32)
    for core in range(8):
        b, tiles = tiles_per_core[core]
        r = res.results[core]["out"]
        for s, t in enumerate(tiles):
            out[b, 128 * t:128 * t + 128, :] = r[128 * s:128 * s + 128, :]
    return out


# version 22
# speedup vs baseline: 1.1944x; 1.0003x over previous
import numpy as np
import ml_dtypes
from contextlib import ExitStack

import concourse.bass as bass
import concourse.mybir as mybir
from concourse.bass_utils import run_bass_kernel_spmd

F32 = mybir.dt.float32
BF16 = mybir.dt.bfloat16
I8 = mybir.dt.int8
AF = mybir.ActivationFunctionType
ALU = mybir.AluOpType
AX = mybir.AxisListType

B, S, D, H = 2, 8192, 2048, 16
DN, DR, DK, KVL, DV = 128, 64, 192, 512, 128
INTER = 8192
NSLOT = 16
OWN = NSLOT * 128
RMS_EPS = 1.1920928955078125e-07
LN_EPS = 1e-5
SSCALE = float(1.0 / np.sqrt(DK))
BOUND_MARGIN = 1.04

ENGS = ("PE", "ACT", "DVE", "SP", "PL")


class Prog:

    NPOOL = 4

    def __init__(self, nc, ctx):
        self.nc = nc
        self.sem = {}
        self.cnt = {}
        for name in ("pe", "act", "dve", "plc"):
            self.sem[name] = ctx.enter_context(nc.semaphore(name))
            self.cnt[name] = 0
        self.pool = {}
        self.pool_idx = {}
        self.open = {}
        self.all_pool_sems = set()
        for q in ("SP", "PL", "SPst", "PLst", "ACTst"):
            names = []
            for i in range(self.NPOOL):
                nm = f"{q.lower()}{i}"
                self.sem[nm] = ctx.enter_context(nc.semaphore(nm))
                self.cnt[nm] = 0
                names.append(nm)
                self.all_pool_sems.add(nm)
            self.pool[q] = names
            self.pool_idx[q] = -1
            self.open[q] = None
        self.qs = None

    def phase(self):
        self.qs = {e: [] for e in ENGS}

    _AUTO = {"DVE": "dve", "ACT": "act", "PL": "plc"}

    def op(self, eng, fn, waits=(), inc=None, n=1, noclose=()):
        auto = self._AUTO.get(eng)
        if auto is not None and (inc is None or inc == auto):
            waits = list(waits) + [(auto, self.cnt[auto])]
            inc = auto
        ws = []
        for (s, v) in waits:
            if s in self.all_pool_sems:
                if s not in noclose:
                    v = self.cnt[s]
                    for q in self.pool:
                        if self.open[q] == s:
                            self.open[q] = None
            if v > 0:
                ws.append((s, v))
        dedup = {}
        for (s, v) in ws:
            dedup[s] = max(v, dedup.get(s, 0))
        ws = tuple(dedup.items())
        self.qs[eng].append((fn, ws, inc, n))
        if inc is not None:
            self.cnt[inc] += n
            return self.cnt[inc]
        return None

    def dma(self, queue, out, in_, waits=(), cls="ld"):
        eng = queue
        if cls == "st":
            queue = queue + "st"
        elif queue == "ACT":
            queue = "ACTst"
        extra = []
        if self.open[queue] is None:
            self.pool_idx[queue] += 1
            sem = self.pool[queue][self.pool_idx[queue] % self.NPOOL]
            self.open[queue] = sem
            if self.cnt[sem] > 0:
                extra = [(sem, self.cnt[sem])]
        sem = self.open[queue]
        self.op(eng, lambda e, o=out, i=in_: e.dma_start(out=o, in_=i),
                waits=list(waits) + extra, inc=sem, n=16, noclose=(sem,))
        return (sem, self.cnt[sem])

    def dma_all(self, queue):
        toks = []
        qs = (queue, queue + "st") if queue != "ACT" else ("ACTst",)
        for q in qs:
            self.open[q] = None
            toks += [(s, self.cnt[s]) for s in self.pool[q] if self.cnt[s] > 0]
        return toks

    def emit(self):
        blk_ctx = self.nc.Block()
        blk = blk_ctx.__enter__()
        try:
            for engname, attr in (("PE", "tensor"), ("ACT", "scalar"),
                                  ("DVE", "vector"), ("SP", "sync"),
                                  ("PL", "gpsimd")):
                ops = self.qs[engname]
                if not ops:
                    continue

                def run(e, ops=ops):
                    for fn, waits, inc, n in ops:
                        for (s, v) in waits:
                            e.wait_ge(self.sem[s], v)
                        i = fn(e)
                        if inc is not None:
                            i.then_inc(self.sem[inc], n)
                getattr(blk, attr)(run)
        finally:
            blk_ctx.__exit__(None, None, None)



def _rope_tables():
    freqs = 1.0 / 10000.0 ** (np.arange(0, DR, 2, dtype=np.float64) / DR)
    t = np.arange(S, dtype=np.float64)[:, None] * freqs[None, :]
    return (np.cos(t).T.astype(np.float32), np.sin(t).T.astype(np.float32))


def _wq_col_order():
    nope = np.concatenate([192 * h + np.arange(128) for h in range(H)])
    rope = []
    for r in range(8):
        h0, h1 = 2 * r, 2 * r + 1
        rope += [192 * h0 + 128 + 2 * np.arange(32),
                 192 * h1 + 128 + 2 * np.arange(32),
                 192 * h0 + 128 + 2 * np.arange(32) + 1,
                 192 * h1 + 128 + 2 * np.arange(32) + 1]
    return np.concatenate([nope] + rope)


def host_prep(inputs):
    x = np.ascontiguousarray(np.asarray(inputs["L0_attn_x"]))
    Wq = np.asarray(inputs["L0_attn_Wq"])
    Wkva = np.asarray(inputs["L0_attn_Wkv_a"])
    kvnw = np.asarray(inputs["L0_attn_kv_norm_w"])
    Wkvb = np.asarray(inputs["L0_attn_Wkv_b"])
    Wo = np.asarray(inputs["L0_attn_Wo"])
    wg = np.asarray(inputs["L0_ffn_w_gate"])
    wu = np.asarray(inputs["L0_ffn_w_up"])
    wd = np.asarray(inputs["L0_ffn_w_down"])
    sg = np.asarray(inputs["L0_ffn_s_gate"])
    su = np.asarray(inputs["L0_ffn_s_up"])
    sd = np.asarray(inputs["L0_ffn_s_down"])

    cosf, sinf = _rope_tables()
    cos2 = np.ascontiguousarray(np.vstack([cosf, cosf]))
    sin2 = np.ascontiguousarray(np.vstack([sinf, sinf]))

    wq_p = np.ascontiguousarray(
        Wq[:, _wq_col_order()].reshape(16, 128, 24, 128).transpose(2, 1, 0, 3)
        .reshape(24, 128, 2048))
    rc = 512 + np.concatenate([2 * np.arange(32), 2 * np.arange(32) + 1])
    wkva_p = np.ascontiguousarray(
        np.concatenate([Wkva[:, :512], Wkva[:, rc]], axis=1)
        .reshape(16, 128, 576))
    kvnw_p = np.ascontiguousarray(kvnw.reshape(4, 128).T)
    wkvb_p = np.ascontiguousarray(Wkvb.reshape(4, 128, 4096))
    wo_p = np.ascontiguousarray(
        Wo.reshape(16, 128, 16, 128).transpose(2, 1, 0, 3).reshape(16, 128, 2048))
    wg_p = np.ascontiguousarray(
        wg.reshape(16, 128, 64, 128).transpose(2, 1, 0, 3))
    wu_p = np.ascontiguousarray(
        wu.reshape(16, 128, 64, 128).transpose(2, 1, 0, 3))
    wd_p = np.ascontiguousarray(
        wd.reshape(64, 128, 16, 128).transpose(2, 1, 0, 3))
    sg_p = np.ascontiguousarray(sg.reshape(64, 128).T)
    su_p = np.ascontiguousarray(su.reshape(64, 128).T)
    sd_p = np.ascontiguousarray(sd.reshape(16, 128).T)
    selv = np.zeros((128, 2), np.float32)
    selv[0:32, 0] = 1.0
    selv[64:96, 0] = 1.0
    selv[32:64, 1] = 1.0
    selv[96:128, 1] = 1.0
    sel = np.ascontiguousarray(np.repeat(selv, 128, axis=1))
    kposc = (np.arange(128, dtype=np.float32)[:, None]
             + 128.0 * np.arange(64, dtype=np.float32)[None, :])
    kposc = np.ascontiguousarray(kposc)
    consts = np.zeros((128, 4), np.float32)
    consts[:, 0] = RMS_EPS
    consts[:, 1] = LN_EPS
    consts = np.ascontiguousarray(consts)

    xf = [np.ascontiguousarray(x[b].T) for b in range(B)]

    in_maps = []
    tiles_per_core = []
    for core in range(8):
        b, j = divmod(core, 4)
        tiles = [4 * s + j for s in range(NSLOT)]
        tiles_per_core.append((b, tiles))
        qcols = np.concatenate([np.arange(128 * t, 128 * t + 128)
                                for t in tiles])
        xq = np.ascontiguousarray(xf[b][:, qcols])
        qpos_rep = np.ascontiguousarray(
            np.broadcast_to(qcols.astype(np.float32)[None, :], (128, OWN)))
        cos_own = np.ascontiguousarray(cos2[:, qcols])
        sin_own = np.ascontiguousarray(sin2[:, qcols])
        in_maps.append(dict(
            xf=xf[b], xq=xq, qpos=qpos_rep,
            cos_own=cos_own, sin_own=sin_own, cosf=cosf, sinf=sinf,
            kposc=kposc, sel=sel, consts=consts,
            wq=wq_p, wkva=wkva_p, kvnw=kvnw_p, wkvb=wkvb_p, wo=wo_p,
            wg=wg_p, wu=wu_p, wd=wd_p, sg=sg_p, su=su_p, sd=sd_p,
        ))
    return in_maps, tiles_per_core



def build_nc(upto=5, dbg=()):
    nc = bass.Bass("TRN2", target_bir_lowering=False)
    dbg = set(dbg)

    t_xf = nc.dram_tensor("xf", [D, S], F32, kind="ExternalInput").ap()
    t_xq = nc.dram_tensor("xq", [D, OWN], F32, kind="ExternalInput").ap()
    t_qpos = nc.dram_tensor("qpos", [128, OWN], F32, kind="ExternalInput").ap()
    t_cos_own = nc.dram_tensor("cos_own", [64, OWN], F32, kind="ExternalInput").ap()
    t_sin_own = nc.dram_tensor("sin_own", [64, OWN], F32, kind="ExternalInput").ap()
    t_cosf = nc.dram_tensor("cosf", [32, S], F32, kind="ExternalInput").ap()
    t_sinf = nc.dram_tensor("sinf", [32, S], F32, kind="ExternalInput").ap()
    t_kposc = nc.dram_tensor("kposc", [128, 64], F32, kind="ExternalInput").ap()
    t_sel = nc.dram_tensor("sel", [128, 256], F32, kind="ExternalInput").ap()
    t_consts = nc.dram_tensor("consts", [128, 4], F32, kind="ExternalInput").ap()
    t_wq = nc.dram_tensor("wq", [24, 128, 2048], F32, kind="ExternalInput").ap()
    t_wkva = nc.dram_tensor("wkva", [16, 128, 576], F32, kind="ExternalInput").ap()
    t_kvnw = nc.dram_tensor("kvnw", [128, 4], F32, kind="ExternalInput").ap()
    t_wkvb = nc.dram_tensor("wkvb", [4, 128, 4096], F32, kind="ExternalInput").ap()
    t_wo = nc.dram_tensor("wo", [16, 128, 2048], F32, kind="ExternalInput").ap()
    t_wg = nc.dram_tensor("wg", [64, 128, 16, 128], I8, kind="ExternalInput").ap()
    t_wu = nc.dram_tensor("wu", [64, 128, 16, 128], I8, kind="ExternalInput").ap()
    t_wd = nc.dram_tensor("wd", [16, 128, 64, 128], I8, kind="ExternalInput").ap()
    t_sg = nc.dram_tensor("sg", [128, 64], F32, kind="ExternalInput").ap()
    t_su = nc.dram_tensor("su", [128, 64], F32, kind="ExternalInput").ap()
    t_sd = nc.dram_tensor("sd", [128, 16], F32, kind="ExternalInput").ap()

    t_out = nc.dram_tensor("out", [OWN, D], F32, kind="ExternalOutput").ap()

    q_scr = nc.dram_tensor("q_scr", [3072, OWN], BF16, kind="Internal").ap()
    o_scr = nc.dram_tensor("o_scr", [4, 128, 16, 512], BF16, kind="Internal").ap()
    h1_scr = nc.dram_tensor("h1_scr", [D, OWN], F32, kind="Internal").ap()
    wo_bf = nc.dram_tensor("wo_bf", [16, 128, 2048], BF16, kind="Internal").ap()

    dbg_aps = {}
    def dbg_out(name, shape, dtype):
        if name in dbg:
            dbg_aps[name] = nc.dram_tensor(
                "dbg_" + name, shape, dtype, kind="ExternalOutput").ap()
            return dbg_aps[name]
        return None

    d_qscr = dbg_out("qscr", [3072, OWN], BF16)
    d_ckv = dbg_out("ckv", [128, 4 * S], BF16)
    d_krope = dbg_out("krope", [64, S], BF16)
    d_qmax = dbg_out("qmax", [1, 16], F32)
    d_krn = dbg_out("krn", [1, S], BF16)
    d_oscr = dbg_out("oscr", [4, 128, 16, 512], BF16)
    d_h1 = dbg_out("h1", [D, OWN], F32)
    d_kn15 = dbg_out("kn15", [128, S], BF16)
    d_vt15 = dbg_out("vt15", [128, S], BF16)

    with ExitStack() as g:
        P = Prog(nc, g)
        qmaxN = g.enter_context(nc.sbuf_tensor("qmaxN", [1, 16], F32))
        qmaxR = g.enter_context(nc.sbuf_tensor("qmaxR", [1, 16], F32))
        qsum_t = g.enter_context(nc.sbuf_tensor("qsum_t", [1, 1], F32))
        ones128 = g.enter_context(nc.sbuf_tensor("ones128", [128, 1], BF16))
        ones1 = g.enter_context(nc.sbuf_tensor("ones1", [1, 128], BF16))
        ones1f = g.enter_context(nc.sbuf_tensor("ones1f", [1, 128], F32))
        ones_mat = g.enter_context(nc.sbuf_tensor("ones_mat", [128, 128], BF16))
        consts_sb = g.enter_context(nc.sbuf_tensor("consts_sb", [128, 4], F32))

        P.phase()
        P.op("PL", lambda e: e.memset(ones128[:, :], 1.0), inc="plc")
        P.op("PL", lambda e: e.memset(ones1[:, :], 1.0), inc="plc")
        P.op("PL", lambda e: e.memset(ones1f[:, :], 1.0), inc="plc")
        P.op("PL", lambda e: e.memset(ones_mat[:, :], 1.0), inc="plc")
        P.op("PL", lambda e: e.memset(qmaxN[:, :], -1e30), inc="plc")
        P.op("PL", lambda e: e.memset(qmaxR[:, :], -1e30), inc="plc")
        t_c = P.dma("SP", consts_sb[:, :], t_consts)
        init_done = [("plc", P.cnt["plc"]), t_c]

        with ExitStack() as a:
            wqb = [a.enter_context(nc.sbuf_tensor(f"wqb{i}", [128, 2048], BF16)) for i in range(2)]
            xb = [a.enter_context(nc.sbuf_tensor(f"xbA{i}", [128, 16 * 512], BF16)) for i in range(4)]
            cosA = a.enter_context(nc.sbuf_tensor("cosA", [64, OWN], F32))
            sinA = a.enter_context(nc.sbuf_tensor("sinA", [64, OWN], F32))
            sel_sb = a.enter_context(nc.sbuf_tensor("sel_sb", [128, 256], BF16))
            qev = [a.enter_context(nc.sbuf_tensor(f"qev{i}", [128, 512], BF16)) for i in range(2)]
            qevr = [a.enter_context(nc.sbuf_tensor(f"qevr{i}", [128, 512], BF16)) for i in range(2)]
            sqb = [a.enter_context(nc.sbuf_tensor(f"sqbA{i}", [128, 512], BF16)) for i in range(4)]
            rm1 = a.enter_context(nc.sbuf_tensor("rm1", [64, 512], F32))
            rm2 = a.enter_context(nc.sbuf_tensor("rm2", [64, 512], F32))
            rm1p = a.enter_context(nc.sbuf_tensor("rm1p", [64, 512], F32))
            rm2p = a.enter_context(nc.sbuf_tensor("rm2p", [64, 512], F32))
            robE = [a.enter_context(nc.sbuf_tensor(f"robE{i}", [64, 512], F32)) for i in range(4)]
            robO = [a.enter_context(nc.sbuf_tensor(f"robO{i}", [64, 512], F32)) for i in range(4)]
            qm_t = a.enter_context(nc.sbuf_tensor("qm_t", [2, 1], F32))
            pq = [a.enter_context(nc.psum_tensor(f"pqA{i}", [128, 512], F32)) for i in range(4)]
            pn = [a.enter_context(nc.psum_tensor(f"pnA{i}", [128, 512], F32)) for i in range(2)]
            pnR = [a.enter_context(nc.psum_tensor(f"pnR{i}", [128, 512], F32)) for i in range(2)]

            def load_xb(tt):
                src = t_xq[:, 512 * tt:512 * (tt + 1)].rearrange(
                    "(c p) q -> p c q", p=128)
                dst = xb[tt][:, :].rearrange("p (c q) -> p c q", c=16)
                return [P.dma("PL", dst, src)]
            t_xb = {0: load_xb(0)}
            P.dma("SP", cosA[:, :], t_cos_own)
            t_trig = P.dma("SP", sinA[:, :], t_sin_own)
            t_sel = P.dma("PL", sel_sb[:, :], t_sel)
            t_wocast = [P.dma("PL", wo_bf[c], t_wo[c], cls="st") for c in range(16)]

            bank_free = {i: [] for i in range(4)}
            wqb_free = {0: [], 1: []}
            qev_free = {0: [], 1: []}
            qevr_free = {0: [], 1: []}
            pn_free = {0: [], 1: []}
            sq_free = {i: [] for i in range(4)}
            rope_buf_free = {i: [] for i in range(4)}
            t_stores = []
            pending_qn = []

            def flush_qn():
                nonlocal_state = None
                for (kind, ocb2, tt2, tok_sq2) in pending_qn:
                    par = tt2 % 2
                    spar = tt2
                    if kind == "nope":
                        h = ocb2
                        tok_n = P.op("PE", lambda e, par=par, spar=spar: e.matmul(
                            pn[par][:, :], ones_mat[:, :], sqb[spar][:, :],
                            start=True, stop=True),
                            waits=[("act", tok_sq2), ("plc", P.cnt["plc"])] + pn_free[par], inc="pe")
                        pn_free[par] = []
                        P.op("DVE", lambda e, par=par: e.reduce_max(
                            qm_t[0:1, :], pn[par][0:1, :], AX.X), waits=[("pe", tok_n)])
                        tok_mx = P.op("DVE", lambda e, h=h: e.tensor_max(
                            qmaxN[:, h:h + 1], qmaxN[:, h:h + 1], qm_t[0:1, :]), inc="dve")
                        pn_free[par] = [("dve", tok_mx)]
                    else:
                        r = ocb2 - 16
                        P.op("PE", lambda e, par=par, spar=spar: e.matmul(
                            pn[par][:, :], sel_sb[:, 0:128], sqb[spar][:, :],
                            start=True, stop=True),
                            waits=[("act", tok_sq2)] + pn_free[par])
                        tok_n = P.op("PE", lambda e, par=par, spar=spar: e.matmul(
                            pnR[par][:, :], sel_sb[:, 128:256], sqb[spar][:, :],
                            start=True, stop=True), inc="pe")
                        pn_free[par] = []
                        P.op("DVE", lambda e, par=par: e.reduce_max(
                            qm_t[0:1, :], pn[par][0:1, :], AX.X), waits=[("pe", tok_n)])
                        P.op("DVE", lambda e, r=r: e.tensor_max(
                            qmaxR[:, 2 * r:2 * r + 1], qmaxR[:, 2 * r:2 * r + 1], qm_t[0:1, :]))
                        P.op("DVE", lambda e, par=par: e.reduce_max(
                            qm_t[0:1, :], pnR[par][0:1, :], AX.X))
                        tok_mx = P.op("DVE", lambda e, r=r: e.tensor_max(
                            qmaxR[:, 2 * r + 1:2 * r + 2], qmaxR[:, 2 * r + 1:2 * r + 2], qm_t[0:1, :]), inc="dve")
                        pn_free[par] = [("dve", tok_mx)]
                    sq_free[spar] = [("pe", tok_n)]
                pending_qn.clear()

            for ocb in range(24):
                wpar = ocb % 2
                t_w = P.dma("PL", wqb[wpar][:, :], t_wq[ocb], waits=wqb_free[wpar])
                wqb_free[wpar] = []
                flush_qn()
                for tt in range(4):
                    bk = (4 * ocb + tt) % 4
                    par = tt % 2
                    for c in range(16):
                        waits = []
                        if c == 0:
                            waits = [t_w, t_xb[tt][-1]] + bank_free[bk] + ([t_sel] if ocb == 0 else [])
                            bank_free[bk] = []
                        P.op("PE", lambda e, c=c, bk=bk, tt=tt, wpar=wpar: e.matmul(
                            pq[bk][:, :], wqb[wpar][:, 128 * c:128 * (c + 1)],
                            xb[tt][:, 512 * c:512 * (c + 1)],
                            start=(c == 0), stop=(c == 15)),
                            waits=waits, inc="pe" if c == 15 else None)
                    tok_mm = P.cnt["pe"]
                    if ocb == 0 and tt + 1 < 4:
                        t_xb[tt + 1] = load_xb(tt + 1)
                    if ocb < 16:
                        h = ocb
                        tok_cp = P.op("ACT", lambda e, par=par, bk=bk: e.copy(
                            qev[par][:, :], pq[bk][:, :]),
                            waits=[("pe", tok_mm)] + qev_free[par], inc="act")
                        qev_free[par] = []
                        tok_sq = P.op("ACT", lambda e, tt=tt, bk=bk: e.activation(
                            sqb[tt][:, :], pq[bk][:, :], AF.Square),
                            waits=sq_free[tt], inc="act")
                        sq_free[tt] = []
                        bank_free[bk] = [("act", tok_sq)]
                        t_st = P.dma("SP", q_scr[128 * h:128 * (h + 1),
                                               512 * tt:512 * (tt + 1)],
                                     qev[par][:, :], waits=[("act", tok_cp)], cls="st")
                        qev_free[par] = [t_st]
                        t_stores.append(t_st)
                        pending_qn.append(("nope", ocb, tt, tok_sq))
                    else:
                        r = ocb - 16
                        tok_sq = P.op("ACT", lambda e, tt=tt, bk=bk: e.activation(
                            sqb[tt][:, :], pq[bk][:, :], AF.Square),
                            waits=[("pe", tok_mm)] + sq_free[tt], inc="act")
                        sq_free[tt] = []
                        pending_qn.append(("rope", ocb, tt, tok_sq))
                        P.op("ACT", lambda e, tt=tt, bk=bk: e.copy(
                            robE[tt][:, :], pq[bk][0:64, :]),
                            waits=rope_buf_free[tt])
                        tok_rc = P.op("ACT", lambda e, tt=tt, bk=bk: e.copy(
                            robO[tt][:, :], pq[bk][64:128, :]), inc="act")
                        rope_buf_free[tt] = []
                        bank_free[bk] = [("act", tok_rc)]
                        eng = "DVE" if tt % 2 == 0 else "PL"
                        m1 = rm1 if eng == "DVE" else rm1p
                        m2 = rm2 if eng == "DVE" else rm2p
                        cs = cosA[:, 512 * tt:512 * (tt + 1)]
                        sn = sinA[:, 512 * tt:512 * (tt + 1)]
                        e_ap = robE[tt][:, :]
                        o_ap = robO[tt][:, :]
                        wts = [("act", tok_rc), t_trig] + qevr_free[par]
                        qevr_free[par] = []
                        P.op(eng, lambda e, a1=e_ap, c1=cs, m1=m1: e.tensor_mul(m1[:, :], a1, c1), waits=wts)
                        P.op(eng, lambda e, a1=o_ap, s1=sn, m2=m2: e.tensor_mul(m2[:, :], a1, s1))
                        P.op(eng, lambda e, par=par, m1=m1, m2=m2: e.tensor_sub(qevr[par][0:64, :], m1[:, :], m2[:, :]))
                        P.op(eng, lambda e, a1=o_ap, c1=cs, m1=m1: e.tensor_mul(m1[:, :], a1, c1))
                        P.op(eng, lambda e, a1=e_ap, s1=sn, m2=m2: e.tensor_mul(m2[:, :], a1, s1))
                        tok_rope = P.op(eng, lambda e, par=par, m1=m1, m2=m2: e.tensor_add(
                            qevr[par][64:128, :], m1[:, :], m2[:, :]),
                            inc="dve" if eng == "DVE" else "plc")
                        rsem = "dve" if eng == "DVE" else "plc"
                        rope_buf_free[tt] = [(rsem, tok_rope)]
                        h0, h1 = 2 * r, 2 * r + 1
                        h0, h1 = 2 * r, 2 * r + 1
                        for (rows_s, rows_d) in (((0, 32), 2048 + 64 * h0),
                                                 ((32, 64), 2048 + 64 * h1),
                                                 ((64, 96), 2048 + 64 * h0 + 32),
                                                 ((96, 128), 2048 + 64 * h1 + 32)):
                            t_st = P.dma("SP",
                                         q_scr[rows_d:rows_d + 32, 512 * tt:512 * (tt + 1)],
                                         qevr[par][rows_s[0]:rows_s[1], :],
                                         waits=[(rsem, tok_rope)], cls="st")
                            t_stores.append(t_st)
                        qevr_free[par] = [t_stores[-1]]
                wqb_free[wpar] = [("pe", P.cnt["pe"])]
            flush_qn()
            P.op("SP", lambda e: e.nop(), waits=P.dma_all("SP") + P.dma_all("PL") + [("dve", P.cnt["dve"])])
            if d_qscr is not None:
                P.dma("SP", d_qscr, q_scr)
                P.op("SP", lambda e: e.nop(), waits=P.dma_all("SP") + P.dma_all("PL"))
            P.emit()


        if upto < 2:
            _final_wait(nc, P, t_out)
            return nc, dbg_aps

        with ExitStack() as bc:
            ckv_sb = bc.enter_context(nc.sbuf_tensor("ckv_sb", [128, 4 * S], BF16))
            krope_sb = bc.enter_context(nc.sbuf_tensor("krope_sb", [128, S], BF16))
            krn_sb = bc.enter_context(nc.sbuf_tensor("krn_sb", [1, S], BF16))

            P.phase()
            with ExitStack() as b:
                wkva_sb = b.enter_context(nc.sbuf_tensor("wkva_sb", [128, 16 * 576], BF16))
                kvnw_sb = b.enter_context(nc.sbuf_tensor("kvnw_sb", [128, 4], F32))
                xb2 = [b.enter_context(nc.sbuf_tensor(f"xbB{i}", [128, 16 * 512], BF16)) for i in range(2)]
                cosB = [b.enter_context(nc.sbuf_tensor(f"cosB{i}", [32, 512], F32)) for i in range(2)]
                sinB = [b.enter_context(nc.sbuf_tensor(f"sinB{i}", [32, 512], F32)) for i in range(2)]
                ckvraw = [b.enter_context(nc.sbuf_tensor(f"ckvraw{i}", [128, 4 * 512], F32)) for i in range(2)]
                sqB = [b.enter_context(nc.sbuf_tensor(f"sqB{i}", [128, 4 * 512], BF16)) for i in range(2)]
                sqRB = [b.enter_context(nc.sbuf_tensor(f"sqRB{i}", [64, 512], BF16)) for i in range(2)]
                rstdB = b.enter_context(nc.sbuf_tensor("rstdB", [128, 512], F32))
                rmB1 = b.enter_context(nc.sbuf_tensor("rmB1", [32, 512], F32))
                rmB2 = b.enter_context(nc.sbuf_tensor("rmB2", [32, 512], F32))
                pkv = [b.enter_context(nc.psum_tensor(f"pkvB{i}", [128, 512], F32)) for i in range(3)]
                prope = b.enter_context(nc.psum_tensor("propeB", [64, 512], F32))
                pss = b.enter_context(nc.psum_tensor("pssB", [128, 512], F32))
                pkr = b.enter_context(nc.psum_tensor("pkrB", [128, 512], F32))


                t_wk_ld = [P.dma("PL", wkva_sb[:, 576 * c:576 * (c + 1)], t_wkva[c])
                           for c in range(16)]
                t_kvnw = P.dma("SP", kvnw_sb[:, :], t_kvnw)

                bank_free = {0: [], 1: [], 2: []}
                rope_free = []
                ss_free = []
                rstd_free = []
                kr_free = []
                raw_free = {0: [], 1: []}
                sq_freeB = {0: [], 1: []}
                xb_free = {0: [], 1: []}
                cos_free = {0: [], 1: []}
                pending_B = None

                def flush_B(pend):
                    nonlocal ss_free, rstd_free, kr_free, sq_freeB, raw_free
                    tt2 = pend["tt"]
                    rp = tt2 % 2
                    for oc in range(4):
                        P.op("PE", lambda e, oc=oc, rp=rp: e.matmul(
                            pss[:, :], ones_mat[:, :], sqB[rp][:, 512 * oc:512 * (oc + 1)],
                            start=(oc == 0), stop=(oc == 3)),
                            waits=([("act", pend["sq3"])] + ss_free) if oc == 0 else [],
                            inc="pe" if oc == 3 else None)
                    tok_ss = P.cnt["pe"]
                    ss_free = []
                    tok_kr = P.op("PE", lambda e, rp=rp: e.matmul(
                        pkr[:, :], ones_mat[0:64, :], sqRB[rp][:, :],
                        start=True, stop=True), waits=[("act", pend["sqr"])] + kr_free, inc="pe")
                    tok_krc = P.op("DVE", lambda e, tt2=tt2: e.tensor_copy(
                        krn_sb[:, 512 * tt2:512 * (tt2 + 1)], pkr[0:1, :]),
                        waits=[("pe", tok_kr)], inc="dve")
                    kr_free = [("dve", tok_krc)]
                    tok_ln = P.op("ACT", lambda e: e.activation(
                        rstdB[:, :], pss[:, :], AF.Ln,
                        bias=consts_sb[:, 0:1], scale=1.0 / 512.0),
                        waits=[("pe", tok_ss), t_c] + rstd_free)
                    rstd_free = []
                    tok_ex = P.op("ACT", lambda e: e.activation(
                        rstdB[:, :], rstdB[:, :], AF.Exp, scale=-0.5), inc="act")
                    ss_free = [("act", tok_ex)]
                    for oc in range(4):
                        P.op("DVE", lambda e, oc=oc, tt2=tt2, rp=rp: e.scalar_tensor_tensor(
                            ckv_sb[:, S * oc + 512 * tt2: S * oc + 512 * (tt2 + 1)],
                            ckvraw[rp][:, 512 * oc:512 * (oc + 1)],
                            kvnw_sb[:, oc:oc + 1], rstdB[:, :],
                            ALU.mult, ALU.mult),
                            waits=[("act", tok_ex), t_kvnw] if oc == 0 else [],
                            inc="dve")
                    ss_free = ss_free + [("dve", P.cnt["dve"])]
                    rstd_free = [("dve", P.cnt["dve"])]
                    sq_freeB[rp] = [("pe", tok_ss)]
                    raw_free[rp] = [("dve", P.cnt["dve"])]

                for tt in range(16):
                    par_x = tt % 2
                    t_x = [P.dma("PL", xb2[par_x][:, :].rearrange("p (c q) -> p c q", c=16),
                                 t_xf[:, 512 * tt:512 * (tt + 1)].rearrange(
                                     "(c p) q -> p c q", p=128),
                                 waits=xb_free[par_x])]
                    xb_free[par_x] = []
                    t_cs = P.dma("SP", cosB[par_x][:, :],
                                 t_cosf[:, 512 * tt:512 * (tt + 1)], waits=cos_free[par_x])
                    t_sn = P.dma("SP", sinB[par_x][:, :],
                                 t_sinf[:, 512 * tt:512 * (tt + 1)])
                    cos_free[par_x] = []
                    mm_tok = {}
                    for oc in range(5):
                        par = oc % 3
                        if oc < 4:
                            dst, lo, w = (pkv[par], 128 * oc, 128)
                        else:
                            dst, lo, w = (prope, 512, 64)
                        def mk(c, dst=dst, lo=lo, w=w, par_x=par_x):
                            def f(e):
                                return e.matmul(
                                    dst[:, :],
                                    wkva_sb[:, 576 * c + lo:576 * c + lo + w],
                                    xb2[par_x][:, 512 * c:512 * (c + 1)],
                                    start=(c == 0), stop=(c == 15))
                            return f
                        for c in range(16):
                            waits = []
                            if c == 0:
                                waits = [t_x[-1], t_wk_ld[15]]
                                waits += (bank_free[par] if oc < 4 else rope_free)
                                if oc < 4:
                                    bank_free[par] = []
                                else:
                                    rope_free = []
                            P.op("PE", mk(c), waits=waits, inc="pe" if c == 15 else None)
                        mm_tok[oc] = P.cnt["pe"]
                        if oc < 4:
                            tok_cp = P.op("ACT", lambda e, par=par, oc=oc, rp=tt % 2: e.copy(
                                ckvraw[rp][:, 512 * oc:512 * (oc + 1)], pkv[par][:, :]),
                                waits=[("pe", mm_tok[oc])] + (raw_free[tt % 2] if oc == 0 else []), inc="act")
                            if oc == 0:
                                raw_free[tt % 2] = []
                            tok_sq = P.op("ACT", lambda e, par=par, oc=oc, rp=tt % 2: e.activation(
                                sqB[rp][:, 512 * oc:512 * (oc + 1)], pkv[par][:, :], AF.Square),
                                waits=(sq_freeB[tt % 2] if oc == 0 else []), inc="act")
                            if oc == 0:
                                sq_freeB[tt % 2] = []
                            bank_free[par] = [("act", tok_sq)]
                            mm_tok["sq", oc] = tok_sq
                        else:
                            tok_sqr = P.op("ACT", lambda e, rp=tt % 2: e.activation(
                                sqRB[rp][:, :], prope[:, :], AF.Square),
                                waits=[("pe", mm_tok[oc])] + kr_free, inc="act")
                            kr_free = []
                            mm_tok["sqr"] = tok_sqr
                            wts = [("pe", mm_tok[oc]), ("act", tok_sqr), t_sn, t_cs]
                            P.op("DVE", lambda e, p=par_x: e.tensor_mul(rmB1[:, :], prope[0:32, :], cosB[p][:, :]), waits=wts)
                            P.op("DVE", lambda e, p=par_x: e.tensor_mul(rmB2[:, :], prope[32:64, :], sinB[p][:, :]))
                            P.op("DVE", lambda e, tt=tt: e.tensor_sub(
                                krope_sb[0:32, 512 * tt:512 * (tt + 1)], rmB1[:, :], rmB2[:, :]))
                            P.op("DVE", lambda e, p=par_x: e.tensor_mul(rmB1[:, :], prope[32:64, :], cosB[p][:, :]))
                            P.op("DVE", lambda e, p=par_x: e.tensor_mul(rmB2[:, :], prope[0:32, :], sinB[p][:, :]))
                            tok_rope = P.op("DVE", lambda e, tt=tt: e.tensor_add(
                                krope_sb[32:64, 512 * tt:512 * (tt + 1)], rmB1[:, :], rmB2[:, :]), inc="dve")
                            cos_free[par_x] = [("dve", tok_rope)]
                            rope_free = [("dve", tok_rope), ("act", tok_sqr)]
                    if pending_B is not None:
                        flush_B(pending_B)
                    pending_B = dict(tt=tt, sq3=mm_tok["sq", 3], sqr=mm_tok["sqr"])
                    xb_free[par_x] = [("pe", mm_tok[4])]
                flush_B(pending_B)
                P.dma("SP", krope_sb[64:128, :], krope_sb[0:64, :],
                      waits=[("dve", P.cnt["dve"])], cls="st")
                P.op("SP", lambda e: e.nop(),
                     waits=P.dma_all("SP") + P.dma_all("PL") + [("dve", P.cnt["dve"])])
                if d_ckv is not None:
                    tk = P.op("DVE", lambda e: e.nop(), waits=[("dve", P.cnt["dve"])], inc="dve")
                    P.dma("SP", d_ckv, ckv_sb[:, :], waits=[("dve", tk)])
                    P.dma("SP", d_krope, krope_sb[:, :])
                    P.dma("SP", d_krn, krn_sb[:, :])
                    P.dma("SP", d_qmax, qmaxN[:, :])
                    P.op("SP", lambda e: e.nop(), waits=P.dma_all("SP") + P.dma_all("PL"))
                P.emit()

            if upto < 3:
                _final_wait(nc, P, t_out)
                return nc, dbg_aps

            P.phase()
            with ExitStack() as cph:
                wkvb_sb = cph.enter_context(nc.sbuf_tensor("wkvb_sb", [128, 4 * 4096], BF16))
                qpos_sb = cph.enter_context(nc.sbuf_tensor("qpos_sb", [128, OWN], F32))
                kposc_sb = cph.enter_context(nc.sbuf_tensor("kposc_sb", [128, 64], F32))
                knope_sb = cph.enter_context(nc.sbuf_tensor("knope_sb", [128, S], BF16))
                vtm_sb = cph.enter_context(nc.sbuf_tensor("vtm_sb", [128, S], BF16))
                qn_sb = [cph.enter_context(nc.sbuf_tensor(f"qn_sb{i}", [128, 512], BF16)) for i in range(2)]
                qr_sb = [cph.enter_context(nc.sbuf_tensor(f"qr_sb{i}", [128, 512], BF16)) for i in range(2)]
                ptb = [cph.enter_context(nc.sbuf_tensor(f"ptb{i}", [128, 512], BF16)) for i in range(4)]
                sqCs = cph.enter_context(nc.sbuf_tensor("sqCs", [128, 16 * 512], BF16))
                km_t = cph.enter_context(nc.sbuf_tensor("km_t", [1, 1], F32))
                kmax = cph.enter_context(nc.sbuf_tensor("kmax", [1, 1], F32))
                bias_sb = cph.enter_context(nc.sbuf_tensor("bias_sb", [128, 1], F32))
                bias_t = cph.enter_context(nc.sbuf_tensor("bias_t", [1, 1], F32))
                den_row = cph.enter_context(nc.sbuf_tensor("den_row", [1, 512], F32))
                dfull = cph.enter_context(nc.sbuf_tensor("dfull", [128, 512], F32))
                rrep = cph.enter_context(nc.sbuf_tensor("rrep", [128, 512], F32))
                ob_sb = [cph.enter_context(nc.sbuf_tensor(f"ob_sb{i}", [128, 512], BF16)) for i in range(2)]
                pool = [cph.enter_context(nc.psum_tensor(f"poolC{i}", [128, 512], F32)) for i in range(4)]
                po = [cph.enter_context(nc.psum_tensor(f"poC{i}", [128, 512], F32)) for i in range(2)]
                pden = cph.enter_context(nc.psum_tensor("pdenC", [128, 512], F32))
                pmisc = cph.enter_context(nc.psum_tensor("pmiscC", [128, 512], F32))

                t_wb = [P.dma("PL", wkvb_sb[:, 4096 * c:4096 * (c + 1)], t_wkvb[c])
                        for c in range(4)]
                t_qp = P.dma("SP", qpos_sb[:, :], t_qpos)
                t_kp = P.dma("SP", kposc_sb[:, :], t_kposc)

                NG = 64
                def q_load(gi, waits):
                    hh, gg = divmod(gi, 4)
                    par = gi % 2
                    tn = P.dma("SP", qn_sb[par][:, :],
                               q_scr[128 * hh:128 * (hh + 1), 512 * gg:512 * (gg + 1)],
                               waits=waits)
                    P.dma("SP", qr_sb[par][0:64, :],
                          q_scr[2048 + 64 * hh:2048 + 64 * (hh + 1), 512 * gg:512 * (gg + 1)])
                    tr = P.dma("SP", qr_sb[par][64:128, :],
                               q_scr[2048 + 64 * hh:2048 + 64 * (hh + 1), 512 * gg:512 * (gg + 1)])
                    return tr

                pool_free = {i: [] for i in range(4)}
                pt_free = {i: [] for i in range(4)}
                po_free = {0: [], 1: []}
                pden_free = []
                pmisc_free = []
                pkn_free = {0: [], 1: []}
                ob_free = {0: [], 1: []}
                kn_free = []
                vt_free = []
                knC_free = []
                sqs_free = []
                q_loaded = {}
                q_loaded[0] = q_load(0, [])
                for h in range(16):
                    P.op("DVE", lambda e: e.memset(kmax[:, :], -1e30), waits=knC_free)
                    knC_free = []
                    kn_cp = None
                    sq_toks = {}
                    for tt in range(16):
                        bk = tt % 4
                        for cc in range(4):
                            waits = []
                            if cc == 0:
                                waits = list(pool_free[bk]) + ([t_wb[3]] if (h == 0 and tt == 0) else [])
                                waits += kn_free
                                kn_free = []
                                pool_free[bk] = []
                            P.op("PE", lambda e, cc=cc, tt=tt, bk=bk, h=h: e.matmul(
                                pool[bk][:, :],
                                wkvb_sb[:, 4096 * cc + 256 * h:4096 * cc + 256 * h + 128],
                                ckv_sb[:, S * cc + 512 * tt:S * cc + 512 * (tt + 1)],
                                start=(cc == 0), stop=(cc == 3)),
                                waits=waits, inc="pe" if cc == 3 else None)
                        tok_mm = P.cnt["pe"]
                        kn_cp = P.op("ACT", lambda e, bk=bk, tt=tt: e.copy(
                            knope_sb[:, 512 * tt:512 * (tt + 1)], pool[bk][:, :]),
                            waits=[("pe", tok_mm)], inc="act")
                        pool_free[bk] = [("act", kn_cp)]
                        sq_toks[tt] = P.op("DVE", lambda e, tt=tt: e.tensor_mul(
                            sqCs[:, 512 * tt:512 * (tt + 1)],
                            knope_sb[:, 512 * tt:512 * (tt + 1)],
                            knope_sb[:, 512 * tt:512 * (tt + 1)]),
                            waits=[("act", kn_cp)] + (sqs_free if tt == 0 else []), inc="dve")
                        if tt == 0:
                            sqs_free = []
                    for tt in range(16):
                        knb = tt % 2
                        knbank = pmisc if knb == 0 else pden
                        knw = [("dve", sq_toks[tt])] + pkn_free[knb]
                        if knb == 0:
                            knw += pmisc_free
                            pmisc_free = []
                        else:
                            knw += pden_free
                            pden_free = []
                        tok_kn = P.op("PE", lambda e, knbank=knbank, tt=tt: e.matmul(
                            knbank[:, :], ones_mat[:, :], sqCs[:, 512 * tt:512 * (tt + 1)],
                            start=True, stop=True),
                            waits=knw, inc="pe")
                        pkn_free[knb] = []
                        tok_red = P.op("DVE", lambda e, knbank=knbank, tt=tt: e.tensor_add(
                            den_row[:, :], knbank[0:1, :],
                            krn_sb[:, 512 * tt:512 * (tt + 1)]),
                            waits=[("pe", tok_kn)], inc="dve")
                        pkn_free[knb] = [("dve", tok_red)]
                        P.op("DVE", lambda e: e.reduce_max(km_t[:, :], den_row[:, :], AX.X))
                        P.op("DVE", lambda e: e.tensor_max(kmax[:, :], kmax[:, :], km_t[:, :]))
                    sqs_free = [("pe", P.cnt["pe"])]
                    tok_kmax = P.cnt["dve"]
                    knC_free = [("dve", tok_kmax)]
                    P.op("DVE", lambda e, h=h: e.tensor_add(
                        qsum_t[:, :], qmaxN[:, h:h + 1], qmaxR[:, h:h + 1]),
                        waits=[("dve", tok_kmax)])
                    tok_b0 = P.op("DVE", lambda e: e.tensor_mul(
                        km_t[:, :], qsum_t[:, :], kmax[:, :]), inc="dve")
                    P.op("ACT", lambda e: e.activation(bias_t[:, :], km_t[:, :], AF.Ln),
                         waits=[("dve", tok_b0)])
                    tok_b1 = P.op("ACT", lambda e: e.activation(
                        bias_t[:, :], bias_t[:, :], AF.Exp, scale=0.5), inc="act")
                    tok_b2 = P.op("DVE", lambda e: e.tensor_scalar_mul(
                        bias_t[:, :], bias_t[:, :], -SSCALE * BOUND_MARGIN),
                        waits=[("act", tok_b1)], inc="dve")
                    vt_cp = None
                    for tg in range(16):
                        bk = tg % 4
                        for kt4 in range(4):
                            kt = 4 * tg + kt4
                            for cc in range(4):
                                waits = []
                                if cc == 0 and kt4 == 0:
                                    waits = list(pool_free[bk]) + vt_free
                                    vt_free = []
                                    pool_free[bk] = []
                                P.op("PE", lambda e, cc=cc, kt=kt, kt4=kt4, bk=bk, h=h: e.matmul(
                                    pool[bk][:, 128 * kt4:128 * (kt4 + 1)],
                                    ckv_sb[:, S * cc + 128 * kt:S * cc + 128 * (kt + 1)],
                                    wkvb_sb[:, 4096 * cc + 256 * h + 128:4096 * cc + 256 * h + 256],
                                    start=(cc == 0), stop=(cc == 3)),
                                    waits=waits, inc="pe" if (cc == 3 and kt4 == 3) else None)
                        tok_mm = P.cnt["pe"]
                        vt_cp = P.op("ACT", lambda e, bk=bk, tg=tg: e.copy(
                            vtm_sb[:, 512 * tg:512 * (tg + 1)], pool[bk][:, :]),
                            waits=[("pe", tok_mm)], inc="act")
                        pool_free[bk] = [("act", vt_cp)]
                    tok_b3 = P.op("PE", lambda e: e.matmul(
                        pmisc[:, 0:1], ones1f[:, :], bias_t[:, :], start=True, stop=True),
                        waits=[("dve", tok_b2)] + pmisc_free + pkn_free[0], inc="pe")
                    pmisc_free = []
                    pkn_free[0] = []
                    tok_bias = P.op("DVE", lambda e: e.tensor_copy(
                        bias_sb[:, :], pmisc[:, 0:1]), waits=[("pe", tok_b3)], inc="dve")
                    pmisc_free = [("dve", tok_bias)]
                    for gq in range(4):
                        gi = 4 * h + gq
                        niter = 16 * (gq + 1)
                        t_q = q_loaded[gi]
                        par_g = gq % 2
                        s_tok = {}
                        den_tok = {}
                        pt_tok = {}

                        def s_mm_pair(p):
                            it0, it1 = 2 * p, 2 * p + 1
                            bk0, bk1 = it0 % 4, it1 % 4
                            waits = []
                            if it0 < 4:
                                waits += pool_free[bk0]
                            pool_free[bk0] = []
                            if it0 == 0:
                                waits += [t_q, ("act", kn_cp)]
                            P.op("PE", lambda e, it=it0, bk=bk0, par_g=par_g: e.matmul(
                                pool[bk][:, :], knope_sb[:, 128 * it:128 * (it + 1)],
                                qn_sb[par_g][:, :], start=True, stop=False), waits=waits)
                            waits1 = []
                            if it1 < 4:
                                waits1 += pool_free[bk1]
                            pool_free[bk1] = []
                            P.op("PE", lambda e, it=it1, bk=bk1, par_g=par_g: e.matmul(
                                pool[bk][:, :], knope_sb[:, 128 * it:128 * (it + 1)],
                                qn_sb[par_g][:, :], start=True, stop=False), waits=waits1)
                            P.op("PE", lambda e, it=it0, bk=bk0, par_g=par_g: e.matmul(
                                pool[bk][:, :], krope_sb[0:64, 128 * it:128 * (it + 1)],
                                qr_sb[par_g][0:64, :], start=False, stop=True,
                                tile_position=(0, 0)))
                            tok = P.op("PE", lambda e, it=it1, bk=bk1, par_g=par_g: e.matmul(
                                pool[bk][:, :], krope_sb[64:128, 128 * it:128 * (it + 1)],
                                qr_sb[par_g][64:128, :], start=False, stop=True,
                                tile_position=(64, 0)), inc="pe")
                            s_tok[it0] = tok
                            s_tok[it1] = tok

                        def s_exp(it, pe_tok):
                            bk = it % 4
                            tok_e = P.op("ACT", lambda e, bk=bk: e.activation(
                                ptb[bk][:, :], pool[bk][:, :], AF.Exp,
                                bias=bias_sb[:, 0:1], scale=SSCALE),
                                waits=[("pe", pe_tok), ("dve", tok_bias)] + pt_free[bk], inc="act")
                            pt_free[bk] = []
                            pool_free[bk] = [("act", tok_e)]
                            if it >= 16 * gq:
                                tok_m = P.op("DVE", lambda e, it=it, bk=bk, gq=gq: e.scalar_tensor_tensor(
                                    ptb[bk][:, :], qpos_sb[:, 512 * gq:512 * (gq + 1)],
                                    kposc_sb[:, it:it + 1], ptb[bk][:, :],
                                    ALU.is_ge, ALU.mult),
                                    waits=[("act", tok_e), t_qp, t_kp], inc="dve")
                                pt_tok[it] = ("dve", tok_m)
                            else:
                                pt_tok[it] = ("act", tok_e)

                        for p in range(2):
                            s_mm_pair(p)
                            s_exp(2 * p, s_tok[2 * p])
                            s_exp(2 * p + 1, s_tok[2 * p + 1])
                        if gi + 1 < NG:
                            q_loaded[gi + 1] = q_load(gi + 1, [("pe", P.cnt["pe"])])
                        pend_bk = None
                        for it in range(niter):
                            bk = it % 4
                            waits_pv = [pt_tok[it]]
                            if it == 0:
                                waits_pv += po_free[par_g]
                                po_free[par_g] = []
                            P.op("PE", lambda e, it=it, bk=bk, par_g=par_g, niter=niter: e.matmul(
                                po[par_g][:, :], vtm_sb[:, 128 * it:128 * (it + 1)],
                                ptb[bk][:, :], start=(it == 0), stop=(it == niter - 1)),
                                waits=waits_pv)
                            den_w = []
                            if it == 0:
                                den_w = pden_free + pkn_free[1]
                                pden_free = []
                                pkn_free[1] = []
                            do_inc = (it % 2 == 1) or (it == niter - 1)
                            tok_pv = P.op("PE", lambda e, it=it, bk=bk, niter=niter: e.matmul(
                                pden[:, :], ones_mat[:, :], ptb[bk][:, :],
                                start=(it == 0), stop=(it == niter - 1)),
                                waits=den_w, inc="pe" if do_inc else None)
                            den_tok[it] = tok_pv
                            if do_inc:
                                pt_free[bk] = [("pe", tok_pv)]
                                if pend_bk is not None:
                                    pt_free[pend_bk] = [("pe", tok_pv)]
                                    pend_bk = None
                            else:
                                pend_bk = bk
                            if it % 2 == 1 and it + 3 < niter:
                                s_mm_pair((it + 3) // 2)
                                s_exp(it + 3, s_tok[it + 3])
                                s_exp(it + 4, s_tok[it + 4])
                        tok_dc = P.op("DVE", lambda e: e.tensor_copy(
                            dfull[:, :], pden[:, :]), waits=[("pe", P.cnt["pe"])], inc="dve")
                        pden_free = [("dve", tok_dc)]
                        tok_r = P.op("DVE", lambda e: e.reciprocal(
                            rrep[:, :], dfull[:, :]), inc="dve")
                        tok_ob = P.op("DVE", lambda e, par_g=par_g: e.tensor_mul(
                            ob_sb[par_g][:, :], po[par_g][:, :], rrep[:, :]),
                            waits=ob_free[par_g], inc="dve")
                        ob_free[par_g] = []
                        po_free[par_g] = [("dve", tok_ob)]
                        t_ost = P.dma("SP", o_scr[gq, :, h, :],
                                      ob_sb[par_g][:, :], waits=[("dve", tok_ob)], cls="st")
                        ob_free[par_g] = [t_ost]
                    kn_free = [("pe", P.cnt["pe"])]
                    vt_free = [("pe", P.cnt["pe"])]
                    if h == 15 and d_kn15 is not None:
                        P.dma("SP", d_kn15, knope_sb[:, :], waits=[("pe", P.cnt["pe"])])
                        P.dma("SP", d_vt15, vtm_sb[:, :])
                P.op("SP", lambda e: e.nop(), waits=P.dma_all("SP") + P.dma_all("PL") + [("dve", P.cnt["dve"])])
                if d_oscr is not None:
                    P.dma("SP", d_oscr, o_scr)
                    P.op("SP", lambda e: e.nop(), waits=P.dma_all("SP") + P.dma_all("PL"))
                P.emit()

        if upto < 4:
            _final_wait(nc, P, t_out)
            return nc, dbg_aps

        P.phase()
        with ExitStack() as dph:
            wob = [dph.enter_context(nc.sbuf_tensor(f"wob{i}", [128, 2048], BF16)) for i in range(2)]
            osb = [dph.enter_context(nc.sbuf_tensor(f"osb{i}", [128, 16 * 512], BF16)) for i in range(2)]
            xtile = [dph.enter_context(nc.sbuf_tensor(f"xtile{i}", [128, 512], F32)) for i in range(2)]
            r1 = [dph.enter_context(nc.sbuf_tensor(f"r1_{i}", [128, 16 * 512], F32)) for i in range(2)]
            r1b = dph.enter_context(nc.sbuf_tensor("r1b", [128, 16 * 512], BF16))
            sqD = dph.enter_context(nc.sbuf_tensor("sqD", [128, 16 * 512], BF16))
            m2rep = dph.enter_context(nc.sbuf_tensor("m2rep", [128, 512], F32))
            mrep = [dph.enter_context(nc.sbuf_tensor(f"mrep{i}", [128, 512], F32)) for i in range(2)]
            rrepD = [dph.enter_context(nc.sbuf_tensor(f"rrepD{i}", [128, 512], F32)) for i in range(2)]
            h1t = dph.enter_context(nc.sbuf_tensor("h1t", [128, 16 * 512], F32))
            tmpD = [dph.enter_context(nc.sbuf_tensor(f"tmpD{i}", [128, 512], F32)) for i in range(2)]
            pwo = [dph.enter_context(nc.psum_tensor(f"pwo{i}", [128, 512], F32)) for i in range(4)]
            pmD = dph.enter_context(nc.psum_tensor("pmD", [128, 512], F32))
            pvD = dph.enter_context(nc.psum_tensor("pvD", [128, 512], F32))


            bank_free = {i: [] for i in range(4)}
            wob_free = {0: [], 1: []}
            osb_free = {0: [], 1: []}
            r1b_free = []
            r1_free = {0: [], 1: []}
            bc_free = {0: [], 1: []}
            h1_free = []
            h1_stores = []
            pending_D = []

            def flush_D(pend):
                nonlocal r1_free, bc_free, h1_free
                gq2, opar2, tok_cr2 = pend["gq"], pend["opar"], pend["tok_cr"]
                last = {"DVE": None, "PL": None}
                for oc in range(16):
                    eng = "DVE" if oc % 2 == 0 else "PL"
                    P.op(eng, lambda e, oc=oc, opar2=opar2: e.tensor_sub(
                        tmpD[oc % 2][:, :],
                        r1[opar2][:, 512 * oc:512 * (oc + 1)], mrep[opar2][:, :]),
                        waits=([("act", tok_cr2)] if oc < 2 else []) + (h1_free if oc < 2 else []))
                    last[eng] = P.op(eng, lambda e, oc=oc, opar2=opar2: e.tensor_mul(
                        h1t[:, 512 * oc:512 * (oc + 1)], tmpD[oc % 2][:, :],
                        rrepD[opar2][:, :]),
                        inc="dve" if eng == "DVE" else "plc")
                h1_free = []
                toks = [("dve", last["DVE"]), ("plc", last["PL"])]
                st = None
                for oc in range(16):
                    st = P.dma("SP", h1_scr[128 * oc:128 * (oc + 1), 512 * gq2:512 * (gq2 + 1)],
                               h1t[:, 512 * oc:512 * (oc + 1)],
                               waits=toks if oc == 0 else (), cls="st")
                    h1_stores.append(st)
                h1_free = [st]
                r1_free[opar2] = toks
                bc_free[opar2] = toks

            t_o = {}
            t_o[0] = [P.dma("SP", osb[0][:, :], o_scr[0])]
            xt_free = {0: [], 1: []}
            o_free = []
            stat_free = []
            for gq in range(4):
                opar = gq % 2
                if gq + 1 < 4:
                    t_o[gq + 1] = [P.dma("SP", osb[(gq + 1) % 2][:, :],
                                         o_scr[gq + 1], waits=osb_free[(gq + 1) % 2])]
                    osb_free[(gq + 1) % 2] = []
                mm_toks = {}
                for oc in range(16):
                    bk = oc % 4
                    par = oc % 2
                    wpar = oc % 2
                    t_w = P.dma("PL", wob[wpar][:, :], wo_bf[oc], waits=wob_free[wpar])
                    wob_free[wpar] = []
                    t_x = P.dma("PL", xtile[par][:, :],
                                t_xq[128 * oc:128 * (oc + 1), 512 * gq:512 * (gq + 1)],
                                waits=xt_free[par])
                    xt_free[par] = []
                    for c in range(16):
                        waits = []
                        if c == 0:
                            waits = [t_o[gq][-1], t_w] + bank_free[bk]
                            bank_free[bk] = []
                        P.op("PE", lambda e, c=c, oc=oc, bk=bk, wpar=wpar, opar=opar: e.matmul(
                            pwo[bk][:, :], wob[wpar][:, 128 * c:128 * (c + 1)],
                            osb[opar][:, 512 * c:512 * (c + 1)],
                            start=(c == 0), stop=(c == 15)),
                            waits=waits, inc="pe" if c == 15 else None)
                    mm_toks[oc] = P.cnt["pe"]
                    wob_free[wpar] = [("pe", mm_toks[oc])]
                    tok_r1 = P.op("DVE", lambda e, oc=oc, bk=bk, par=par, opar=opar: e.tensor_add(
                        r1[opar][:, 512 * oc:512 * (oc + 1)], pwo[bk][:, :], xtile[par][:, :]),
                        waits=[("pe", mm_toks[oc]), t_x] + (r1_free[opar] if oc == 0 else []), inc="dve")
                    if oc == 0:
                        r1_free[opar] = []
                    bank_free[bk] = [("dve", tok_r1)]
                    xt_free[par] = [("dve", tok_r1)]
                    tok_cb = P.op("ACT", lambda e, oc=oc, opar=opar: e.copy(
                        r1b[:, 512 * oc:512 * (oc + 1)], r1[opar][:, 512 * oc:512 * (oc + 1)]),
                        waits=[("dve", tok_r1)] + (r1b_free if oc == 0 else []), inc="act")
                    if oc == 0:
                        r1b_free = []
                    tok_sq = P.op("ACT", lambda e, oc=oc: e.activation(
                        sqD[:, 512 * oc:512 * (oc + 1)],
                        r1b[:, 512 * oc:512 * (oc + 1)], AF.Square), inc="act")
                    sq_tok_last = tok_sq
                for oc in range(16):
                    P.op("PE", lambda e, oc=oc: e.matmul(
                        pmD[:, :], ones_mat[:, :], r1b[:, 512 * oc:512 * (oc + 1)],
                        start=(oc == 0), stop=(oc == 15)),
                        waits=([("act", sq_tok_last)] + stat_free) if oc == 0 else [],
                        inc="pe" if oc == 15 else None)
                stat_free = []
                for oc in range(16):
                    P.op("PE", lambda e, oc=oc: e.matmul(
                        pvD[:, :], ones_mat[:, :], sqD[:, 512 * oc:512 * (oc + 1)],
                        start=(oc == 0), stop=(oc == 15)),
                        inc="pe" if oc == 15 else None)
                r1b_free = [("pe", P.cnt["pe"])]
                tok_stats = P.cnt["pe"]
                P.op("ACT", lambda e, opar=opar: e.activation(
                    mrep[opar][:, :], pmD[:, :], AF.Copy, scale=1.0 / 2048.0),
                    waits=[("pe", tok_stats)] + bc_free[opar])
                bc_free[opar] = []
                P.op("ACT", lambda e, opar=opar: e.activation(
                    rrepD[opar][:, :], pvD[:, :], AF.Copy, scale=1.0 / 2048.0))
                tok_m2 = P.op("ACT", lambda e, opar=opar: e.activation(
                    m2rep[:, :], mrep[opar][:, :], AF.Square), inc="act")
                stat_free = [("act", tok_m2)]
                tok_var = P.op("DVE", lambda e, opar=opar: e.tensor_sub(
                    rrepD[opar][:, :], rrepD[opar][:, :], m2rep[:, :]), waits=[("act", tok_m2)], inc="dve")
                P.op("ACT", lambda e, opar=opar: e.activation(
                    rrepD[opar][:, :], rrepD[opar][:, :], AF.Ln, bias=consts_sb[:, 1:2]),
                    waits=[("dve", tok_var)])
                tok_cr = P.op("ACT", lambda e, opar=opar: e.activation(
                    rrepD[opar][:, :], rrepD[opar][:, :], AF.Exp, scale=-0.5), inc="act")
                pending_D.append(dict(gq=gq, opar=opar, tok_cr=tok_cr))
                if len(pending_D) > 1:
                    flush_D(pending_D.pop(0))
                osb_free[opar] = [("pe", mm_toks[15])]
            flush_D(pending_D.pop(0))
            P.op("SP", lambda e: e.nop(), waits=P.dma_all("SP") + P.dma_all("PL") + P.dma_all("ACT") + [("dve", P.cnt["dve"])])
            if d_h1 is not None:
                P.dma("SP", d_h1, h1_scr)
                P.op("SP", lambda e: e.nop(), waits=P.dma_all("SP") + P.dma_all("PL"))
            P.emit()

        if upto < 5:
            _final_wait(nc, P, t_out)
            return nc, dbg_aps

        P.phase()
        with ExitStack() as eph:
            h1b = eph.enter_context(nc.sbuf_tensor("h1b", [128, 16 * 512], BF16))
            fuse = eph.enter_context(nc.sbuf_tensor("fuse", [128, 64 * 512], BF16))
            wgb = [eph.enter_context(nc.sbuf_tensor(f"wgb{i}", [128, 2048], BF16)) for i in range(2)]
            wub = [eph.enter_context(nc.sbuf_tensor(f"wub{i}", [128, 2048], BF16)) for i in range(2)]
            wdb = [eph.enter_context(nc.sbuf_tensor(f"wdb{i}", [128, 8192], BF16)) for i in range(2)]
            sg_sb = eph.enter_context(nc.sbuf_tensor("sg_sb", [128, 64], F32))
            su_sb = eph.enter_context(nc.sbuf_tensor("su_sb", [128, 64], F32))
            sd_sb = eph.enter_context(nc.sbuf_tensor("sd_sb", [128, 16], F32))
            tg = [eph.enter_context(nc.sbuf_tensor(f"tgE{i}", [128, 512], BF16)) for i in range(2)]
            tu = [eph.enter_context(nc.sbuf_tensor(f"tuE{i}", [128, 512], BF16)) for i in range(2)]
            h1f = [eph.enter_context(nc.sbuf_tensor(f"h1f{i}", [128, 512], F32)) for i in range(2)]
            r2 = eph.enter_context(nc.sbuf_tensor("r2", [128, 16 * 512], F32))
            identE = eph.enter_context(nc.sbuf_tensor("identE", [128, 128], F32))
            rtm = eph.enter_context(nc.sbuf_tensor("rtm", [128, 2048], F32))
            sq2 = eph.enter_context(nc.sbuf_tensor("sq2", [128, 2048], F32))
            st1 = eph.enter_context(nc.sbuf_tensor("st1", [128, 1], F32))
            st2 = eph.enter_context(nc.sbuf_tensor("st2", [128, 1], F32))
            stv = eph.enter_context(nc.sbuf_tensor("stv", [128, 1], F32))
            outb = eph.enter_context(nc.sbuf_tensor("outb", [128, 2048], F32))
            psG = [eph.enter_context(nc.psum_tensor(f"psG{i}", [128, 512], F32)) for i in range(2)]
            psU = [eph.enter_context(nc.psum_tensor(f"psU{i}", [128, 512], F32)) for i in range(2)]
            psD = [eph.enter_context(nc.psum_tensor(f"psD{i}", [128, 512], F32)) for i in range(2)]

            t_sg = P.dma("SP", sg_sb[:, :], t_sg)
            t_su = P.dma("SP", su_sb[:, :], t_su)
            t_sd = P.dma("SP", sd_sb[:, :], t_sd)
            P.op("PL", lambda e: e.memset(identE[:, :], 1.0))
            tok_id = P.op("PL", lambda e: e.affine_select(
                out=identE[:, :], in_=identE[:, :], compare_op=ALU.is_equal,
                fill=0.0, base=0, pattern=[[-1, 128]], channel_multiplier=1),
                inc="plc")

            wg_free = {0: [], 1: []}
            wu_free = {0: [], 1: []}
            wd_free = {0: [], 1: []}
            gu_free = {0: [], 1: []}
            d_free = {0: [], 1: []}
            tg_free = {0: [], 1: []}
            tu_free = {0: [], 1: []}
            fuse_free = []
            h1b_free = []
            h1f_free = {0: [], 1: []}
            r2_free = []
            rtm_free = []
            out_free = []
            out_stores = []
            def load_h1b(sw, waits=()):
                src = h1_scr[:, 512 * sw:512 * (sw + 1)].rearrange(
                    "(c p) q -> p c q", p=128)
                dst = h1b[:, :].rearrange("p (c q) -> p c q", c=16)
                return [P.dma("PL", dst, src, waits=waits)]
            t_h1map = {}
            t_h1map[0] = load_h1b(0)
            for sw in range(4):
                t_h1 = t_h1map[sw]
                for ic in range(64):
                    par = ic % 2
                    t_g = P.dma("PL", wgb[par][:, :], t_wg[ic], waits=wg_free[par])
                    t_u = P.dma("PL", wub[par][:, :], t_wu[ic], waits=wu_free[par])
                    wg_free[par], wu_free[par] = [], []
                    for c in range(16):
                        waits = []
                        if c == 0:
                            waits = [t_g, t_h1[-1]] + gu_free[par] + (fuse_free if ic == 0 else [])
                            gu_free[par] = []
                            if ic == 0:
                                fuse_free = []
                        P.op("PE", lambda e, c=c, par=par: e.matmul(
                            psG[par][:, :], wgb[par][:, 128 * c:128 * (c + 1)],
                            h1b[:, 512 * c:512 * (c + 1)],
                            start=(c == 0), stop=(c == 15)), waits=waits)
                    for c in range(16):
                        waits = [t_u] if c == 0 else []
                        P.op("PE", lambda e, c=c, par=par: e.matmul(
                            psU[par][:, :], wub[par][:, 128 * c:128 * (c + 1)],
                            h1b[:, 512 * c:512 * (c + 1)],
                            start=(c == 0), stop=(c == 15)),
                            waits=waits, inc="pe" if c == 15 else None)
                    tok_mm = P.cnt["pe"]
                    wg_free[par] = [("pe", tok_mm)]
                    wu_free[par] = [("pe", tok_mm)]
                    tok_g = P.op("ACT", lambda e, par=par, ic=ic: e.activation(
                        tg[par][:, :], psG[par][:, :], AF.Silu,
                        scale=sg_sb[:, ic:ic + 1]),
                        waits=[("pe", tok_mm), t_sg] + tg_free[par], inc="act")
                    tg_free[par] = []
                    tok_u = P.op("DVE", lambda e, par=par, ic=ic: e.tensor_scalar_mul(
                        tu[par][:, :], psU[par][:, :], su_sb[:, ic:ic + 1]),
                        waits=[("pe", tok_mm), t_su] + tu_free[par], inc="dve")
                    tu_free[par] = []
                    gu_free[par] = [("act", tok_g), ("dve", tok_u)]
                    tok_f = P.op("DVE", lambda e, par=par, ic=ic: e.tensor_mul(
                        fuse[:, 512 * ic:512 * (ic + 1)], tg[par][:, :], tu[par][:, :]),
                        waits=[("act", tok_g)], inc="dve")
                    tg_free[par] = [("dve", tok_f)]
                    tu_free[par] = [("dve", tok_f)]
                tok_fuse_all = P.cnt["dve"]
                if sw + 1 < 4:
                    t_h1map[sw + 1] = load_h1b(sw + 1, waits=[("pe", P.cnt["pe"])])
                for oc in range(16):
                    par = oc % 2
                    t_wd_ld = P.dma("PL", wdb[par][:, :], t_wd[oc], waits=wd_free[par])
                    wd_free[par] = []
                    t_h1f = P.dma("SP", h1f[par][:, :],
                                  h1_scr[128 * oc:128 * (oc + 1), 512 * sw:512 * (sw + 1)],
                                  waits=h1f_free[par])
                    h1f_free[par] = []
                    for c in range(64):
                        waits = []
                        if c == 0:
                            waits = [t_wd_ld, ("dve", tok_fuse_all)] + d_free[par] + (r2_free if oc == 0 else [])
                            d_free[par] = []
                            if oc == 0:
                                r2_free = []
                        P.op("PE", lambda e, c=c, par=par: e.matmul(
                            psD[par][:, :], wdb[par][:, 128 * c:128 * (c + 1)],
                            fuse[:, 512 * c:512 * (c + 1)],
                            start=(c == 0), stop=(c == 63)),
                            waits=waits, inc="pe" if c == 63 else None)
                    tok_mm = P.cnt["pe"]
                    wd_free[par] = [("pe", tok_mm)]
                    tok_r2 = P.op("DVE", lambda e, oc=oc, par=par: e.scalar_tensor_tensor(
                        r2[:, 512 * oc:512 * (oc + 1)], psD[par][:, :],
                        sd_sb[:, oc:oc + 1], h1f[par][:, :], ALU.mult, ALU.add),
                        waits=[("pe", tok_mm), t_h1f, t_sd], inc="dve")
                    d_free[par] = [("dve", tok_r2)]
                    h1f_free[par] = [("dve", tok_r2)]
                tok_r2_all = P.cnt["dve"]
                fuse_free = [("pe", P.cnt["pe"])]
                tr_free = []
                for q4 in range(4):
                    for oc in range(16):
                        bank = (psG + psU)[oc // 4]
                        waits = []
                        if oc == 0:
                            waits = [("dve", tok_r2_all), ("plc", tok_id)] + tr_free
                            tr_free = []
                            if q4 == 0:
                                waits += gu_free[0] + gu_free[1]
                                gu_free[0], gu_free[1] = [], []
                        P.op("PE", lambda e, oc=oc, q4=q4, bank=bank: e.transpose(
                            bank[:, 128 * (oc % 4):128 * (oc % 4 + 1)],
                            r2[:, 512 * oc + 128 * q4:512 * oc + 128 * q4 + 128],
                            identE[:, :]),
                            waits=waits, inc="pe" if oc % 4 == 3 else None)
                    tok_tr = P.cnt["pe"]
                    tok_rtm = None
                    for bk in range(4):
                        tok_rtm = P.op("DVE", lambda e, bk=bk: e.tensor_copy(
                            rtm[:, 512 * bk:512 * (bk + 1)], (psG + psU)[bk][:, :]),
                            waits=[("pe", tok_tr)] if bk == 0 else [],
                            inc="dve" if bk == 3 else None)
                    tr_free = [("dve", tok_rtm)]
                    gu_free[0] = [("dve", tok_rtm)]
                    gu_free[1] = [("dve", tok_rtm)]
                    P.op("DVE", lambda e: e.reduce_sum(st1[:, :], rtm[:, :], AX.X),
                         waits=[("dve", tok_rtm)])
                    tok_sq = P.op("ACT", lambda e: e.activation(
                        sq2[:, :], rtm[:, :], AF.Square), waits=[("dve", tok_rtm)], inc="act")
                    tok_s2 = P.op("DVE", lambda e: e.reduce_sum(st2[:, :], sq2[:, :], AX.X),
                                  waits=[("act", tok_sq)], inc="dve")
                    P.op("DVE", lambda e: e.tensor_scalar_mul(st1[:, :], st1[:, :], 1.0 / 2048.0))
                    tok_sts = P.op("DVE", lambda e: e.tensor_scalar_mul(
                        st2[:, :], st2[:, :], 1.0 / 2048.0), inc="dve")
                    tok_m2 = P.op("ACT", lambda e: e.activation(
                        stv[:, :], st1[:, :], AF.Square), waits=[("dve", tok_sts)], inc="act")
                    tok_var = P.op("DVE", lambda e: e.tensor_sub(
                        stv[:, :], st2[:, :], stv[:, :]), waits=[("act", tok_m2)], inc="dve")
                    P.op("ACT", lambda e: e.activation(
                        stv[:, :], stv[:, :], AF.Ln, bias=consts_sb[:, 1:2]),
                        waits=[("dve", tok_var)])
                    tok_rstd = P.op("ACT", lambda e: e.activation(
                        stv[:, :], stv[:, :], AF.Exp, scale=-0.5), inc="act")
                    tok_out = P.op("DVE", lambda e: e.tensor_scalar(
                        outb[:, :], rtm[:, :], st1[:, :], stv[:, :],
                        ALU.subtract, ALU.mult),
                        waits=[("act", tok_rstd)] + out_free, inc="dve")
                    out_free = []
                    rtm_free = [("dve", tok_out)]
                    t_st = P.dma("SP", t_out[128 * (4 * sw + q4):128 * (4 * sw + q4 + 1), :],
                                 outb[:, :], waits=[("dve", tok_out)], cls="st")
                    out_stores.append(t_st)
                    out_free = [t_st]
                r2_free = [("pe", P.cnt["pe"])]
            P.op("SP", lambda e: e.nop(), waits=P.dma_all("SP") + P.dma_all("PL"))
            P.emit()

    return nc, dbg_aps


def _final_wait(nc, P, t_out):
    P.phase()
    with nc.sbuf_tensor("zz", [128, 2048], F32) as zz:
        P.op("PL", lambda e: e.memset(zz[:, :], 0.0), inc="plc")
        for i in range(16):
            P.dma("SP", t_out[128 * i:128 * (i + 1), :], zz[:, :],
                  waits=[("plc", P.cnt["plc"])])
        P.op("SP", lambda e: e.nop(), waits=P.dma_all("SP") + P.dma_all("PL"))
        P.emit()


_NC_CACHE = {}


def _get_nc(upto=5, dbg=()):
    key = (upto, tuple(sorted(dbg)))
    if key not in _NC_CACHE:
        _NC_CACHE[key] = build_nc(upto, dbg)
    return _NC_CACHE[key]


def kernel(**inputs):
    in_maps, tiles_per_core = host_prep(inputs)
    nc, _ = _get_nc()
    res = run_bass_kernel_spmd(nc, in_maps, core_ids=list(range(8)))
    out = np.zeros((B, S, D), np.float32)
    for core in range(8):
        b, tiles = tiles_per_core[core]
        r = res.results[core]["out"]
        for s, t in enumerate(tiles):
            out[b, 128 * t:128 * t + 128, :] = r[128 * s:128 * s + 128, :]
    return out
